# version 07 (final)
# speedup vs baseline: 1.9103x; 1.1033x over previous
import contextlib
import numpy as np
import ml_dtypes
import concourse.bass as bass
import concourse.mybir as mybir
from concourse.bass_utils import run_bass_kernel_spmd

F32 = mybir.dt.float32
F16 = mybir.dt.float16
BF16 = mybir.dt.bfloat16
I32 = mybir.dt.int32
AF = mybir.ActivationFunctionType
ALU = mybir.AluOpType
AX = mybir.AxisListType.X

BF = ml_dtypes.bfloat16

NCORES = 8
T = 1024
S = 4096
B = 2
D = 4096



class Sched:

    ENGS = ('tensor', 'vector', 'scalar', 'gpsimd', 'sync')

    def __init__(self):
        self.items = []
        self.serialize_recording = False
        self.chain_compute_recording = False

    @staticmethod
    def _rng(ap):
        try:
            t = ap.tensor
            name = t.name
        except AttributeError:
            return None
        off = ap.offset
        kind = type(t).__name__
        if kind == 'SBTensorHandle':
            W = 1
            for d in list(t.shape)[1:]:
                W *= int(d)
            lo = off % W if W else 0
            ext = 1
            ok = True
            for pair in ap.ap:
                step, count = abs(pair[0]), pair[1]
                if step == W and W > 1:
                    continue
                if step > W:
                    ok = False
                    break
                ext += (count - 1) * step
            if ok:
                hi = lo + ext
                if hi <= W:
                    return (name, lo, hi)
            return (name, 0, W)
        ext = 1
        for pair in ap.ap:
            step, count = pair[0], pair[1]
            ext += (count - 1) * abs(step)
        return (name, off, off + ext)

    def _op(self, eng, inc, fn, reads=(), writes=(), is_coll=False):
        rd = [r for r in (self._rng(a) for a in reads if a is not None) if r]
        wr = [r for r in (self._rng(a) for a in writes if a is not None) if r]
        self.items.append((eng, fn, inc, rd, wr, is_coll, self.chain_compute_recording))
        if self.serialize_recording:
            self.items.append(('BARRIER', None, 0, (), (), (), False))

    def barrier(self):
        self.items.append(('BARRIER', None, 0, (), (), False, False))

    def dma(self, dst, src, cast=False, slow=False):
        eng = 'gpsimd' if cast else 'sync'
        kw = {}
        if slow:
            kw['allow_slow_non_contiguous'] = True
        self._op(eng, 16, lambda e, d=dst, s=src, kw=kw: e.dma_start(out=d, in_=s, **kw),
                 reads=[src], writes=[dst])

    def mm(self, out, lhsT, rhs, start=True, stop=True):
        self._op('tensor', 1,
                 lambda e, o=out, l=lhsT, r=rhs, st=start, sp=stop: e.matmul(o, l, r, start=st, stop=sp),
                 reads=[lhsT, rhs], writes=[out])

    def tr(self, out, in_, ident):
        self._op('tensor', 1, lambda e, o=out, i=in_, d=ident: e.transpose(o, i, d),
                 reads=[in_, ident], writes=[out])

    def act(self, out, in_, func, bias=0.0, scale=1.0, accum_out=None):
        aps_r = [in_] + [x for x in (bias, scale) if not isinstance(x, (int, float))]
        aps_w = [out] + ([accum_out] if accum_out is not None else [])
        self._op('scalar', 1,
                 lambda e, o=out, i=in_, f=func, b=bias, s=scale, a=accum_out:
                 e.activation(o, i, f, bias=b, scale=s, accum_out=a),
                 reads=aps_r, writes=aps_w)

    def scopy(self, out, in_):
        self._op('scalar', 1, lambda e, o=out, i=in_: e.copy(o, i),
                 reads=[in_], writes=[out])

    def tt(self, out, a, b, op):
        self._op('vector', 1, lambda e, o=out, x=a, y=b, p=op: e.tensor_tensor(o, x, y, p),
                 reads=[a, b], writes=[out])

    def ts(self, out, in0, s1, s2, op0, op1=ALU.bypass, accum_out=None):
        aps_r = [in0] + [x for x in (s1, s2) if x is not None and not isinstance(x, (int, float))]
        aps_w = [out] + ([accum_out] if accum_out is not None else [])
        self._op('vector', 1,
                 lambda e, o=out, i=in0, a=s1, b=s2, p0=op0, p1=op1, ac=accum_out:
                 e.tensor_scalar(o, i, a, b, p0, p1, accum_out=ac),
                 reads=aps_r, writes=aps_w)

    def stt(self, out, in0, scalar, in1, op0, op1):
        aps_r = [in0, in1] + ([scalar] if not isinstance(scalar, (int, float)) else [])
        self._op('vector', 1,
                 lambda e, o=out, i=in0, s=scalar, j=in1, p0=op0, p1=op1:
                 e.scalar_tensor_tensor(o, i, s, j, p0, p1),
                 reads=aps_r, writes=[out])

    def vcopy(self, out, in_):
        self._op('vector', 1, lambda e, o=out, i=in_: e.tensor_copy(o, i),
                 reads=[in_], writes=[out])

    def rmax(self, out, in_):
        self._op('vector', 1, lambda e, o=out, i=in_: e.reduce_max(o, i, axis=AX),
                 reads=[in_], writes=[out])

    def rsum(self, out, in_):
        self._op('vector', 1, lambda e, o=out, i=in_: e.reduce_sum(o, i, axis=AX),
                 reads=[in_], writes=[out])

    def recip(self, out, in_):
        self._op('vector', 1, lambda e, o=out, i=in_: e.reciprocal(o, i),
                 reads=[in_], writes=[out])

    def memset(self, ap, val):
        self._op('vector', 1, lambda e, a=ap, v=val: e.memset(a, v), writes=[ap])

    def replay(self, nc, serialize=False):
        if serialize and serialize not in ('dveact', 'dvegp', 'split'):
            items2 = []
            for it in self.items:
                if it[0] != 'BARRIER':
                    items2.append(it)
                    if serialize is True:
                        items2.append(('BARRIER', None, 0, (), (), False, False))
                    elif serialize == 'dma' and it[2] >= 16:
                        items2.append(('BARRIER', None, 0, (), (), False, False))
                    elif serialize == 'compute' and it[2] < 16:
                        items2.append(('BARRIER', None, 0, (), (), False, False))
            self.items = items2
        n = len(self.items)

        HIST_MAX = 96
        hist = {}
        dep_of = [0] * n
        deps = [[] for _ in range(n)]
        eng_of = [it[0] for it in self.items]
        barrier_dep = -1

        def conflicts(name, lo, hi, want_writes, want_reads):
            out = []
            h = hist.get(name)
            if not h:
                return out
            for (l, h2, idx, isw) in h:
                if l < hi and lo < h2:
                    if (isw and want_writes) or ((not isw) and want_reads):
                        out.append(idx)
            return out

        last_before_barrier = -1
        prev_dveact = -1
        prev_act = [-1]
        prev_ch3 = [-1]
        last_dma = {'sync': -1, 'gpsimd': -1}
        for i, (eng, fn, inc, rd, wr, is_coll, ch3) in enumerate(self.items):
            if eng == 'BARRIER':
                last_before_barrier = i - 1
                hist.clear()
                continue
            dd = set()
            if last_before_barrier >= 0:
                dd.add(('ALL', last_before_barrier))
            for (name, lo, hi) in rd:
                for j in conflicts(name, lo, hi, True, False):
                    dd.add(('I', j))
            for (name, lo, hi) in wr:
                for j in conflicts(name, lo, hi, True, True):
                    dd.add(('I', j))
            _sergrp = {'dveact': ('vector', 'scalar'),
                       'dvegp': ('vector', 'scalar', 'gpsimd')}.get(serialize)
            prev_ch3_target = -1
            if ch3 and eng in ('tensor', 'vector', 'scalar'):
                if prev_ch3[0] >= 0:
                    dd.add(('I', prev_ch3[0]))
                    prev_ch3_target = prev_ch3[0]
                prev_ch3[0] = i
                if last_dma['sync'] >= 0:
                    dd.add(('I', last_dma['sync']))
                if last_dma['gpsimd'] >= 0:
                    dd.add(('I', last_dma['gpsimd']))
            if eng in ('sync', 'gpsimd') and inc >= 16 and not is_coll:
                last_dma[eng] = i
            elif _sergrp and eng in _sergrp:
                if prev_dveact >= 0:
                    dd.add(('I', prev_dveact))
                prev_dveact = i
            elif serialize == 'split':
                if eng == 'vector':
                    if prev_dveact >= 0:
                        dd.add(('I', prev_dveact))
                    prev_dveact = i
                elif eng == 'scalar':
                    if prev_act[0] >= 0:
                        dd.add(('I', prev_act[0]))
                    prev_act[0] = i
            pruned = []
            for kind, j in dd:
                if (kind == 'I' and eng == 'tensor' and eng_of[j] == 'tensor'
                        and j != prev_ch3_target):
                    continue
                pruned.append((kind, j))
            deps[i] = pruned
            for (name, lo, hi) in rd:
                hist.setdefault(name, []).append((lo, hi, i, False))
            for (name, lo, hi) in wr:
                hist.setdefault(name, []).append((lo, hi, i, True))
            for name in (set(x[0] for x in rd) | set(x[0] for x in wr)):
                h = hist[name]
                if len(h) > HIST_MAX:
                    cut = h[:HIST_MAX // 2]
                    keep = h[HIST_MAX // 2:]
                    coll_entries = []
                    cr = [x[2] for x in cut if not x[3]]
                    cw = [x[2] for x in cut if x[3]]
                    if cr:
                        coll_entries.append((-1 << 40, 1 << 40, max(cr), False))
                    if cw:
                        coll_entries.append((-1 << 40, 1 << 40, max(cw), True))
                    hist[name] = coll_entries + keep

        depset = set()
        for i in range(n):
            for kind, j in deps[i]:
                if kind == 'I':
                    depset.add(j)
        last_of = {}
        for i, it in enumerate(self.items):
            eng = it[0]
            if eng == 'BARRIER':
                depset.update(last_of.values())
            else:
                key = 'cc' if it[5] else eng
                last_of[key] = i
        depset.update(last_of.values())
        for i, it in enumerate(self.items):
            if it[5]:
                depset.add(i)

        att = [0] * n
        pend = {e: 0 for e in self.ENGS}
        pend_cc = 0
        cum = {e: [0] * n for e in self.ENGS}
        cum_cc = [0] * n
        run = {e: 0 for e in self.ENGS}
        run_cc = 0
        for i, (eng, fn, inc, rd, wr, is_coll, ch3) in enumerate(self.items):
            if eng != 'BARRIER':
                if is_coll:
                    pend_cc += inc
                    if i in depset:
                        att[i] = pend_cc
                        run_cc += pend_cc
                        pend_cc = 0
                else:
                    pend[eng] += inc
                    if i in depset or inc >= 16:
                        att[i] = pend[eng]
                        run[eng] += pend[eng]
                        pend[eng] = 0
            for e in self.ENGS:
                cum[e][i] = run[e]
            cum_cc[i] = run_cc

        with contextlib.ExitStack() as st:
            sems = {e: st.enter_context(nc.semaphore(f"sem_{e}")) for e in self.ENGS}
            sem_cc = st.enter_context(nc.semaphore("sem_cc"))
            block = st.enter_context(nc.Block())
            final_targets = {e: run[e] for e in self.ENGS}
            final_cc = run_cc

            def make_body(name):
                def body(eng):
                    maxw = {e: 0 for e in self.ENGS}
                    maxw_cc = 0
                    for i, (e_, fn, inc, rd, wr, is_coll, ch3) in enumerate(self.items):
                        if e_ != name:
                            continue
                        want = {}
                        want_cc = 0
                        for kind, j in deps[i]:
                            if kind == 'ALL':
                                for e2 in self.ENGS:
                                    want[e2] = max(want.get(e2, 0), cum[e2][j])
                                want_cc = max(want_cc, cum_cc[j])
                            else:
                                ej = eng_of[j]
                                if self.items[j][5]:
                                    want_cc = max(want_cc, cum_cc[j])
                                else:
                                    want[ej] = max(want.get(ej, 0), cum[ej][j])
                        for e2, v in want.items():
                            if v > maxw[e2]:
                                eng.wait_ge(sems[e2], v)
                                maxw[e2] = v
                        if want_cc > maxw_cc:
                            eng.wait_ge(sem_cc, want_cc)
                            maxw_cc = want_cc
                        ins = fn(eng)
                        if att[i]:
                            ins.then_inc(sem_cc if is_coll else sems[name], att[i])
                    if name == 'sync':
                        for e2 in self.ENGS:
                            if final_targets[e2] > maxw[e2]:
                                eng.wait_ge(sems[e2], final_targets[e2])
                        if final_cc > maxw_cc:
                            eng.wait_ge(sem_cc, final_cc)
                return body

            for name in self.ENGS:
                getattr(block, name)(make_body(name))



def _bf(x):
    return np.ascontiguousarray(x, dtype=np.float32)


def _t(x):
    return np.ascontiguousarray(np.asarray(x, dtype=np.float32).T)


def _rope_tabs_l0(pos):
    freqs = 1.0 / 10000.0 ** (np.arange(0, 64, 2, dtype=np.float32) / 64)
    ang = np.outer(pos.astype(np.float32), freqs)
    return np.ascontiguousarray(np.cos(ang).T), np.ascontiguousarray(np.sin(ang).T)


def _rope_tabs_l1(pos):
    inv = 1.0 / 10000.0 ** (np.arange(0, 128, 2, dtype=np.float32) / 128)
    ang = np.outer(pos.astype(np.float32), inv)
    return np.ascontiguousarray(np.cos(ang).T), np.ascontiguousarray(np.sin(ang).T)


def _deint_perm():
    return np.concatenate([np.arange(0, 64, 2), np.arange(1, 64, 2)])



def build_launch_a():
    nc = bass.Bass("TRN2", target_bir_lowering=False, debug=False)
    dr = lambda n, s, d, k: nc.dram_tensor(n, s, d, kind=k).ap()

    xT = dr("xT", [D, T], F32, "ExternalInput")
    wqaT = dr("wqaT", [D, 1536], F32, "ExternalInput")
    wqbnT = dr("wqbnT", [1536, 4096], F32, "ExternalInput")
    wqbpT = dr("wqbpT", [1536, 2048], F32, "ExternalInput")
    qnw = dr("qnw", [128, 12], F32, "ExternalInput")
    wkvaT = dr("wkvaT", [D, 576], F32, "ExternalInput")
    kvnw = dr("kvnw", [128, 4], F32, "ExternalInput")
    cosp = dr("cosp", [32, T], F32, "ExternalInput")
    sinp = dr("sinp", [32, T], F32, "ExternalInput")
    qwT = dr("qwT", [512, 512], I32, "ExternalInput")
    scT = dr("scT", [512, 32], F32, "ExternalInput")
    zrT = dr("zrT", [512, 32], F32, "ExternalInput")
    identf_in = dr("identf_in", [128, 128], F32, "ExternalInput")

    o_qn = dr("o_qn", [4096, T], BF16, "ExternalOutput")
    o_qp = dr("o_qp", [2048, T], BF16, "ExternalOutput")
    o_ckv = dr("o_ckv", [512, T], BF16, "ExternalOutput")
    o_kpe = dr("o_kpe", [64, T], BF16, "ExternalOutput")
    o_wdq = dr("o_wdq", [4096, 512], BF16, "ExternalOutput")

    S_ = Sched()
    with contextlib.ExitStack() as st:
        sb = lambda n, s, d: st.enter_context(nc.sbuf_tensor(n, s, d))
        ps = lambda n, s, d: st.enter_context(nc.psum_tensor(n, s, d))

        x_bf = sb("x_bf", [128, 32 * T], BF16)
        wcol = sb("wcol", [128, 32 * 128], BF16)
        qlr = sb("qlr", [128, 12 * T], F16)
        qln = sb("qln", [128, 12 * T], BF16)
        kvr = sb("kvr", [128, 5 * T], F16)
        sq = sb("sq", [128, T], F16)
        row = sb("row", [1, T], F32)
        row2 = sb("row2", [1, T], F32)
        bc = sb("bc", [128, T], F32)
        onecol = sb("onecol", [128, 1], F16)
        onerow = sb("onerow", [1, 128], F32)
        c_eps6 = sb("c_eps6", [128, 1], F32)
        qnw_s = sb("qnw_s", [128, 12], F32)
        kvnw_s = sb("kvnw_s", [128, 4], F32)
        trig = sb("trig", [64, 2 * T], F32)
        rp = [sb(f"rp{i}", [32, T], F32) for i in range(4)]
        ro = [sb(f"ro{i}", [32, T], BF16) for i in range(2)]
        x2c = sb("x2c", [32, T], F32)
        otile = sb("otile", [128, T], BF16)
        qw_s = sb("qw_s", [128, 4 * 512], I32)
        qtmp = sb("qtmp", [128, 512], I32)
        wun = sb("wun", [128, 4096], F16)
        waf = sb("waf", [128, 4096], BF16)
        sc_s = sb("sc_s", [128, 4 * 32], F32)
        zr_s = sb("zr_s", [128, 4 * 32], F32)
        wtr = sb("wtr", [128, 512], BF16)
        identf = sb("identf", [128, 128], F32)
        ident_b = sb("ident_b", [128, 128], BF16)

        pQ = ps("pQ", [128, T], F32)
        pR = ps("pR", [1, T], F32)
        pB = ps("pB", [128, T], F32)
        pT = ps("pT", [128, 512], BF16)

        for t in range(32):
            S_.dma(x_bf[:, t * T:(t + 1) * T], xT[t * 128:(t + 1) * 128, :], cast=True)
        S_.dma(qnw_s[:], qnw)
        S_.dma(kvnw_s[:], kvnw)
        S_.dma(trig[0:32, 0:T], cosp)
        S_.dma(trig[0:32, T:2 * T], sinp)
        S_.dma(identf[:], identf_in)
        S_.dma(ident_b[:], identf_in, cast=True)
        S_.memset(onecol[:], 1.0)
        S_.memset(onerow[:], 1.0)
        S_.memset(c_eps6[:], 1e-6)

        def mm_otile(psum, wT_dram, ocol0, owidth, rhs_sb, nk, extra_wcols=0):
            S_.dma(wcol[:, 0:nk * owidth].rearrange("p (k o) -> p k o", k=nk),
                   wT_dram[:, ocol0:ocol0 + owidth].rearrange("(k p) o -> p k o", p=128),
                   cast=True)
            for k in range(nk):
                l = wcol[0:128, k * owidth:(k + 1) * owidth]
                for h in range(0, T, 512):
                    S_.mm(psum[0:owidth, h:h + 512], l, rhs_sb[:, k * T + h:k * T + h + 512],
                          start=(k == 0), stop=(k == nk - 1))

        for ot in range(12):
            mm_otile(pQ, wqaT, ot * 128, 128, x_bf, 32)
            S_.vcopy(qlr[:, ot * T:(ot + 1) * T], pQ[:, :])
            S_.act(sq[:], pQ[:, :], AF.Square)
            for h in range(0, T, 512):
                S_.mm(pR[0:1, h:h + 512], onecol[:], sq[:, h:h + 512],
                      start=(ot == 0), stop=(ot == 11))
        S_.act(row[:], pR[0:1, :], AF.Ln, bias=c_eps6[0:1, 0:1], scale=1.0 / 1536)
        S_.act(row[:], row[:], AF.Exp, scale=-0.5)
        for h in range(0, T, 512):
            S_.mm(pB[:, h:h + 512], onerow[:], row[0:1, h:h + 512])
        S_.vcopy(bc[:], pB[:, :])
        for ot in range(12):
            S_.stt(qln[:, ot * T:(ot + 1) * T], qlr[:, ot * T:(ot + 1) * T],
                   qnw_s[:, ot:ot + 1], bc[:], ALU.mult, ALU.mult)

        for ot in range(4):
            mm_otile(pQ, wkvaT, ot * 128, 128, x_bf, 32)
            S_.vcopy(kvr[:, ot * T:(ot + 1) * T], pQ[:, :])
            S_.act(sq[:], pQ[:, :], AF.Square)
            for h in range(0, T, 512):
                S_.mm(pR[0:1, h:h + 512], onecol[:], sq[:, h:h + 512],
                      start=(ot == 0), stop=(ot == 3))
        mm_otile(pQ, wkvaT, 512, 64, x_bf, 32)
        S_.vcopy(kvr[0:64, 4 * T:5 * T], pQ[0:64, :])
        S_.act(row[:], pR[0:1, :], AF.Ln, bias=c_eps6[0:1, 0:1], scale=1.0 / 512)
        S_.act(row[:], row[:], AF.Exp, scale=-0.5)
        for h in range(0, T, 512):
            S_.mm(pB[:, h:h + 512], onerow[:], row[0:1, h:h + 512])
        S_.vcopy(bc[:], pB[:, :])
        for ot in range(4):
            S_.stt(otile[:, :], kvr[:, ot * T:(ot + 1) * T],
                   kvnw_s[:, ot:ot + 1], bc[:], ALU.mult, ALU.mult)
            S_.dma(o_ckv[ot * 128:(ot + 1) * 128, :], otile[:, :])

        def rope_deint(src32_hi_getter, x1ap, x2ap):
            S_.vcopy(x2c[:, :], x2ap)
            cos_t = trig[0:32, 0:T]
            sin_t = trig[0:32, T:2 * T]
            S_.tt(rp[0][:, :], x1ap, cos_t, ALU.mult)
            S_.tt(rp[1][:, :], x2c[:, :], sin_t, ALU.mult)
            S_.tt(rp[2][:, :], x1ap, sin_t, ALU.mult)
            S_.tt(rp[3][:, :], x2c[:, :], cos_t, ALU.mult)
            S_.tt(ro[0][:, :], rp[0][:, :], rp[1][:, :], ALU.subtract)
            S_.tt(ro[1][:, :], rp[2][:, :], rp[3][:, :], ALU.add)

        rope_deint(None, kvr[0:32, 4 * T:5 * T], kvr[32:64, 4 * T:5 * T])
        S_.dma(o_kpe[0:32, :], ro[0][:, :])
        S_.dma(o_kpe[32:64, :], ro[1][:, :])

        for ot in range(32):
            mm_otile(pQ, wqbnT, ot * 128, 128, qln, 12)
            S_.vcopy(otile[:, :], pQ[:, :])
            S_.dma(o_qn[ot * 128:(ot + 1) * 128, :], otile[:, :])

        for hh in range(32):
            mm_otile(pQ, wqbpT, hh * 64, 64, qln, 12)
            S_.vcopy(kvr[0:64, 4 * T:5 * T], pQ[0:64, :])
            rope_deint(None, kvr[0:32, 4 * T:5 * T], kvr[32:64, 4 * T:5 * T])
            S_.dma(o_qp[hh * 64:hh * 64 + 32, :], ro[0][:, :])
            S_.dma(o_qp[hh * 64 + 32:hh * 64 + 64, :], ro[1][:, :])

        for t in range(4):
            S_.dma(qw_s[:, t * 512:(t + 1) * 512], qwT[t * 128:(t + 1) * 128, :])
            S_.dma(sc_s[:, t * 32:(t + 1) * 32], scT[t * 128:(t + 1) * 128, :])
            S_.dma(zr_s[:, t * 32:(t + 1) * 32], zrT[t * 128:(t + 1) * 128, :])
        for t in range(4):
            q = qw_s[:, t * 512:(t + 1) * 512]
            for j in range(8):
                S_.ts(qtmp[:], q, float(4 * j), 15.0,
                      ALU.logical_shift_right, ALU.bitwise_and)
                S_.vcopy(wun[:, j::8], qtmp[:])
            for g in range(32):
                S_.ts(waf[:, g * 128:(g + 1) * 128], wun[:, g * 128:(g + 1) * 128],
                      sc_s[:, t * 32 + g:t * 32 + g + 1],
                      zr_s[:, t * 32 + g:t * 32 + g + 1],
                      ALU.mult, ALU.subtract)
            for g in range(0, 32, 4):
                for gg in range(4):
                    S_.tr(pT[:, gg * 128:(gg + 1) * 128],
                          waf[:, (g + gg) * 128:(g + gg + 1) * 128], ident_b[:])
                S_.vcopy(wtr[:, :], pT[:, :])
                for gg in range(4):
                    S_.dma(o_wdq[(g + gg) * 128:(g + gg + 1) * 128, t * 128:(t + 1) * 128],
                           wtr[:, gg * 128:(gg + 1) * 128])

        S_.replay(nc)
    return nc


def host_prep_a(inputs):
    x = np.asarray(inputs["L0_attn_x"], np.float32)
    wq_a = np.asarray(inputs["L0_attn_wq_a"], np.float32)
    q_norm_w = np.asarray(inputs["L0_attn_q_norm_w"], np.float32)
    wq_b = np.asarray(inputs["L0_attn_wq_b"], np.float32)
    wkv_a = np.asarray(inputs["L0_attn_wkv_a"], np.float32)
    kv_norm_w = np.asarray(inputs["L0_attn_kv_norm_w"], np.float32)
    qw = np.asarray(inputs["L0_ffn_q_weights"], np.int32)
    sc = np.asarray(inputs["L0_ffn_scales"], np.float32)
    zr = np.asarray(inputs["L0_ffn_zeros"], np.float32)

    perm = _deint_perm()
    wqb = wq_b.reshape(32, 192, 1536)
    wq_b_nope = wqb[:, :128, :].reshape(4096, 1536)
    wq_b_pe = wqb[:, 128:, :][:, perm, :].reshape(2048, 1536)
    wqbnT = _t(wq_b_nope)
    wqbpT = _t(wq_b_pe)
    wqaT = _t(wq_a)
    wkva_p = wkv_a.copy()
    wkva_p[512:576] = wkv_a[512:576][perm]
    wkvaT = _t(wkva_p)
    qnw_t = np.ascontiguousarray(q_norm_w.reshape(12, 128).T)
    kvnw_t = np.ascontiguousarray(kv_norm_w.reshape(4, 128).T)
    qwT = _t(qw.astype(np.float64)).astype(np.int32)
    qwT = np.ascontiguousarray(qw.T)
    scT = _t(sc)
    zrT = _t(zr)
    ident = np.eye(128, dtype=np.float32)

    maps = []
    for c in range(NCORES):
        b, t0 = c // 4, (c % 4) * T
        pos = np.arange(t0, t0 + T)
        cosp, sinp = _rope_tabs_l0(pos)
        maps.append({
            "xT": np.ascontiguousarray(x[b, t0:t0 + T, :].T),
            "wqaT": wqaT, "wqbnT": wqbnT, "wqbpT": wqbpT,
            "qnw": qnw_t, "wkvaT": wkvaT, "kvnw": kvnw_t,
            "cosp": cosp, "sinp": sinp,
            "qwT": np.ascontiguousarray(qwT[512 * c:512 * (c + 1), :]),
            "scT": np.ascontiguousarray(scT[512 * c:512 * (c + 1), :]),
            "zrT": np.ascontiguousarray(zrT[512 * c:512 * (c + 1), :]),
            "identf_in": ident,
        })
    return maps





L0_SCALE = 1.0 / float(np.sqrt(192.0))


def build_launch_b():
    nc = bass.Bass("TRN2", target_bir_lowering=False, debug=False)
    dr = lambda n, s, d, k: nc.dram_tensor(n, s, d, kind=k).ap()

    qn_h = dr("qn_h", [1024, S], BF16, "ExternalInput")
    qp_h = dr("qp_h", [512, S], BF16, "ExternalInput")
    ckv = dr("ckv", [512, S], BF16, "ExternalInput")
    kpe = dr("kpe", [64, S], BF16, "ExternalInput")
    wkvkT = dr("wkvkT", [512, 1024], F32, "ExternalInput")
    wkvvT = dr("wkvvT", [512, 1024], F32, "ExternalInput")
    identf_in = dr("identf_in", [128, 128], F32, "ExternalInput")
    maskneg_in = dr("maskneg_in", [128, 128], F32, "ExternalInput")

    o_y = dr("o_y", [S, 1024], BF16, "ExternalOutput")

    S_ = Sched()
    with contextlib.ExitStack() as st:
        sb = lambda n, s, d: st.enter_context(nc.sbuf_tensor(n, s, d))
        ps = lambda n, s, d: st.enter_context(nc.psum_tensor(n, s, d))

        ckv_s = sb("ckv_s", [128, 4 * S], BF16)
        kpe_s = sb("kpe_s", [64, S], BF16)
        wkvk_s = sb("wkvk_s", [128, 4 * 1024], BF16)
        wkvv_s = sb("wkvv_s", [128, 4 * 1024], BF16)
        qn_s = sb("qn_s", [128, S], BF16)
        qp_s = sb("qp_s", [64, S], BF16)
        kh_s = sb("kh_s", [128, S], BF16)
        vtm_s = sb("vtm_s", [128, S], BF16)
        Ssb = sb("Ssb", [128, S], F32)
        Pb = sb("Pb", [128, S], BF16)
        PT_s = sb("PT_s", [128, S], BF16)
        junk = sb("junk", [128, 512], F32)
        mparts = sb("mparts", [128, 8], F32)
        rsparts = sb("rsparts", [128, 8], F32)
        m_col = sb("m_col", [128, 1], F32)
        negm = sb("negm", [128, 1], F32)
        rs_col = sb("rs_col", [128, 1], F32)
        rinv = sb("rinv", [128, 1], F32)
        ysb = sb("ysb", [128, 128], BF16)
        ident_b = sb("ident_b", [128, 128], BF16)
        maskneg = sb("maskneg", [128, 128], F32)

        pS = ps("pS", [128, 512], F32)
        pV = ps("pV", [128, 128], F32)
        pTt = ps("pTt", [128, 512], BF16)
        pAV = ps("pAV", [128, 128], F32)

        for t in range(4):
            S_.dma(ckv_s[:, t * S:(t + 1) * S], ckv[t * 128:(t + 1) * 128, :])
            S_.dma(wkvk_s[:, t * 1024:(t + 1) * 1024], wkvkT[t * 128:(t + 1) * 128, :], cast=True)
            S_.dma(wkvv_s[:, t * 1024:(t + 1) * 1024], wkvvT[t * 128:(t + 1) * 128, :], cast=True)
        S_.dma(kpe_s[:, :], kpe)
        S_.dma(ident_b[:], identf_in, cast=True)
        S_.dma(maskneg[:], maskneg_in)

        for hl in range(8):
            S_.dma(qn_s[:, :], qn_h[hl * 128:(hl + 1) * 128, :])
            S_.dma(qp_s[:, :], qp_h[hl * 64:(hl + 1) * 64, :])
            for tc in range(8):
                for kc in range(4):
                    S_.mm(pS[:, :], wkvk_s[:, kc * 1024 + hl * 128:kc * 1024 + (hl + 1) * 128],
                          ckv_s[:, kc * S + tc * 512:kc * S + (tc + 1) * 512],
                          start=(kc == 0), stop=(kc == 3))
                S_.vcopy(kh_s[:, tc * 512:(tc + 1) * 512], pS[:, :])
            for tt in range(32):
                for kc in range(4):
                    S_.mm(pV[:, :], ckv_s[:, kc * S + tt * 128:kc * S + (tt + 1) * 128],
                          wkvv_s[:, kc * 1024 + hl * 128:kc * 1024 + (hl + 1) * 128],
                          start=(kc == 0), stop=(kc == 3))
                S_.vcopy(vtm_s[:, tt * 128:(tt + 1) * 128], pV[:, :])
            for qt in range(32):
                nk = qt + 1
                kw = nk * 128
                nkc = (kw + 511) // 512
                for ci in range(nkc):
                    w = min(512, kw - ci * 512)
                    S_.mm(pS[0:128, 0:w], qn_s[:, qt * 128:(qt + 1) * 128],
                          kh_s[:, ci * 512:ci * 512 + w], start=True, stop=False)
                    S_.mm(pS[0:128, 0:w], qp_s[0:64, qt * 128:(qt + 1) * 128],
                          kpe_s[0:64, ci * 512:ci * 512 + w], start=False, stop=True)
                    last = (ci == nkc - 1)
                    wnd = w - 128 if last else w
                    if wnd > 0:
                        S_.vcopy(Ssb[:, ci * 512:ci * 512 + wnd], pS[0:128, 0:wnd])
                    if last:
                        S_.tt(Ssb[:, kw - 128:kw], pS[0:128, w - 128:w], maskneg[:], ALU.add)
                    S_.ts(junk[:, 0:w], Ssb[:, ci * 512:ci * 512 + w], 1.0, None,
                          ALU.mult, ALU.max, accum_out=mparts[:, ci:ci + 1])
                S_.rmax(m_col[:], mparts[:, 0:nkc])
                S_.ts(negm[:], m_col[:], -L0_SCALE, None, ALU.mult)
                for ci in range(nkc):
                    w = min(512, kw - ci * 512)
                    S_.act(Pb[:, ci * 512:ci * 512 + w], Ssb[:, ci * 512:ci * 512 + w],
                           AF.Exp, bias=negm[:, 0:1], scale=L0_SCALE,
                           accum_out=rsparts[:, ci:ci + 1])
                S_.rsum(rs_col[:], rsparts[:, 0:nkc])
                S_.recip(rinv[:], rs_col[:])
                for g in range(0, nk, 4):
                    ng = min(4, nk - g)
                    for kk in range(ng):
                        S_.tr(pTt[:, kk * 128:(kk + 1) * 128],
                              Pb[:, (g + kk) * 128:(g + kk + 1) * 128], ident_b[:])
                    S_.vcopy(PT_s[:, g * 128:(g + ng) * 128], pTt[:, 0:ng * 128])
                for kt in range(nk):
                    S_.mm(pAV[:, :], PT_s[:, kt * 128:(kt + 1) * 128],
                          vtm_s[:, kt * 128:(kt + 1) * 128],
                          start=(kt == 0), stop=(kt == nk - 1))
                S_.act(ysb[:], pAV[:, :], AF.Copy, scale=rinv[:, 0:1])
                S_.dma(o_y[qt * 128:(qt + 1) * 128, hl * 128:(hl + 1) * 128], ysb[:])

        S_.replay(nc)
    return nc


def host_prep_b(inputs, res_a):
    wkv_b = np.asarray(inputs["L0_attn_wkv_b"], np.float32)
    qn_full, qp_full, ckv_full, kpe_full = [], [], [], []
    for b in range(2):
        qn_full.append(np.concatenate([res_a[4 * b + i]["o_qn"] for i in range(4)], axis=1))
        qp_full.append(np.concatenate([res_a[4 * b + i]["o_qp"] for i in range(4)], axis=1))
        ckv_full.append(np.concatenate([res_a[4 * b + i]["o_ckv"] for i in range(4)], axis=1))
        kpe_full.append(np.concatenate([res_a[4 * b + i]["o_kpe"] for i in range(4)], axis=1))
    maskneg = np.where(np.arange(128)[None, :] <= np.arange(128)[:, None],
                       0.0, -1e30).astype(np.float32)
    ident = np.eye(128, dtype=np.float32)
    maps = []
    for c in range(NCORES):
        b, h0 = c // 4, 8 * (c % 4)
        wkvb = wkv_b.reshape(32, 256, 512)
        wk = np.concatenate([wkvb[h0 + i, :128, :].T for i in range(8)], axis=1)
        wv = np.concatenate([wkvb[h0 + i, 128:, :].T for i in range(8)], axis=1)
        maps.append({
            "qn_h": np.ascontiguousarray(qn_full[b][h0 * 128:(h0 + 8) * 128, :]),
            "qp_h": np.ascontiguousarray(qp_full[b][h0 * 64:(h0 + 8) * 64, :]),
            "ckv": np.ascontiguousarray(ckv_full[b]),
            "kpe": np.ascontiguousarray(kpe_full[b]),
            "wkvkT": np.ascontiguousarray(wk, dtype=np.float32),
            "wkvvT": np.ascontiguousarray(wv, dtype=np.float32),
            "identf_in": ident,
            "maskneg_in": maskneg,
        })
    return maps



def build_launch_c():
    nc = bass.Bass("TRN2", target_bir_lowering=False, debug=False)
    dr = lambda n, s, d, k: nc.dram_tensor(n, s, d, kind=k).ap()

    yT = dr("yT", [D, T], BF16, "ExternalInput")
    xT = dr("xT", [D, T], F32, "ExternalInput")
    woT = dr("woT", [D, D], F32, "ExternalInput")
    wdq = dr("wdq", [D, D], BF16, "ExternalInput")
    bias_t = dr("bias_t", [128, 32], F32, "ExternalInput")
    o_x2 = dr("o_x2", [D, T], F32, "ExternalOutput")

    S_ = Sched()
    TC = 512
    with contextlib.ExitStack() as st:
        sb = lambda n, s, d: st.enter_context(nc.sbuf_tensor(n, s, d))
        ps = lambda n, s, d: st.enter_context(nc.psum_tensor(n, s, d))

        ybf = sb("ybf", [128, 32 * TC], BF16)
        wcol = sb("wcol", [128, 32 * 128], BF16)
        xf32 = sb("xf32", [128, TC], F32)
        r16 = sb("r16", [128, 32 * TC], F16)
        x1b = sb("x1b", [128, 32 * TC], BF16)
        x1h = sb("x1h", [128, 32 * TC], F16)
        sq = sb("sq", [128, TC], F16)
        onecol = sb("onecol", [128, 1], F16)
        onerow = sb("onerow", [1, 128], F32)
        bias_s = sb("bias_s", [128, 32], F32)
        tmp32 = sb("tmp32", [128, TC], F32)
        rowa = sb("rowa", [1, TC], F32)
        rowb = sb("rowb", [1, TC], F32)
        rowc = sb("rowc", [1, TC], F32)
        m_bc = sb("m_bc", [128, TC], F32)
        rstd_bc = sb("rstd_bc", [128, TC], F32)
        c_eps5 = sb("c_eps5", [1, 1], F32)

        pA = ps("pA", [128, TC], F32)
        pR = ps("pR", [1, 2 * TC], F32)
        pB = ps("pB", [128, TC], F32)

        S_.memset(onecol[:], 1.0)
        S_.memset(onerow[:], 1.0)
        S_.memset(c_eps5[:], 1e-5)
        S_.dma(bias_s[:], bias_t)

        def wload(dst, wT_dram, ocol0, owidth, nk, cast):
            S_.dma(dst[:, 0:nk * owidth].rearrange("p (k o) -> p k o", k=nk),
                   wT_dram[:, ocol0:ocol0 + owidth].rearrange("(k p) o -> p k o", p=128),
                   cast=cast)

        def layer_norm(src16, dst_bf, dst_f16, out_dram=None, tc0=0):
            S_.act(rowa[:], pR[0:1, 0:TC], AF.Copy, scale=1.0 / D)
            S_.act(rowb[:], pR[0:1, TC:2 * TC], AF.Copy, scale=1.0 / D)
            S_.tt(rowc[:], rowa[:], rowa[:], ALU.mult)
            S_.tt(rowb[:], rowb[:], rowc[:], ALU.subtract)
            S_.act(rowb[:], rowb[:], AF.Ln, bias=c_eps5[0:1, 0:1])
            S_.act(rowb[:], rowb[:], AF.Exp, scale=-0.5)
            S_.mm(pB[:, :], onerow[:], rowa[0:1, :])
            S_.vcopy(m_bc[:], pB[:, :])
            S_.mm(pB[:, :], onerow[:], rowb[0:1, :])
            S_.vcopy(rstd_bc[:], pB[:, :])
            for ot in range(32):
                sl = src16[:, ot * TC:(ot + 1) * TC]
                S_.tt(tmp32[:], sl, m_bc[:], ALU.subtract)
                if dst_bf is not None:
                    S_.tt(dst_bf[:, ot * TC:(ot + 1) * TC], tmp32[:], rstd_bc[:], ALU.mult)
                if dst_f16 is not None:
                    S_.tt(dst_f16[:, ot * TC:(ot + 1) * TC], tmp32[:], rstd_bc[:], ALU.mult)
                if out_dram is not None:
                    S_.tt(tmp32[:], tmp32[:], rstd_bc[:], ALU.mult)
                    S_.dma(out_dram[ot * 128:(ot + 1) * 128, tc0:tc0 + TC], tmp32[:])

        for tc in range(0, T, TC):
            for t in range(32):
                S_.dma(ybf[:, t * TC:(t + 1) * TC], yT[t * 128:(t + 1) * 128, tc:tc + TC])
            for ot in range(32):
                wload(wcol, woT, ot * 128, 128, 32, True)
                for k in range(32):
                    S_.mm(pA[:, :], wcol[:, k * 128:(k + 1) * 128], ybf[:, k * TC:(k + 1) * TC],
                          start=(k == 0), stop=(k == 31))
                S_.dma(xf32[:], xT[ot * 128:(ot + 1) * 128, tc:tc + TC])
                S_.tt(r16[:, ot * TC:(ot + 1) * TC], pA[:, :], xf32[:], ALU.add)
                S_.act(sq[:], r16[:, ot * TC:(ot + 1) * TC], AF.Square)
                S_.mm(pR[0:1, 0:TC], onecol[:], r16[:, ot * TC:(ot + 1) * TC],
                      start=(ot == 0), stop=(ot == 31))
                S_.mm(pR[0:1, TC:2 * TC], onecol[:], sq[:], start=(ot == 0), stop=(ot == 31))
            layer_norm(r16, x1b, x1h)
            for ot in range(32):
                wload(wcol, wdq, ot * 128, 128, 32, False)
                for k in range(32):
                    S_.mm(pA[:, :], wcol[:, k * 128:(k + 1) * 128], x1b[:, k * TC:(k + 1) * TC],
                          start=(k == 0), stop=(k == 31))
                S_.stt(r16[:, ot * TC:(ot + 1) * TC], pA[:, :], bias_s[:, ot:ot + 1],
                       x1h[:, ot * TC:(ot + 1) * TC], ALU.add, ALU.add)
                S_.act(sq[:], r16[:, ot * TC:(ot + 1) * TC], AF.Square)
                S_.mm(pR[0:1, 0:TC], onecol[:], r16[:, ot * TC:(ot + 1) * TC],
                      start=(ot == 0), stop=(ot == 31))
                S_.mm(pR[0:1, TC:2 * TC], onecol[:], sq[:], start=(ot == 0), stop=(ot == 31))
            layer_norm(r16, None, None, out_dram=o_x2, tc0=tc)

        S_.replay(nc)
    return nc


def host_prep_c(inputs, res_a, res_b):
    wo = np.asarray(inputs["L0_attn_wo"], np.float32)
    bias = np.asarray(inputs["L0_ffn_bias"], np.float32)
    x = np.asarray(inputs["L0_attn_x"], np.float32)
    wdq_full = np.concatenate([res_a[c]["o_wdq"] for c in range(8)], axis=1)
    y_fm = []
    for b in range(2):
        ytm = np.concatenate([res_b[4 * b + i]["o_y"] for i in range(4)], axis=1)
        y_fm.append(np.ascontiguousarray(ytm.T))
    woT = _t(wo)
    bias_t = np.ascontiguousarray(bias.reshape(32, 128).T)
    maps = []
    for c in range(NCORES):
        b, t0 = c // 4, (c % 4) * T
        maps.append({
            "yT": np.ascontiguousarray(y_fm[b][:, t0:t0 + T]),
            "xT": np.ascontiguousarray(x[b, t0:t0 + T, :].T),
            "woT": woT, "wdq": wdq_full, "bias_t": bias_t,
        })
    return maps



def build_launch_d():
    nc = bass.Bass("TRN2", target_bir_lowering=False, debug=False)
    dr = lambda n, s, d, k: nc.dram_tensor(n, s, d, kind=k).ap()

    x2T = dr("x2T", [D, T], F32, "ExternalInput")
    WqT = dr("WqT", [D, D], F32, "ExternalInput")
    WkT = dr("WkT", [D, 1024], F32, "ExternalInput")
    WvT = dr("WvT", [D, 1024], F32, "ExternalInput")
    WinT = dr("WinT", [D, 32], F32, "ExternalInput")
    dtb = dr("dtb", [32, 1], F32, "ExternalInput")
    alog = dr("alog", [32, 1], F32, "ExternalInput")
    cos1 = dr("cos1", [64, T], F32, "ExternalInput")
    sin1 = dr("sin1", [64, T], F32, "ExternalInput")

    o_qr = dr("o_qr", [D, T], BF16, "ExternalOutput")
    o_kr = dr("o_kr", [1024, T], BF16, "ExternalOutput")
    o_v = dr("o_v", [1024, T], BF16, "ExternalOutput")
    o_dt = dr("o_dt", [32, T], F32, "ExternalOutput")
    o_g = dr("o_g", [32, T], F32, "ExternalOutput")

    S_ = Sched()
    with contextlib.ExitStack() as st:
        sb = lambda n, s, d: st.enter_context(nc.sbuf_tensor(n, s, d))
        ps = lambda n, s, d: st.enter_context(nc.psum_tensor(n, s, d))

        xb = sb("xb", [128, 32 * T], BF16)
        wcol = sb("wcol", [128, 32 * 128], BF16)
        qf = sb("qf", [128, T], F32)
        xh0 = sb("xh0", [64, T], F32)
        pr1 = sb("pr1", [64, T], F32)
        pr2 = sb("pr2", [64, T], F32)
        ro1 = sb("ro1", [64, T], BF16)
        ro2 = sb("ro2", [64, T], BF16)
        trig_c = sb("trig_c", [64, T], F32)
        trig_s = sb("trig_s", [64, T], F32)
        otile = sb("otile", [128, T], BF16)
        dtb_s = sb("dtb_s", [32, 1], F32)
        alog_s = sb("alog_s", [32, 1], F32)
        a1 = sb("a1", [32, T], F32)
        l1 = sb("l1", [32, T], F32)
        r1v = sb("r1v", [32, T], F32)
        dt_s = sb("dt_s", [32, T], F32)
        g_s = sb("g_s", [32, T], F32)
        nega = sb("nega", [32, 1], F32)
        c_one = sb("c_one", [32, 1], F32)

        pA = ps("pA", [128, T], F32)
        pD = ps("pD", [32, T], F32)

        for t in range(32):
            S_.dma(xb[:, t * T:(t + 1) * T], x2T[t * 128:(t + 1) * 128, :], cast=True)
        S_.dma(trig_c[:], cos1)
        S_.dma(trig_s[:], sin1)
        S_.dma(dtb_s[:], dtb)
        S_.dma(alog_s[:], alog)
        S_.memset(c_one[:], 1.0)

        def wload(dst, wT_dram, ocol0, owidth, nk, cast=True):
            S_.dma(dst[:, 0:nk * owidth].rearrange("p (k o) -> p k o", k=nk),
                   wT_dram[:, ocol0:ocol0 + owidth].rearrange("(k p) o -> p k o", p=128),
                   cast=cast)

        def proj_tile(psum, wT_dram, ocol0, owidth):
            wload(wcol, wT_dram, ocol0, owidth, 32)
            for k in range(32):
                l = wcol[0:128, k * owidth:(k + 1) * owidth]
                for h in range(0, T, 512):
                    S_.mm(psum[0:owidth, h:h + 512], l, xb[:, k * T + h:k * T + h + 512],
                          start=(k == 0), stop=(k == 31))

        def rope1(dst_dram, row0):
            S_.vcopy(xh0[:], qf[64:128, :])
            S_.tt(pr1[:], qf[0:64, :], trig_c[:], ALU.mult)
            S_.tt(pr2[:], xh0[:], trig_s[:], ALU.mult)
            S_.tt(pr1[:], pr1[:], pr2[:], ALU.subtract)
            S_.vcopy(ro1[:], pr1[:])
            S_.tt(pr1[:], xh0[:], trig_c[:], ALU.mult)
            S_.tt(pr2[:], qf[0:64, :], trig_s[:], ALU.mult)
            S_.tt(pr1[:], pr1[:], pr2[:], ALU.add)
            S_.vcopy(ro2[:], pr1[:])
            S_.dma(dst_dram[row0:row0 + 64, :], ro1[:])
            S_.dma(dst_dram[row0 + 64:row0 + 128, :], ro2[:])

        for hh in range(32):
            proj_tile(pA, WqT, hh * 128, 128)
            S_.vcopy(qf[:], pA[:, :])
            rope1(o_qr, hh * 128)
        for hh in range(8):
            proj_tile(pA, WkT, hh * 128, 128)
            S_.vcopy(qf[:], pA[:, :])
            rope1(o_kr, hh * 128)
        for hh in range(8):
            proj_tile(pA, WvT, hh * 128, 128)
            S_.vcopy(otile[:], pA[:, :])
            S_.dma(o_v[hh * 128:(hh + 1) * 128, :], otile[:])
        proj_tile(pD, WinT, 0, 32)
        S_.act(a1[:], pD[0:32, :], AF.Abs, bias=dtb_s[:, 0:1])
        S_.act(a1[:], a1[:], AF.Exp, scale=-1.0)
        S_.act(l1[:], a1[:], AF.Ln, bias=c_one[:, 0:1])
        S_.act(r1v[:], pD[0:32, :], AF.Relu, bias=dtb_s[:, 0:1])
        S_.tt(dt_s[:], l1[:], r1v[:], ALU.add)
        S_.dma(o_dt, dt_s[:])
        S_.act(nega[:], alog_s[:], AF.Exp)
        S_.ts(nega[:], nega[:], -1.0, None, ALU.mult)
        S_.ts(g_s[:], dt_s[:], nega[:, 0:1], None, ALU.mult)
        S_.dma(o_g, g_s[:])

        S_.replay(nc)
    return nc


def host_prep_d(inputs, x2_by_core):
    Wq = np.asarray(inputs["L1_attn_Wq"], np.float32)
    Wk = np.asarray(inputs["L1_attn_Wk"], np.float32)
    Wv = np.asarray(inputs["L1_attn_Wv"], np.float32)
    Win = np.asarray(inputs["L1_attn_Win"], np.float32)
    dtb = np.asarray(inputs["L1_attn_dt_bias"], np.float32).reshape(32, 1)
    alog = np.asarray(inputs["L1_attn_A_log_bias"], np.float32).reshape(32, 1)
    WqT, WkT, WvT, WinT = _t(Wq), _t(Wk), _t(Wv), _t(Win)
    maps = []
    for c in range(NCORES):
        t0 = (c % 4) * T
        cos1, sin1 = _rope_tabs_l1(np.arange(t0, t0 + T))
        maps.append({
            "x2T": x2_by_core[c], "WqT": WqT, "WkT": WkT, "WvT": WvT, "WinT": WinT,
            "dtb": dtb, "alog": alog, "cos1": cos1, "sin1": sin1,
        })
    return maps



def build_launch_e():
    nc = bass.Bass("TRN2", target_bir_lowering=False, debug=False)
    dr = lambda n, s, d, k: nc.dram_tensor(n, s, d, kind=k).ap()

    q_h = dr("q_h", [1024, S], BF16, "ExternalInput")
    k_h = dr("k_h", [256, S], BF16, "ExternalInput")
    v_h = dr("v_h", [256, S], BF16, "ExternalInput")
    dt_tm = dr("dt_tm", [S, 8], F32, "ExternalInput")
    g_ch = dr("g_ch", [128, 256], F32, "ExternalInput")
    gnt = dr("gnt", [128, 128], F32, "ExternalInput")
    identf_in = dr("identf_in", [128, 128], F32, "ExternalInput")
    triuf_in = dr("triuf_in", [128, 128], F32, "ExternalInput")
    masknegD_in = dr("masknegD_in", [128, 128], F32, "ExternalInput")

    o_y1 = dr("o_y1", [S, 1024], BF16, "ExternalOutput")
    grows = nc.dram_tensor("grows", [256, 128], F32, kind="Internal")

    S_ = Sched()
    with contextlib.ExitStack() as st:
        sb = lambda n, s, d: st.enter_context(nc.sbuf_tensor(n, s, d))
        ps = lambda n, s, d: st.enter_context(nc.psum_tensor(n, s, d))

        qh_s = sb("qh_s", [128, S], BF16)
        k_s = sb("k_s", [128, 2 * S], BF16)
        v_s = sb("v_s", [128, 2 * S], BF16)
        dt_s = sb("dt_s", [128, 32 * 8], F32)
        g_s = sb("g_s", [128, 256], F32)
        G_all = sb("G_all", [128, 256], F32)
        negG = sb("negG", [128, 256], F32)
        bGl = sb("bGl", [128, 256], F32)
        GT_s = sb("GT_s", [128, 256], F32)
        Gl_row = sb("Gl_row", [1, 256], F32)
        grow_c = sb("grow_c", [1, 4096], F32)
        onecol = sb("onecol", [128, 1], F32)
        onerow = sb("onerow", [1, 128], F32)
        S_a = sb("S_a", [128, 128], F32)
        S_b = sb("S_b", [128, 128], F32)
        St_bf = sb("St_bf", [128, 128], BF16)
        kt_tm = sb("kt_tm", [128, 128], BF16)
        vt_tm = sb("vt_tm", [128, 128], BF16)
        xc_tm = sb("xc_tm", [128, 128], BF16)
        kdec = sb("kdec", [128, 128], BF16)
        A_T = sb("A_T", [128, 128], BF16)
        Dm = sb("Dm", [128, 128], F32)
        t1 = sb("t1", [128, 128], F32)
        t2 = sb("t2", [128, 128], F32)
        y_s = sb("y_s", [128, 128], F32)
        y_n = sb("y_n", [128, 128], BF16)
        jnk = sb("jnk", [128, 128], F32)
        eG = sb("eG", [128, 1], F32)
        edec = sb("edec", [128, 1], F32)
        cd = sb("cd", [128, 1], F32)
        ss_col = sb("ss_col", [128, 1], F32)
        rstd_c = sb("rstd_c", [128, 1], F32)
        identf = sb("identf", [128, 128], F32)
        ident_b = sb("ident_b", [128, 128], BF16)
        triuf = sb("triuf", [128, 128], F32)
        masknegD = sb("masknegD", [128, 128], F32)
        gnorm_t = sb("gnorm_t", [128, 128], F32)
        c_eps5 = sb("c_eps5", [128, 1], F32)

        pS = ps("pS", [128, 128], F32)
        pK1 = ps("pK1", [128, 256], F32)
        pC = ps("pC", [128, 128], F32)
        pD = ps("pD", [128, 128], F32)
        pE = ps("pE", [128, 128], F32)
        pT = ps("pT", [128, 256], BF16)
        pRow = ps("pRow", [1, 256], F32)
        pG = ps("pG", [128, 256], F32)

        for t in range(2):
            S_.dma(k_s[:, t * S:(t + 1) * S], k_h[t * 128:(t + 1) * 128, :])
            S_.dma(v_s[:, t * S:(t + 1) * S], v_h[t * 128:(t + 1) * 128, :])
        S_.dma(dt_s[:, :].rearrange("p (c h) -> p c h", c=32),
               dt_tm.rearrange("(c p) h -> p c h", p=128))
        S_.dma(g_s[:], g_ch)
        S_.dma(identf[:], identf_in)
        S_.dma(ident_b[:], identf_in, cast=True)
        S_.dma(triuf[:], triuf_in)
        S_.dma(masknegD[:], masknegD_in)
        S_.dma(gnorm_t[:], gnt)
        S_.memset(onecol[:], 1.0)
        S_.memset(onerow[:], 1.0)
        S_.memset(c_eps5[:], 1e-5)

        S_.mm(pG[:, :], triuf[:], g_s[:])
        S_.vcopy(G_all[:], pG[:, :])
        S_.ts(negG[:], G_all[:], -1.0, None, ALU.mult)
        S_.mm(pRow[0:1, :], onecol[:], g_s[:])
        S_.vcopy(Gl_row[:], pRow[0:1, :])
        S_.mm(pK1[:, :], onerow[:], Gl_row[0:1, :])
        S_.vcopy(bGl[:], pK1[:, :])
        S_.tr(pG[:, 0:128], G_all[:, 0:128], identf[:])
        S_.tr(pG[:, 128:256], G_all[:, 128:256], identf[:])
        S_.vcopy(GT_s[:], pG[:, :])
        S_.dma(grows.ap()[0:128, :], GT_s[:, 0:128])
        S_.dma(grows.ap()[128:256, :], GT_s[:, 128:256])

        grows_flat = grows.ap().rearrange("u w -> (u w)")

        for hl in range(8):
            lkv = hl // 4
            S_.dma(qh_s[:, :], q_h[hl * 128:(hl + 1) * 128, :])
            S_.dma(grow_c[0:1, :], grows_flat[None, hl * 4096:(hl + 1) * 4096])
            S_.memset(S_a[:], 0.0)
            S_.memset(St_bf[:], 0.0)
            for c in range(32):
                u = hl * 32 + c
                G_col = G_all[:, u:u + 1]
                negG_col = negG[:, u:u + 1]
                bGl_col = bGl[:, u:u + 1]
                Sc, Sn = (S_a, S_b) if c % 2 == 0 else (S_b, S_a)
                S_.tr(pT[:, 0:128], k_s[:, lkv * S + c * 128:lkv * S + (c + 1) * 128], ident_b[:])
                S_.tr(pT[:, 128:256], v_s[:, lkv * S + c * 128:lkv * S + (c + 1) * 128], ident_b[:])
                S_.vcopy(kt_tm[:], pT[:, 0:128])
                S_.vcopy(vt_tm[:], pT[:, 128:256])
                S_.act(eG[:], G_col, AF.Exp)
                S_.act(edec[:], G_col, AF.Exp, scale=-1.0, bias=bGl_col)
                S_.act(cd[:], bGl_col, AF.Exp)
                S_.mm(pK1[:, 0:128], onerow[:], grow_c[0:1, c * 128:(c + 1) * 128])
                S_.tt(t1[:], pK1[:, 0:128], masknegD[:], ALU.add)
                S_.act(Dm[:], t1[:], AF.Exp, bias=negG_col)
                S_.mm(pS[:, :], k_s[:, lkv * S + c * 128:lkv * S + (c + 1) * 128],
                      qh_s[:, c * 128:(c + 1) * 128])
                S_.tt(A_T[:], pS[:, :], Dm[:], ALU.mult)
                S_.ts(xc_tm[:], vt_tm[:], dt_s[:, c * 8 + hl:c * 8 + hl + 1], None, ALU.mult)
                S_.ts(kdec[:], kt_tm[:], edec[:, 0:1], None, ALU.mult)
                S_.mm(pC[:, :], kdec[:], xc_tm[:])
                S_.mm(pD[:, :], A_T[:], xc_tm[:])
                S_.mm(pE[:, :], qh_s[:, c * 128:(c + 1) * 128], St_bf[:])
                S_.ts(t2[:], pE[:, :], eG[:, 0:1], None, ALU.mult)
                S_.tt(y_s[:], pD[:, :], t2[:], ALU.add)
                S_.act(jnk[:], y_s[:], AF.Square, accum_out=ss_col[:, 0:1])
                S_.act(rstd_c[:], ss_col[:], AF.Ln, scale=1.0 / 128, bias=c_eps5[:, 0:1])
                S_.act(rstd_c[:], rstd_c[:], AF.Exp, scale=-0.5)
                S_.stt(y_n[:], y_s[:], rstd_c[:, 0:1], gnorm_t[:], ALU.mult, ALU.mult)
                S_.dma(o_y1[c * 128:(c + 1) * 128, hl * 128:(hl + 1) * 128], y_n[:])
                S_.ts(Sn[:], Sc[:], cd[:, 0:1], None, ALU.mult)
                S_.tt(Sn[:], Sn[:], pC[:, :], ALU.add)
                S_.vcopy(St_bf[:], Sn[:])

        S_.replay(nc)
    return nc


def host_prep_e(inputs, res_d):
    qr_full, kr_full, v_full, dt_full, g_full = [], [], [], [], []
    for b in range(2):
        qr_full.append(np.concatenate([res_d[4 * b + i]["o_qr"] for i in range(4)], axis=1))
        kr_full.append(np.concatenate([res_d[4 * b + i]["o_kr"] for i in range(4)], axis=1))
        v_full.append(np.concatenate([res_d[4 * b + i]["o_v"] for i in range(4)], axis=1))
        dt_full.append(np.concatenate([res_d[4 * b + i]["o_dt"] for i in range(4)], axis=1))
        g_full.append(np.concatenate([res_d[4 * b + i]["o_g"] for i in range(4)], axis=1))
    ident = np.eye(128, dtype=np.float32)
    triu = np.triu(np.ones((128, 128), np.float32))
    masknegD = np.where(np.arange(128)[None, :] >= np.arange(128)[:, None],
                        0.0, -1e30).astype(np.float32)
    gnw = np.asarray(inputs["L1_attn_gnorm_w"], np.float32)
    gnt = np.tile(gnw[None, :], (128, 1)).astype(np.float32)
    maps = []
    for c in range(NCORES):
        b, h0 = c // 4, 8 * (c % 4)
        kvh0 = h0 // 4
        dt_sl = dt_full[b][h0:h0 + 8, :]
        g_sl = g_full[b][h0:h0 + 8, :]
        g_ch = np.ascontiguousarray(
            g_sl.reshape(8, 32, 128).transpose(2, 0, 1).reshape(128, 256))
        maps.append({
            "q_h": np.ascontiguousarray(qr_full[b][h0 * 128:(h0 + 8) * 128, :]),
            "k_h": np.ascontiguousarray(kr_full[b][kvh0 * 128:(kvh0 + 2) * 128, :]),
            "v_h": np.ascontiguousarray(v_full[b][kvh0 * 128:(kvh0 + 2) * 128, :]),
            "dt_tm": np.ascontiguousarray(dt_sl.T),
            "g_ch": g_ch,
            "gnt": gnt, "identf_in": ident, "triuf_in": triu, "masknegD_in": masknegD,
        })
    return maps



def build_launch_f():
    nc = bass.Bass("TRN2", target_bir_lowering=False, debug=False)
    dr = lambda n, s, d, k: nc.dram_tensor(n, s, d, kind=k).ap()

    y1T = dr("y1T", [D, T], BF16, "ExternalInput")
    x2T = dr("x2T", [D, T], F32, "ExternalInput")
    WgT = dr("WgT", [D, D], F32, "ExternalInput")
    WoT = dr("WoT", [D, D], F32, "ExternalInput")
    baseT = dr("baseT", [D, D], F32, "ExternalInput")
    gateT = dr("gateT", [D, 28], F32, "ExternalInput")
    loraAT = dr("loraAT", [D, 224], F32, "ExternalInput")
    loraBf = dr("loraBf", [224, D], F32, "ExternalInput")
    R1_in = dr("R1_in", [28, 128], F32, "ExternalInput")
    R2_in = dr("R2_in", [28, 96], F32, "ExternalInput")
    identf_in = dr("identf_in", [128, 128], F32, "ExternalInput")
    o_x4 = dr("o_x4", [D, T], F32, "ExternalOutput")

    S_ = Sched()
    TC = 256
    with contextlib.ExitStack() as st:
        sb = lambda n, s, d: st.enter_context(nc.sbuf_tensor(n, s, d))
        ps = lambda n, s, d: st.enter_context(nc.psum_tensor(n, s, d))

        y1b = sb("y1b", [128, 32 * TC], BF16)
        x2b = sb("x2b", [128, 32 * TC], BF16)
        ygb = sb("ygb", [128, 32 * TC], BF16)
        wcol = sb("wcol", [128, 32 * 128], BF16)
        xf32 = sb("xf32", [128, TC], F32)
        r16 = sb("r16", [128, 32 * TC], F16)
        x3b = sb("x3b", [128, 32 * TC], BF16)
        x3h = sb("x3h", [128, 32 * TC], F16)
        sq = sb("sq", [128, TC], F16)
        sg = sb("sg", [128, TC], BF16)
        onecol = sb("onecol", [128, 1], F16)
        onerow = sb("onerow", [1, 128], F32)
        tmp32 = sb("tmp32", [128, TC], F32)
        rowa = sb("rowa", [1, TC], F32)
        rowb = sb("rowb", [1, TC], F32)
        rowc = sb("rowc", [1, TC], F32)
        m_bc = sb("m_bc", [128, TC], F32)
        rstd_bc = sb("rstd_bc", [128, TC], F32)
        c_eps5 = sb("c_eps5", [1, 1], F32)
        lg_f = sb("lg_f", [28, TC], F32)
        lt = sb("lt", [128, 28], F32)
        msk1 = sb("msk1", [128, 28], F32)
        msk2 = sb("msk2", [128, 28], F32)
        l2t = sb("l2t", [128, 28], F32)
        tb = sb("tb", [128, 28], F32)
        ga_t = sb("ga_t", [128, 28], F32)
        gb_t = sb("gb_t", [128, 28], F32)
        gtm = sb("gtm", [128, 28], F32)
        m1c = sb("m1c", [128, 1], F32)
        m2c = sb("m2c", [128, 1], F32)
        dgc = sb("dgc", [128, 1], F32)
        p1c = sb("p1c", [128, 1], F32)
        p2c = sb("p2c", [128, 1], F32)
        gate_fm = sb("gate_fm", [28, TC], F32)
        gsc1 = sb("gsc1", [128, TC], F32)
        gsc2 = sb("gsc2", [96, TC], F32)
        hg1 = sb("hg1", [128, TC], BF16)
        hg2 = sb("hg2", [96, TC], BF16)
        lb1 = sb("lb1", [128, 128], BF16)
        lb2 = sb("lb2", [96, 128], BF16)
        identf = sb("identf", [128, 128], F32)
        R1s = sb("R1s", [28, 128], F32)
        R2s = sb("R2s", [28, 96], F32)

        pA = ps("pA", [128, TC], F32)
        pR = ps("pR", [1, 2 * TC], F32)
        pB = ps("pB", [128, TC], F32)
        pL = ps("pL", [28, TC], F32)
        pTr = ps("pTr", [128, 128], F32)
        pH = ps("pH", [128, TC], F32)

        S_.memset(onecol[:], 1.0)
        S_.memset(onerow[:], 1.0)
        S_.memset(c_eps5[:], 1e-5)
        S_.dma(identf[:], identf_in)
        S_.dma(R1s[:], R1_in)
        S_.dma(R2s[:], R2_in)

        def wload(dst, wT_dram, ocol0, owidth, nk, cast=True):
            S_.dma(dst[:, 0:nk * owidth].rearrange("p (k o) -> p k o", k=nk),
                   wT_dram[:, ocol0:ocol0 + owidth].rearrange("(k p) o -> p k o", p=128),
                   cast=cast)

        def layer_norm(src16, dst_bf, dst_f16, out_dram=None, tc0=0):
            S_.act(rowa[:], pR[0:1, 0:TC], AF.Copy, scale=1.0 / D)
            S_.act(rowb[:], pR[0:1, TC:2 * TC], AF.Copy, scale=1.0 / D)
            S_.tt(rowc[:], rowa[:], rowa[:], ALU.mult)
            S_.tt(rowb[:], rowb[:], rowc[:], ALU.subtract)
            S_.act(rowb[:], rowb[:], AF.Ln, bias=c_eps5[0:1, 0:1])
            S_.act(rowb[:], rowb[:], AF.Exp, scale=-0.5)
            S_.mm(pB[:, :], onerow[:], rowa[0:1, :])
            S_.vcopy(m_bc[:], pB[:, :])
            S_.mm(pB[:, :], onerow[:], rowb[0:1, :])
            S_.vcopy(rstd_bc[:], pB[:, :])
            for ot in range(32):
                sl = src16[:, ot * TC:(ot + 1) * TC]
                S_.tt(tmp32[:], sl, m_bc[:], ALU.subtract)
                if dst_bf is not None:
                    S_.tt(dst_bf[:, ot * TC:(ot + 1) * TC], tmp32[:], rstd_bc[:], ALU.mult)
                if dst_f16 is not None:
                    S_.tt(dst_f16[:, ot * TC:(ot + 1) * TC], tmp32[:], rstd_bc[:], ALU.mult)
                if out_dram is not None:
                    S_.tt(tmp32[:], tmp32[:], rstd_bc[:], ALU.mult)
                    S_.dma(out_dram[ot * 128:(ot + 1) * 128, tc0:tc0 + TC], tmp32[:])

        for tc in range(0, T, TC):
            for t in range(32):
                S_.dma(y1b[:, t * TC:(t + 1) * TC], y1T[t * 128:(t + 1) * 128, tc:tc + TC])
                S_.dma(x2b[:, t * TC:(t + 1) * TC], x2T[t * 128:(t + 1) * 128, tc:tc + TC],
                       cast=True)
            for ot in range(32):
                wload(wcol, WgT, ot * 128, 128, 32, True)
                for k in range(32):
                    S_.mm(pA[:, :], wcol[:, k * 128:(k + 1) * 128], x2b[:, k * TC:(k + 1) * TC],
                          start=(k == 0), stop=(k == 31))
                S_.act(sg[:], pA[:, :], AF.Silu)
                S_.tt(ygb[:, ot * TC:(ot + 1) * TC], y1b[:, ot * TC:(ot + 1) * TC],
                      sg[:], ALU.mult)
            for ot in range(32):
                wload(wcol, WoT, ot * 128, 128, 32, True)
                for k in range(32):
                    S_.mm(pA[:, :], wcol[:, k * 128:(k + 1) * 128], ygb[:, k * TC:(k + 1) * TC],
                          start=(k == 0), stop=(k == 31))
                S_.dma(xf32[:], x2T[ot * 128:(ot + 1) * 128, tc:tc + TC])
                S_.tt(r16[:, ot * TC:(ot + 1) * TC], pA[:, :], xf32[:], ALU.add)
                S_.act(sq[:], r16[:, ot * TC:(ot + 1) * TC], AF.Square)
                S_.mm(pR[0:1, 0:TC], onecol[:], r16[:, ot * TC:(ot + 1) * TC],
                      start=(ot == 0), stop=(ot == 31))
                S_.mm(pR[0:1, TC:2 * TC], onecol[:], sq[:], start=(ot == 0), stop=(ot == 31))
            layer_norm(r16, x3b, x3h)
            wload(wcol, gateT, 0, 28, 32, True)
            for k in range(32):
                S_.mm(pL[0:28, :], wcol[:, k * 28:(k + 1) * 28], x3b[:, k * TC:(k + 1) * TC],
                      start=(k == 0), stop=(k == 31))
            S_.vcopy(lg_f[:], pL[0:28, :])
            for j in range(TC // 128):
                S_.tr(pTr[0:128, 0:28], lg_f[0:28, j * 128:(j + 1) * 128], identf[0:28, 0:28])
                S_.vcopy(lt[:], pTr[0:128, 0:28])
                S_.rmax(m1c[:], lt[:])
                S_.ts(msk1[:], lt[:], m1c[:, 0:1], None, ALU.is_equal)
                S_.ts(tb[:], msk1[:], 1e30, None, ALU.mult)
                S_.tt(l2t[:], lt[:], tb[:], ALU.subtract)
                S_.rmax(m2c[:], l2t[:])
                S_.tt(dgc[:], m1c[:], m2c[:], ALU.subtract)
                S_.act(p1c[:], dgc[:], AF.Sigmoid)
                S_.ts(p2c[:], p1c[:], -1.0, 1.0, ALU.mult, ALU.add)
                S_.ts(msk2[:], l2t[:], m2c[:, 0:1], None, ALU.is_equal)
                S_.ts(ga_t[:], msk1[:], p1c[:, 0:1], None, ALU.mult)
                S_.ts(gb_t[:], msk2[:], p2c[:, 0:1], None, ALU.mult)
                S_.tt(gtm[:], ga_t[:], gb_t[:], ALU.add)
                S_.tr(pTr[0:28, 0:128], gtm[0:128, 0:28], identf[:])
                S_.vcopy(gate_fm[0:28, j * 128:(j + 1) * 128], pTr[0:28, 0:128])
            S_.mm(pB[0:128, :], R1s[:], gate_fm[0:28, :])
            S_.vcopy(gsc1[:], pB[0:128, :])
            S_.mm(pB[0:96, :], R2s[:], gate_fm[0:28, :])
            S_.vcopy(gsc2[:], pB[0:96, :])
            wload(wcol, loraAT, 0, 128, 32, True)
            for k in range(32):
                S_.mm(pH[:, :], wcol[:, k * 128:(k + 1) * 128], x3b[:, k * TC:(k + 1) * TC],
                      start=(k == 0), stop=(k == 31))
            S_.stt(hg1[:], pH[:, :], 2.0, gsc1[:], ALU.mult, ALU.mult)
            wload(wcol, loraAT, 128, 96, 32, True)
            for k in range(32):
                S_.mm(pH[0:96, :], wcol[:, k * 96:(k + 1) * 96], x3b[:, k * TC:(k + 1) * TC],
                      start=(k == 0), stop=(k == 31))
            S_.stt(hg2[:], pH[0:96, :], 2.0, gsc2[:], ALU.mult, ALU.mult)
            for ot in range(32):
                wload(wcol, baseT, ot * 128, 128, 32, True)
                for k in range(32):
                    S_.mm(pA[:, :], wcol[:, k * 128:(k + 1) * 128], x3b[:, k * TC:(k + 1) * TC],
                          start=(k == 0), stop=False)
                S_.dma(lb1[:], loraBf[0:128, ot * 128:(ot + 1) * 128], cast=True)
                S_.dma(lb2[:], loraBf[128:224, ot * 128:(ot + 1) * 128], cast=True)
                S_.mm(pA[:, :], lb1[:], hg1[:], start=False, stop=False)
                S_.mm(pA[:, :], lb2[0:96, :], hg2[0:96, :], start=False, stop=True)
                S_.tt(r16[:, ot * TC:(ot + 1) * TC], pA[:, :], x3h[:, ot * TC:(ot + 1) * TC],
                      ALU.add)
                S_.act(sq[:], r16[:, ot * TC:(ot + 1) * TC], AF.Square)
                S_.mm(pR[0:1, 0:TC], onecol[:], r16[:, ot * TC:(ot + 1) * TC],
                      start=(ot == 0), stop=(ot == 31))
                S_.mm(pR[0:1, TC:2 * TC], onecol[:], sq[:], start=(ot == 0), stop=(ot == 31))
            layer_norm(r16, None, None, out_dram=o_x4, tc0=tc)

        S_.replay(nc)
    return nc


def host_prep_f(inputs, res_e, x2_by_core):
    Wg = np.asarray(inputs["L1_attn_Wg"], np.float32)
    Wo = np.asarray(inputs["L1_attn_Wo"], np.float32)
    baseW = np.asarray(inputs["L1_ffn_base_W"], np.float32)
    gateW = np.asarray(inputs["L1_ffn_gate_W"], np.float32)
    loraA = np.asarray(inputs["L1_ffn_lora_A"], np.float32)
    loraB = np.asarray(inputs["L1_ffn_lora_B"], np.float32)
    WgT, WoT, baseT, gateT = _t(Wg), _t(Wo), _t(baseW), _t(gateW)
    loraAT = _t(loraA.reshape(224, 4096))
    loraBf = np.ascontiguousarray(loraB.transpose(0, 2, 1).reshape(224, 4096))
    R = np.zeros((28, 224), np.float32)
    for e in range(28):
        R[e, 8 * e:8 * e + 8] = 1.0
    ident = np.eye(128, dtype=np.float32)
    y1_fm = []
    for b in range(2):
        ytm = np.concatenate([res_e[4 * b + i]["o_y1"] for i in range(4)], axis=1)
        y1_fm.append(np.ascontiguousarray(ytm.T))
    maps = []
    for c in range(NCORES):
        b, t0 = c // 4, (c % 4) * T
        maps.append({
            "y1T": np.ascontiguousarray(y1_fm[b][:, t0:t0 + T]),
            "x2T": x2_by_core[c],
            "WgT": WgT, "WoT": WoT, "baseT": baseT, "gateT": gateT,
            "loraAT": loraAT, "loraBf": loraBf,
            "R1_in": np.ascontiguousarray(R[:, 0:128]),
            "R2_in": np.ascontiguousarray(R[:, 128:224]),
            "identf_in": ident,
        })
    return maps




IOA = bass.IndirectOffsetOnAxis
GROUPS4 = [[0, 1, 2, 3], [4, 5, 6, 7]]
GROUPS8 = [[0, 1, 2, 3, 4, 5, 6, 7]]


def _flat(dram_handle):
    return dram_handle.ap().rearrange("r c -> (r c)")[:, None]


_PHASE_LIMIT = 99
_F_TC_LIMIT = 99
_F_SEC = 99
_SERIALIZE = 'dveact'


def build_fused():
    nc = bass.Bass("TRN2", target_bir_lowering=False, debug=False)
    dr = lambda n, s, d, k: nc.dram_tensor(n, s, d, kind=k).ap()
    dri = lambda n, s, d: nc.dram_tensor(n, s, d, kind="Internal")

    xT = dr("xT", [D, T], F32, "ExternalInput")
    wqaT = dr("wqaT", [D, 1536], F32, "ExternalInput")
    wqbnT = dr("wqbnT", [1536, 4096], F32, "ExternalInput")
    wqbpT = dr("wqbpT", [1536, 2048], F32, "ExternalInput")
    qnw = dr("qnw", [128, 12], F32, "ExternalInput")
    wkvaT = dr("wkvaT", [D, 576], F32, "ExternalInput")
    kvnw = dr("kvnw", [128, 4], F32, "ExternalInput")
    cosp = dr("cosp", [32, T], F32, "ExternalInput")
    sinp = dr("sinp", [32, T], F32, "ExternalInput")
    qwT = dr("qwT", [512, 512], I32, "ExternalInput")
    scT = dr("scT", [512, 32], F32, "ExternalInput")
    zrT = dr("zrT", [512, 32], F32, "ExternalInput")
    identf_in = dr("identf_in", [128, 128], F32, "ExternalInput")
    maskneg_in = dr("maskneg_in", [128, 128], F32, "ExternalInput")
    wkvkT = dr("wkvkT", [512, 1024], F32, "ExternalInput")
    wkvvT = dr("wkvvT", [512, 1024], F32, "ExternalInput")
    woT = dr("woT", [D, D], F32, "ExternalInput")
    bias_t = dr("bias_t", [128, 32], F32, "ExternalInput")
    WqT = dr("WqT", [D, D], F32, "ExternalInput")
    WkT = dr("WkT", [D, 1024], F32, "ExternalInput")
    WvT = dr("WvT", [D, 1024], F32, "ExternalInput")
    WinT = dr("WinT", [D, 32], F32, "ExternalInput")
    dtb = dr("dtb", [32, 1], F32, "ExternalInput")
    alog = dr("alog", [32, 1], F32, "ExternalInput")
    cos1 = dr("cos1", [64, T], F32, "ExternalInput")
    sin1 = dr("sin1", [64, T], F32, "ExternalInput")
    gnt = dr("gnt", [128, 128], F32, "ExternalInput")
    triuf_in = dr("triuf_in", [128, 128], F32, "ExternalInput")
    masknegD_in = dr("masknegD_in", [128, 128], F32, "ExternalInput")
    WgT = dr("WgT", [D, D], F32, "ExternalInput")
    WoT1 = dr("WoT1", [D, D], F32, "ExternalInput")
    baseT = dr("baseT", [D, D], F32, "ExternalInput")
    gateT = dr("gateT", [D, 28], F32, "ExternalInput")
    loraAT = dr("loraAT", [D, 224], F32, "ExternalInput")
    loraBf = dr("loraBf", [224, D], F32, "ExternalInput")
    R1_in = dr("R1_in", [28, 128], F32, "ExternalInput")
    R2_in = dr("R2_in", [28, 96], F32, "ExternalInput")
    idxB_qn = dr("idxB_qn", [128, 32], I32, "ExternalInput")
    idxB_qp = dr("idxB_qp", [64, 32], I32, "ExternalInput")
    idxY = dr("idxY", [128, 128], I32, "ExternalInput")
    idxE_kv = dr("idxE_kv", [128, 8], I32, "ExternalInput")
    idxE_dtg = dr("idxE_dtg", [128, 32], I32, "ExternalInput")

    o_x4 = dr("o_x4", [D, T], F32, "ExternalOutput")

    qn_l = dri("qn_l", [4096, T], BF16); qn_g = dri("qn_g", [4 * 4096, T], BF16)
    qp_l = dri("qp_l", [2048, T], BF16); qp_g = dri("qp_g", [4 * 2048, T], BF16)
    ckv_l = dri("ckv_l", [512, T], BF16); ckv_g = dri("ckv_g", [4 * 512, T], BF16)
    kpe_l = dri("kpe_l", [64, T], BF16); kpe_g = dri("kpe_g", [4 * 64, T], BF16)
    wdq_l = dri("wdq_l", [4096, 512], BF16); wdq_g = dri("wdq_g", [8 * 4096, 512], BF16)
    y_l = dri("y_l", [4096, T], BF16); y_g = dri("y_g", [4 * 4096, T], BF16)
    x2_l = dri("x2_l", [D, T], F32)
    qr_l = dri("qr_l", [4096, T], BF16); qr_g = dri("qr_g", [4 * 4096, T], BF16)
    kr_l = dri("kr_l", [1024, T], BF16); kr_g = dri("kr_g", [4 * 1024, T], BF16)
    v_l = dri("v_l", [1024, T], BF16); v_g = dri("v_g", [4 * 1024, T], BF16)
    dtg_l = dri("dtg_l", [4096, 16], F32); dtg_g = dri("dtg_g", [4 * 4096, 16], F32)
    y1_l = dri("y1_l", [4096, T], BF16); y1_g = dri("y1_g", [4 * 4096, T], BF16)
    grows = dri("grows", [256, 128], F32)

    S_ = Sched()
    gst = contextlib.ExitStack()
    with gst:
        gsb = lambda n, s, d: gst.enter_context(nc.sbuf_tensor(n, s, d))
        identf = gsb("identf", [128, 128], F32)
        ident_b = gsb("ident_b", [128, 128], BF16)
        xB_qn = gsb("xB_qn", [128, 32], I32)
        xB_qp = gsb("xB_qp", [64, 32], I32)
        xY = gsb("xY", [128, 128], I32)
        xE_kv = gsb("xE_kv", [128, 8], I32)
        xE_dtg = gsb("xE_dtg", [128, 32], I32)

        S_.dma(identf[:], identf_in)
        S_.dma(ident_b[:], identf_in, cast=True)
        S_.dma(xB_qn[:], idxB_qn)
        S_.dma(xB_qp[:], idxB_qp)
        S_.dma(xY[:], idxY)
        S_.dma(xE_kv[:], idxE_kv)
        S_.dma(xE_dtg[:], idxE_dtg)

        def coll(kind, groups, in_h, out_h):
            S_._op('gpsimd', 1,
                   lambda e, k=kind, g=groups, i=in_h, o=out_h:
                   e.collective_compute(k, ALU.bypass, replica_groups=g,
                                        ins=[i.ap().opt()], outs=[o.ap().opt()]),
                   reads=[in_h.ap()], writes=[out_h.ap()], is_coll=True)

        def idma(out_ap, flat_src, idx_col):
            S_._op('gpsimd', 16,
                   lambda e, o=out_ap, s=flat_src, x=idx_col:
                   e.indirect_dma_start(o, None, s, IOA(x, 0)),
                   reads=[flat_src, idx_col], writes=[out_ap])

        with contextlib.ExitStack() as st:
            sb = lambda n, s, d: st.enter_context(nc.sbuf_tensor(n, s, d))
            ps = lambda n, s, d: st.enter_context(nc.psum_tensor(n, s, d))
            x_bf = sb("a_x_bf", [128, 32 * T], BF16)
            wcol = sb("a_wcol", [128, 2 * 32 * 128], BF16)
            wsel = [0]
            qlr = sb("a_qlr", [128, 12 * T], BF16)
            qln = sb("a_qln", [128, 12 * T], BF16)
            kvr = sb("a_kvr", [128, 5 * T], BF16)
            sq = sb("a_sq", [128, T], F16)
            row = sb("a_row", [1, T], F32)
            bc = sb("a_bc", [128, T], F32)
            onecol = sb("a_onecol", [128, 1], F16)
            onerow = sb("a_onerow", [1, 128], F32)
            c_eps6 = sb("a_c_eps6", [128, 1], F32)
            qnw_s = sb("a_qnw_s", [128, 12], F32)
            kvnw_s = sb("a_kvnw_s", [128, 4], F32)
            trig = sb("a_trig", [64, 2 * T], F32)
            rp = [sb(f"a_rp{i}", [32, T], F16) for i in range(4)]
            ro = [sb(f"a_ro{i}", [32, T], BF16) for i in range(2)]
            x2c = sb("a_x2c", [32, T], F16)
            otile = sb("a_otile", [128, T], BF16)
            qw_s = sb("a_qw_s", [128, 4 * 512], I32)
            qtmp = sb("a_qtmp", [128, 512], I32)
            wun = sb("a_wun", [128, 4096], F16)
            waf = sb("a_waf", [128, 4096], BF16)
            sc_s = sb("a_sc_s", [128, 4 * 32], F32)
            zr_s = sb("a_zr_s", [128, 4 * 32], F32)
            wtr = sb("a_wtr", [128, 512], BF16)
            pQ = ps("a_pQ", [128, T], F32)
            pR = ps("a_pR", [1, T], F32)
            pB = ps("a_pB", [128, T], F32)
            pT = ps("a_pT", [128, 512], BF16)

            for t in range(32):
                S_.dma(x_bf[:, t * T:(t + 1) * T], xT[t * 128:(t + 1) * 128, :], cast=True)
            S_.dma(qnw_s[:], qnw)
            S_.dma(kvnw_s[:], kvnw)
            S_.dma(trig[0:32, 0:T], cosp)
            S_.dma(trig[0:32, T:2 * T], sinp)
            S_.memset(onecol[:], 1.0)
            S_.memset(onerow[:], 1.0)
            S_.memset(c_eps6[:], 1e-6)

            def mm_otile(psum, wT_dram, ocol0, owidth, rhs_sb, nk):
                b0 = wsel[0] * 4096
                wsel[0] ^= 1
                S_.dma(wcol[:, b0:b0 + nk * owidth].rearrange("p (k o) -> p k o", k=nk),
                       wT_dram[:, ocol0:ocol0 + owidth].rearrange("(k p) o -> p k o", p=128),
                       cast=True)
                for k in range(nk):
                    l = wcol[0:128, b0 + k * owidth:b0 + (k + 1) * owidth]
                    for h in range(0, T, 512):
                        S_.mm(psum[0:owidth, h:h + 512], l, rhs_sb[:, k * T + h:k * T + h + 512],
                              start=(k == 0), stop=(k == nk - 1))

            for ot in range(12):
                mm_otile(pQ, wqaT, ot * 128, 128, x_bf, 32)
                S_.vcopy(qlr[:, ot * T:(ot + 1) * T], pQ[:, :])
                S_.act(sq[:], pQ[:, :], AF.Square)
                for h in range(0, T, 512):
                    S_.mm(pR[0:1, h:h + 512], onecol[:], sq[:, h:h + 512],
                          start=(ot == 0), stop=(ot == 11))
            S_.act(row[:], pR[0:1, :], AF.Ln, bias=c_eps6[0:1, 0:1], scale=1.0 / 1536)
            S_.act(row[:], row[:], AF.Exp, scale=-0.5)
            for h in range(0, T, 512):
                S_.mm(pB[:, h:h + 512], onerow[:], row[0:1, h:h + 512])
            S_.vcopy(bc[:], pB[:, :])
            for ot in range(12):
                S_.stt(qln[:, ot * T:(ot + 1) * T], qlr[:, ot * T:(ot + 1) * T],
                       qnw_s[:, ot:ot + 1], bc[:], ALU.mult, ALU.mult)

            for ot in range(4):
                mm_otile(pQ, wkvaT, ot * 128, 128, x_bf, 32)
                S_.vcopy(kvr[:, ot * T:(ot + 1) * T], pQ[:, :])
                S_.act(sq[:], pQ[:, :], AF.Square)
                for h in range(0, T, 512):
                    S_.mm(pR[0:1, h:h + 512], onecol[:], sq[:, h:h + 512],
                          start=(ot == 0), stop=(ot == 3))
            mm_otile(pQ, wkvaT, 512, 64, x_bf, 32)
            S_.vcopy(kvr[0:64, 4 * T:5 * T], pQ[0:64, :])
            S_.act(row[:], pR[0:1, :], AF.Ln, bias=c_eps6[0:1, 0:1], scale=1.0 / 512)
            S_.act(row[:], row[:], AF.Exp, scale=-0.5)
            for h in range(0, T, 512):
                S_.mm(pB[:, h:h + 512], onerow[:], row[0:1, h:h + 512])
            S_.vcopy(bc[:], pB[:, :])
            for ot in range(4):
                S_.stt(otile[:, :], kvr[:, ot * T:(ot + 1) * T],
                       kvnw_s[:, ot:ot + 1], bc[:], ALU.mult, ALU.mult)
                S_.dma(ckv_l.ap()[ot * 128:(ot + 1) * 128, :], otile[:, :])

            def rope_deint(x1ap, x2ap):
                S_.vcopy(x2c[:, :], x2ap)
                cos_t = trig[0:32, 0:T]
                sin_t = trig[0:32, T:2 * T]
                S_.tt(rp[0][:, :], x1ap, cos_t, ALU.mult)
                S_.tt(rp[1][:, :], x2c[:, :], sin_t, ALU.mult)
                S_.tt(rp[2][:, :], x1ap, sin_t, ALU.mult)
                S_.tt(rp[3][:, :], x2c[:, :], cos_t, ALU.mult)
                S_.tt(ro[0][:, :], rp[0][:, :], rp[1][:, :], ALU.subtract)
                S_.tt(ro[1][:, :], rp[2][:, :], rp[3][:, :], ALU.add)

            rope_deint(kvr[0:32, 4 * T:5 * T], kvr[32:64, 4 * T:5 * T])
            S_.dma(kpe_l.ap()[0:32, :], ro[0][:, :])
            S_.dma(kpe_l.ap()[32:64, :], ro[1][:, :])

            for ot in range(32):
                mm_otile(pQ, wqbnT, ot * 128, 128, qln, 12)
                S_.vcopy(otile[:, :], pQ[:, :])
                S_.dma(qn_l.ap()[ot * 128:(ot + 1) * 128, :], otile[:, :])

            for hh in range(32):
                mm_otile(pQ, wqbpT, hh * 64, 64, qln, 12)
                S_.vcopy(kvr[0:64, 4 * T:5 * T], pQ[0:64, :])
                rope_deint(kvr[0:32, 4 * T:5 * T], kvr[32:64, 4 * T:5 * T])
                S_.dma(qp_l.ap()[hh * 64:hh * 64 + 32, :], ro[0][:, :])
                S_.dma(qp_l.ap()[hh * 64 + 32:hh * 64 + 64, :], ro[1][:, :])

            for t in range(4):
                S_.dma(qw_s[:, t * 512:(t + 1) * 512], qwT[t * 128:(t + 1) * 128, :])
                S_.dma(sc_s[:, t * 32:(t + 1) * 32], scT[t * 128:(t + 1) * 128, :])
                S_.dma(zr_s[:, t * 32:(t + 1) * 32], zrT[t * 128:(t + 1) * 128, :])
            for t in range(4):
                q = qw_s[:, t * 512:(t + 1) * 512]
                for j in range(8):
                    S_.ts(qtmp[:], q, float(4 * j), 15.0,
                          ALU.logical_shift_right, ALU.bitwise_and)
                    S_.vcopy(wun[:, j::8], qtmp[:])
                for g in range(32):
                    S_.ts(waf[:, g * 128:(g + 1) * 128], wun[:, g * 128:(g + 1) * 128],
                          sc_s[:, t * 32 + g:t * 32 + g + 1],
                          zr_s[:, t * 32 + g:t * 32 + g + 1],
                          ALU.mult, ALU.subtract)
                for g in range(0, 32, 4):
                    for gg in range(4):
                        S_.tr(pT[:, gg * 128:(gg + 1) * 128],
                              waf[:, (g + gg) * 128:(g + gg + 1) * 128], ident_b[:])
                    S_.vcopy(wtr[:, :], pT[:, :])
                    for gg in range(4):
                        S_.dma(wdq_l.ap()[(g + gg) * 128:(g + gg + 1) * 128,
                                          t * 128:(t + 1) * 128],
                               wtr[:, gg * 128:(gg + 1) * 128])

        S_.barrier()
        coll("AllGather", GROUPS4, qn_l, qn_g)
        coll("AllGather", GROUPS4, qp_l, qp_g)
        coll("AllGather", GROUPS4, ckv_l, ckv_g)
        coll("AllGather", GROUPS4, kpe_l, kpe_g)
        coll("AllGather", GROUPS8, wdq_l, wdq_g)

        if _PHASE_LIMIT <= 1:
            S_.replay(nc, serialize=_SERIALIZE)
            return nc
        qn_gf = _flat(qn_g)
        qp_gf = _flat(qp_g)
        y_gf = _flat(y_g)
        qr_gf = _flat(qr_g)
        kr_gf = _flat(kr_g)
        v_gf = _flat(v_g)
        dtg_gf = _flat(dtg_g)
        y1_gf = _flat(y1_g)

        with contextlib.ExitStack() as st:
            sb = lambda n, s, d: st.enter_context(nc.sbuf_tensor(n, s, d))
            ps = lambda n, s, d: st.enter_context(nc.psum_tensor(n, s, d))
            ckv_s = sb("b_ckv_s", [128, 4 * S], BF16)
            kpe_s = sb("b_kpe_s", [64, S], BF16)
            wkvk_s = sb("b_wkvk_s", [128, 4 * 1024], BF16)
            wkvv_s = sb("b_wkvv_s", [128, 4 * 1024], BF16)
            qn_s = sb("b_qn_s", [128, 2 * S], BF16)
            qp_s = sb("b_qp_s", [64, 2 * S], BF16)
            kh_s = sb("b_kh_s", [128, S], BF16)
            vtm_s = sb("b_vtm_s", [128, S], BF16)
            Ssb = sb("b_Ssb", [128, 2 * S], F32)
            Pb = sb("b_Pb", [128, 2 * S], BF16)
            PT_s = sb("b_PT_s", [128, 2 * S], BF16)
            mparts = sb("b_mparts", [128, 18], F32)
            jnk128 = sb("b_jnk128", [128, 128], F32)
            rsparts = sb("b_rsparts", [128, 16], F32)
            m_col = sb("b_m_col", [128, 2], F32)
            negm = sb("b_negm", [128, 2], F32)
            rs_col = sb("b_rs_col", [128, 2], F32)
            rinv = sb("b_rinv", [128, 2], F32)
            ysb = sb("b_ysb", [128, 2 * 128], BF16)
            ytb = sb("b_ytb", [128, 2 * 128], BF16)
            maskneg_b = sb("b_maskneg_b", [128, 128], BF16)
            pS = ps("b_pS", [128, 2048], F32)
            pV = ps("b_pV", [128, 128], F32)
            pTt = ps("b_pTt", [128, 512], BF16)
            pAV = ps("b_pAV", [128, 2 * 128], F32)
            pYt = ps("b_pYt", [128, 2 * 128], BF16)

            for r in range(4):
                for t in range(4):
                    S_.dma(ckv_s[:, t * S + r * 1024:t * S + (r + 1) * 1024],
                           ckv_g.ap()[r * 512 + t * 128:r * 512 + (t + 1) * 128, :])
                S_.dma(kpe_s[:, r * 1024:(r + 1) * 1024],
                       kpe_g.ap()[r * 64:(r + 1) * 64, :])
            for t in range(4):
                S_.dma(wkvk_s[:, t * 1024:(t + 1) * 1024], wkvkT[t * 128:(t + 1) * 128, :], cast=True)
                S_.dma(wkvv_s[:, t * 1024:(t + 1) * 1024], wkvvT[t * 128:(t + 1) * 128, :], cast=True)
            S_.dma(maskneg_b[:], maskneg_in, cast=True)

            y_l3 = y_l.ap().rearrange("(r b) t -> r b t", b=4)

            for hl in range(8):
                hb = (hl % 2) * S
                for r in range(4):
                    idma(qn_s[:, hb + r * 1024:hb + (r + 1) * 1024], qn_gf,
                         xB_qn[:, hl * 4 + r:hl * 4 + r + 1])
                    idma(qp_s[:, hb + r * 1024:hb + (r + 1) * 1024], qp_gf,
                         xB_qp[:, hl * 4 + r:hl * 4 + r + 1])
                for tc in range(8):
                    ps0 = (tc % 4) * 512
                    for kc in range(4):
                        S_.mm(pS[:, ps0:ps0 + 512],
                              wkvk_s[:, kc * 1024 + hl * 128:kc * 1024 + (hl + 1) * 128],
                              ckv_s[:, kc * S + tc * 512:kc * S + (tc + 1) * 512],
                              start=(kc == 0), stop=(kc == 3))
                    S_.vcopy(kh_s[:, tc * 512:(tc + 1) * 512], pS[:, ps0:ps0 + 512])
                for tt in range(32):
                    for kc in range(4):
                        S_.mm(pV[:, :], ckv_s[:, kc * S + tt * 128:kc * S + (tt + 1) * 128],
                              wkvv_s[:, kc * 1024 + hl * 128:kc * 1024 + (hl + 1) * 128],
                              start=(kc == 0), stop=(kc == 3))
                    S_.vcopy(vtm_s[:, tt * 128:(tt + 1) * 128], pV[:, :])
                for qt in range(32):
                    nk = qt + 1
                    kw = nk * 128
                    nkc = (kw + 511) // 512
                    pa = qt % 2
                    sb0 = pa * S
                    ng2 = (kw + 2047) // 2048
                    for ci in range(nkc):
                        w = min(512, kw - ci * 512)
                        ps0 = (ci % 4) * 512
                        last = (ci == nkc - 1)
                        S_.mm(pS[0:128, ps0:ps0 + w], qn_s[:, hb + qt * 128:hb + (qt + 1) * 128],
                              kh_s[:, ci * 512:ci * 512 + w], start=True, stop=False)
                        S_.mm(pS[0:128, ps0:ps0 + w], qp_s[0:64, hb + qt * 128:hb + (qt + 1) * 128],
                              kpe_s[0:64, ci * 512:ci * 512 + w], start=False, stop=not last)
                        if last:
                            S_.mm(pS[0:128, ps0 + w - 128:ps0 + w], ident_b[:], maskneg_b[:],
                                  start=False, stop=True)
                        if ci % 4 == 3 or last:
                            g2 = ci // 4
                            w4 = (ci % 4) * 512 + w
                            S_.ts(Ssb[:, sb0 + g2 * 2048:sb0 + g2 * 2048 + w4],
                                  pS[0:128, 0:w4], 1.0, None, ALU.mult, ALU.max,
                                  accum_out=mparts[:, pa * 9 + g2:pa * 9 + g2 + 1])
                    S_.rmax(m_col[:, pa:pa + 1], mparts[:, pa * 9:pa * 9 + ng2])
                    S_.ts(negm[:, pa:pa + 1], m_col[:, pa:pa + 1], -L0_SCALE, None, ALU.mult)
                    S_.act(Pb[:, sb0:sb0 + kw], Ssb[:, sb0:sb0 + kw],
                           AF.Exp, bias=negm[:, pa:pa + 1], scale=L0_SCALE,
                           accum_out=rs_col[:, pa:pa + 1])
                    S_.recip(rinv[:, pa:pa + 1], rs_col[:, pa:pa + 1])
                    for g in range(0, nk, 4):
                        ng = min(4, nk - g)
                        for kk in range(ng):
                            S_.tr(pTt[:, kk * 128:(kk + 1) * 128],
                                  Pb[:, sb0 + (g + kk) * 128:sb0 + (g + kk + 1) * 128], ident_b[:])
                        S_.vcopy(PT_s[:, sb0 + g * 128:sb0 + (g + ng) * 128],
                                 pTt[:, 0:ng * 128])
                    for kt in range(nk):
                        S_.mm(pAV[:, pa * 128:(pa + 1) * 128],
                              PT_s[:, sb0 + kt * 128:sb0 + (kt + 1) * 128],
                              vtm_s[:, kt * 128:(kt + 1) * 128],
                              start=(kt == 0), stop=(kt == nk - 1))
                    S_.act(ysb[:, pa * 128:(pa + 1) * 128], pAV[:, pa * 128:(pa + 1) * 128],
                           AF.Copy, scale=rinv[:, pa:pa + 1])
                    S_.tr(pYt[:, pa * 128:(pa + 1) * 128], ysb[:, pa * 128:(pa + 1) * 128],
                          ident_b[:])
                    S_.vcopy(ytb[:, pa * 128:(pa + 1) * 128], pYt[:, pa * 128:(pa + 1) * 128])
                    S_.dma(y_l3[hl * 128:(hl + 1) * 128, qt // 8, (qt % 8) * 128:(qt % 8 + 1) * 128],
                           ytb[:, pa * 128:(pa + 1) * 128])

        S_.barrier()
        coll("AllGather", GROUPS4, y_l, y_g)
        if _PHASE_LIMIT <= 2:
            S_.replay(nc, serialize=_SERIALIZE)
            return nc

        with contextlib.ExitStack() as st:
            sb = lambda n, s, d: st.enter_context(nc.sbuf_tensor(n, s, d))
            ps = lambda n, s, d: st.enter_context(nc.psum_tensor(n, s, d))
            TC = 512
            ybf = sb("c_ybf", [128, 32 * TC], BF16)
            wcol = sb("c_wcol", [128, 2 * 32 * 128], BF16)
            wsel = [0]
            xf32 = sb("c_xf32", [128, 2 * TC], F32)
            sqb = [0]
            r16 = sb("c_r16", [128, 32 * TC], F16)
            x1b = sb("c_x1b", [128, 32 * TC], BF16)
            x1h = sb("c_x1h", [128, 32 * TC], F16)
            sq = sb("c_sq", [128, 2 * TC], F16)
            onecol = sb("c_onecol", [128, 1], F16)
            onerow = sb("c_onerow", [1, 128], F32)
            bias_s = sb("c_bias_s", [128, 32], F32)
            tmp32 = sb("c_tmp32", [128, TC], F32)
            rowa = sb("c_rowa", [1, TC], F32)
            rowb = sb("c_rowb", [1, TC], F32)
            rowc = sb("c_rowc", [1, TC], F32)
            m_bc = sb("c_m_bc", [128, TC], F32)
            rstd_bc = sb("c_rstd_bc", [128, TC], F32)
            c_eps5 = sb("c_c_eps5", [1, 1], F32)
            pA = ps("c_pA", [128, 2 * TC], F32)
            pR = ps("c_pR", [1, 2 * TC], F32)
            pB = ps("c_pB", [128, TC], F32)

            S_.memset(onecol[:], 1.0)
            S_.memset(onerow[:], 1.0)
            S_.memset(c_eps5[:], 1e-5)
            S_.dma(bias_s[:], bias_t)

            def wload_c(dst, wT_dram, ocol0, owidth, nk, cast):
                b0 = wsel[0] * 4096
                wsel[0] ^= 1
                S_.dma(dst[:, b0:b0 + nk * owidth].rearrange("p (k o) -> p k o", k=nk),
                       wT_dram[:, ocol0:ocol0 + owidth].rearrange("(k p) o -> p k o", p=128),
                       cast=cast)
                return b0

            def wload_wdq(dst, ot):
                b0 = wsel[0] * 4096
                wsel[0] ^= 1
                r, c0 = ot // 4, (ot % 4) * 128
                S_.dma(dst[:, b0:b0 + 32 * 128].rearrange("p (k o) -> p k o", k=32),
                       wdq_g.ap()[r * 4096:(r + 1) * 4096, c0:c0 + 128]
                       .rearrange("(k p) o -> p k o", p=128))
                return b0

            def layer_norm(src16, dst_bf, dst_f16, out_dram=None, tc0=0):
                S_.act(rowa[:], pR[0:1, 0:TC], AF.Copy, scale=1.0 / D)
                S_.act(rowb[:], pR[0:1, TC:2 * TC], AF.Copy, scale=1.0 / D)
                S_.tt(rowc[:], rowa[:], rowa[:], ALU.mult)
                S_.tt(rowb[:], rowb[:], rowc[:], ALU.subtract)
                S_.act(rowb[:], rowb[:], AF.Ln, bias=c_eps5[0:1, 0:1])
                S_.act(rowb[:], rowb[:], AF.Exp, scale=-0.5)
                S_.mm(pB[:, :], onerow[:], rowa[0:1, :])
                S_.vcopy(m_bc[:], pB[:, :])
                S_.mm(pB[:, :], onerow[:], rowb[0:1, :])
                S_.vcopy(rstd_bc[:], pB[:, :])
                for ot in range(32):
                    sl = src16[:, ot * TC:(ot + 1) * TC]
                    S_.tt(tmp32[:], sl, m_bc[:], ALU.subtract)
                    if dst_bf is not None:
                        S_.tt(dst_bf[:, ot * TC:(ot + 1) * TC], tmp32[:], rstd_bc[:], ALU.mult)
                    if dst_f16 is not None:
                        S_.tt(dst_f16[:, ot * TC:(ot + 1) * TC], tmp32[:], rstd_bc[:], ALU.mult)
                    if out_dram is not None:
                        S_.tt(tmp32[:], tmp32[:], rstd_bc[:], ALU.mult)
                        S_.dma(out_dram[ot * 128:(ot + 1) * 128, tc0:tc0 + TC], tmp32[:])

            for tci, tc in enumerate(range(0, T, TC)):
                for t in range(32):
                    idma(ybf[:, t * TC:(t + 1) * TC], y_gf, xY[:, t * 4 + 2 * tci:t * 4 + 2 * tci + 1])
                for ot in range(32):
                    b0 = wload_c(wcol, woT, ot * 128, 128, 32, True)
                    pa0 = (ot % 2) * TC
                    for k in range(32):
                        S_.mm(pA[:, pa0:pa0 + TC], wcol[:, b0 + k * 128:b0 + (k + 1) * 128],
                              ybf[:, k * TC:(k + 1) * TC],
                              start=(k == 0), stop=(k == 31))
                    xb0 = (ot % 2) * TC
                    S_.dma(xf32[:, xb0:xb0 + TC], xT[ot * 128:(ot + 1) * 128, tc:tc + TC])
                    S_.tt(r16[:, ot * TC:(ot + 1) * TC], pA[:, pa0:pa0 + TC],
                          xf32[:, xb0:xb0 + TC], ALU.add)
                    sb0 = (ot % 2) * TC
                    S_.act(sq[:, sb0:sb0 + TC], r16[:, ot * TC:(ot + 1) * TC], AF.Square)
                    S_.mm(pR[0:1, 0:TC], onecol[:], r16[:, ot * TC:(ot + 1) * TC],
                          start=(ot == 0), stop=(ot == 31))
                    S_.mm(pR[0:1, TC:2 * TC], onecol[:], sq[:, sb0:sb0 + TC],
                          start=(ot == 0), stop=(ot == 31))
                layer_norm(r16, x1b, x1h)
                for ot in range(32):
                    b0 = wload_wdq(wcol, ot)
                    pa0 = (ot % 2) * TC
                    for k in range(32):
                        S_.mm(pA[:, pa0:pa0 + TC], wcol[:, b0 + k * 128:b0 + (k + 1) * 128],
                              x1b[:, k * TC:(k + 1) * TC],
                              start=(k == 0), stop=(k == 31))
                    S_.stt(r16[:, ot * TC:(ot + 1) * TC], pA[:, pa0:pa0 + TC],
                           bias_s[:, ot:ot + 1],
                           x1h[:, ot * TC:(ot + 1) * TC], ALU.add, ALU.add)
                    sb0 = (ot % 2) * TC
                    S_.act(sq[:, sb0:sb0 + TC], r16[:, ot * TC:(ot + 1) * TC], AF.Square)
                    S_.mm(pR[0:1, 0:TC], onecol[:], r16[:, ot * TC:(ot + 1) * TC],
                          start=(ot == 0), stop=(ot == 31))
                    S_.mm(pR[0:1, TC:2 * TC], onecol[:], sq[:, sb0:sb0 + TC],
                          start=(ot == 0), stop=(ot == 31))
                layer_norm(r16, None, None, out_dram=x2_l.ap(), tc0=tc)

        if _PHASE_LIMIT <= 3:
            S_.replay(nc, serialize=_SERIALIZE)
            return nc
        S_.barrier()
        with contextlib.ExitStack() as st:
            sb = lambda n, s, d: st.enter_context(nc.sbuf_tensor(n, s, d))
            ps = lambda n, s, d: st.enter_context(nc.psum_tensor(n, s, d))
            xb = sb("d_xb", [128, 32 * T], BF16)
            wcol = sb("d_wcol", [128, 2 * 32 * 128], BF16)
            wsel = [0]
            qf = sb("d_qf", [128, T], F32)
            xh0 = sb("d_xh0", [64, T], F32)
            pr1 = sb("d_pr1", [64, T], F32)
            pr2 = sb("d_pr2", [64, T], F32)
            ro1 = sb("d_ro1", [64, T], BF16)
            ro2 = sb("d_ro2", [64, T], BF16)
            trig_c = sb("d_trig_c", [64, T], F32)
            trig_s = sb("d_trig_s", [64, T], F32)
            otile = sb("d_otile", [128, T], BF16)
            dtb_s = sb("d_dtb_s", [32, 1], F32)
            alog_s = sb("d_alog_s", [32, 1], F32)
            a1 = sb("d_a1", [32, T], F32)
            l1 = sb("d_l1", [32, T], F32)
            r1v = sb("d_r1v", [32, T], F32)
            dt_s = sb("d_dt_s", [32, T], F32)
            g_s = sb("d_g_s", [32, T], F32)
            nega = sb("d_nega", [32, 1], F32)
            c_one = sb("d_c_one", [32, 1], F32)
            dgt = sb("d_dgt", [128, 64], F32)
            pA = ps("d_pA", [128, 2 * T], F32)
            pacnt = [0]
            pD = ps("d_pD", [32, T], F32)
            pTr = ps("d_pTr", [128, 64], F32)

            for t in range(32):
                S_.dma(xb[:, t * T:(t + 1) * T], x2_l.ap()[t * 128:(t + 1) * 128, :], cast=True)
            S_.dma(trig_c[:], cos1)
            S_.dma(trig_s[:], sin1)
            S_.dma(dtb_s[:], dtb)
            S_.dma(alog_s[:], alog)
            S_.memset(c_one[:], 1.0)

            def proj_tile(psum, wT_dram, ocol0, owidth):
                b0 = wsel[0] * 4096
                wsel[0] ^= 1
                pa0 = (pacnt[0] % 2) * T if psum is pA else 0
                pacnt[0] += 1
                S_.dma(wcol[:, b0:b0 + 32 * owidth].rearrange("p (k o) -> p k o", k=32),
                       wT_dram[:, ocol0:ocol0 + owidth].rearrange("(k p) o -> p k o", p=128),
                       cast=True)
                for k in range(32):
                    l = wcol[0:128, b0 + k * owidth:b0 + (k + 1) * owidth]
                    for h in range(0, T, 512):
                        S_.mm(psum[0:owidth, pa0 + h:pa0 + h + 512],
                              l, xb[:, k * T + h:k * T + h + 512],
                              start=(k == 0), stop=(k == 31))
                return pa0

            def rope1(dst_dram, row0):
                S_.vcopy(xh0[:], qf[64:128, :])
                S_.tt(pr1[:], qf[0:64, :], trig_c[:], ALU.mult)
                S_.tt(pr2[:], xh0[:], trig_s[:], ALU.mult)
                S_.tt(pr1[:], pr1[:], pr2[:], ALU.subtract)
                S_.vcopy(ro1[:], pr1[:])
                S_.tt(pr1[:], xh0[:], trig_c[:], ALU.mult)
                S_.tt(pr2[:], qf[0:64, :], trig_s[:], ALU.mult)
                S_.tt(pr1[:], pr1[:], pr2[:], ALU.add)
                S_.vcopy(ro2[:], pr1[:])
                S_.dma(dst_dram[row0:row0 + 64, :], ro1[:])
                S_.dma(dst_dram[row0 + 64:row0 + 128, :], ro2[:])

            for hh in range(32):
                pa0 = proj_tile(pA, WqT, hh * 128, 128)
                S_.vcopy(qf[:], pA[:, pa0:pa0 + T])
                rope1(qr_l.ap(), hh * 128)
            for hh in range(8):
                pa0 = proj_tile(pA, WkT, hh * 128, 128)
                S_.vcopy(qf[:], pA[:, pa0:pa0 + T])
                rope1(kr_l.ap(), hh * 128)
            for hh in range(8):
                pa0 = proj_tile(pA, WvT, hh * 128, 128)
                S_.vcopy(otile[:], pA[:, pa0:pa0 + T])
                S_.dma(v_l.ap()[hh * 128:(hh + 1) * 128, :], otile[:])
            proj_tile(pD, WinT, 0, 32)
            S_.act(a1[:], pD[0:32, :], AF.Abs, bias=dtb_s[:, 0:1])
            S_.act(a1[:], a1[:], AF.Exp, scale=-1.0)
            S_.act(l1[:], a1[:], AF.Ln, bias=c_one[:, 0:1])
            S_.act(r1v[:], pD[0:32, :], AF.Relu, bias=dtb_s[:, 0:1])
            S_.tt(dt_s[:], l1[:], r1v[:], ALU.add)
            S_.act(nega[:], alog_s[:], AF.Exp)
            S_.ts(nega[:], nega[:], -1.0, None, ALU.mult)
            S_.ts(g_s[:], dt_s[:], nega[:, 0:1], None, ALU.mult)
            dtg_l3 = dtg_l.ap().rearrange("(t b) c -> t b c", b=4)
            for tt in range(8):
                S_.tr(pTr[:, 0:32], dt_s[0:32, tt * 128:(tt + 1) * 128], identf[0:32, 0:32])
                S_.tr(pTr[:, 32:64], g_s[0:32, tt * 128:(tt + 1) * 128], identf[0:32, 0:32])
                S_.vcopy(dgt[:], pTr[:, :])
                for hg in range(4):
                    S_.dma(dtg_l3[tt * 128:(tt + 1) * 128, hg, 0:8],
                           dgt[:, hg * 8:(hg + 1) * 8])
                    S_.dma(dtg_l3[tt * 128:(tt + 1) * 128, hg, 8:16],
                           dgt[:, 32 + hg * 8:32 + (hg + 1) * 8])

        S_.barrier()
        coll("AllGather", GROUPS4, qr_l, qr_g)
        coll("AllGather", GROUPS4, kr_l, kr_g)
        coll("AllGather", GROUPS4, v_l, v_g)
        coll("AllGather", GROUPS4, dtg_l, dtg_g)

        if _PHASE_LIMIT <= 4:
            S_.replay(nc, serialize=_SERIALIZE)
            return nc
        with contextlib.ExitStack() as st:
            sb = lambda n, s, d: st.enter_context(nc.sbuf_tensor(n, s, d))
            ps = lambda n, s, d: st.enter_context(nc.psum_tensor(n, s, d))
            qh_all = sb("e_qh_all", [128, 8 * S], BF16)
            k_s = sb("e_k_s", [128, 2 * S], BF16)
            v_s = sb("e_v_s", [128, 2 * S], BF16)
            dtg_s = sb("e_dtg_s", [128, 32 * 16], F32)
            g_s = sb("e_g_s", [128, 256], F32)
            G_all = sb("e_G_all", [128, 256], F32)
            negG = sb("e_negG", [128, 256], F32)
            bGl = sb("e_bGl", [128, 256], F32)
            GT_s = sb("e_GT_s", [128, 256], F32)
            Gl_row = sb("e_Gl_row", [1, 256], F32)
            grow_c = sb("e_grow_c", [1, 1024], F32)
            onecol = sb("e_onecol", [128, 1], F32)
            onerow = sb("e_onerow", [1, 128], F32)
            S_a = sb("e_S_a", [128, 8 * 128], F32)
            S_b = sb("e_S_b", [128, 8 * 128], F32)
            St_bf = sb("e_St_bf", [128, 8 * 128], BF16)
            kt_tm = sb("e_kt_tm", [128, 2 * 128], BF16)
            vt_tm = sb("e_vt_tm", [128, 2 * 128], BF16)
            xc_tm = sb("e_xc_tm", [128, 128], BF16)
            kdec = sb("e_kdec", [128, 128], BF16)
            A_T = sb("e_A_T", [128, 128], BF16)
            Dm = sb("e_Dm", [128, 128], F32)
            t1 = sb("e_t1", [128, 128], F32)
            t2 = sb("e_t2", [128, 128], F32)
            y_s = sb("e_y_s", [128, 128], F32)
            y_n = sb("e_y_n", [128, 128], BF16)
            y1tb = sb("e_y1tb", [128, 128], BF16)
            jnk = sb("e_jnk", [128, 128], F32)
            eG = sb("e_eG", [128, 1], F32)
            edec = sb("e_edec", [128, 1], F32)
            cd = sb("e_cd", [128, 1], F32)
            ss_col = sb("e_ss_col", [128, 1], F32)
            rstd_c = sb("e_rstd_c", [128, 1], F32)
            triuf = sb("e_triuf", [128, 128], F32)
            masknegD = sb("e_masknegD", [128, 128], F32)
            gnorm_t = sb("e_gnorm_t", [128, 128], F32)
            c_eps5 = sb("e_c_eps5", [128, 1], F32)
            pS = ps("e_pS", [128, 128], F32)
            pK1 = ps("e_pK1", [128, 256], F32)
            pC = ps("e_pC", [128, 128], F32)
            pD_ = ps("e_pD", [128, 128], F32)
            pE = ps("e_pE", [128, 128], F32)
            pT = ps("e_pT", [128, 256], BF16)
            pRow = ps("e_pRow", [1, 256], F32)
            pG = ps("e_pG", [128, 256], F32)

            for hl in range(8):
                for r in range(4):
                    idma(qh_all[:, hl * S + r * 1024:hl * S + (r + 1) * 1024],
                         qr_gf, xB_qn[:, hl * 4 + r:hl * 4 + r + 1])
            for ti in range(2):
                for r in range(4):
                    idma(k_s[:, ti * S + r * 1024:ti * S + (r + 1) * 1024],
                         kr_gf, xE_kv[:, ti * 4 + r:ti * 4 + r + 1])
                    idma(v_s[:, ti * S + r * 1024:ti * S + (r + 1) * 1024],
                         v_gf, xE_kv[:, ti * 4 + r:ti * 4 + r + 1])
            for c in range(32):
                idma(dtg_s[:, c * 16:(c + 1) * 16], dtg_gf, xE_dtg[:, c:c + 1])
            for c in range(32):
                S_.vcopy(g_s[:, c * 8:(c + 1) * 8], dtg_s[:, c * 16 + 8:(c + 1) * 16])
            S_.dma(triuf[:], triuf_in)
            S_.dma(masknegD[:], masknegD_in)
            S_.dma(gnorm_t[:], gnt)
            S_.memset(onecol[:], 1.0)
            S_.memset(onerow[:], 1.0)
            S_.memset(c_eps5[:], 1e-5)

            S_.mm(pG[:, :], triuf[:], g_s[:])
            S_.vcopy(G_all[:], pG[:, :])
            S_.ts(negG[:], G_all[:], -1.0, None, ALU.mult)
            S_.mm(pRow[0:1, :], onecol[:], g_s[:])
            S_.vcopy(Gl_row[:], pRow[0:1, :])
            S_.mm(pK1[:, :], onerow[:], Gl_row[0:1, :])
            S_.vcopy(bGl[:], pK1[:, :])
            S_.tr(pG[:, 0:128], G_all[:, 0:128], identf[:])
            S_.tr(pG[:, 128:256], G_all[:, 128:256], identf[:])
            S_.vcopy(GT_s[:], pG[:, :])
            S_.dma(grows.ap()[0:128, :], GT_s[:, 0:128])
            S_.dma(grows.ap()[128:256, :], GT_s[:, 128:256])
            grows_flat = grows.ap().rearrange("u w -> (u w)")
            y1_l3 = y1_l.ap().rearrange("(r b) t -> r b t", b=4)

            S_.memset(S_a[:], 0.0)
            S_.memset(St_bf[:], 0.0)
            for c in range(32):
                S_.dma(grow_c[0:1, :], grows_flat[None, c * 1024:(c + 1) * 1024])
                for lkv in range(2):
                    S_.tr(pT[:, 0:128], k_s[:, lkv * S + c * 128:lkv * S + (c + 1) * 128], ident_b[:])
                    S_.tr(pT[:, 128:256], v_s[:, lkv * S + c * 128:lkv * S + (c + 1) * 128], ident_b[:])
                    S_.vcopy(kt_tm[:, lkv * 128:(lkv + 1) * 128], pT[:, 0:128])
                    S_.vcopy(vt_tm[:, lkv * 128:(lkv + 1) * 128], pT[:, 128:256])
                for hl in range(8):
                    u = c * 8 + hl
                    lkv = hl // 4
                    G_col = G_all[:, u:u + 1]
                    negG_col = negG[:, u:u + 1]
                    bGl_col = bGl[:, u:u + 1]
                    Sc = S_a[:, hl * 128:(hl + 1) * 128] if c % 2 == 0 else S_b[:, hl * 128:(hl + 1) * 128]
                    Sn = S_b[:, hl * 128:(hl + 1) * 128] if c % 2 == 0 else S_a[:, hl * 128:(hl + 1) * 128]
                    Stb = St_bf[:, hl * 128:(hl + 1) * 128]
                    S_.act(eG[:], G_col, AF.Exp)
                    S_.act(edec[:], G_col, AF.Exp, scale=-1.0, bias=bGl_col)
                    S_.act(cd[:], bGl_col, AF.Exp)
                    S_.mm(pK1[:, 0:128], onerow[:], grow_c[0:1, hl * 128:(hl + 1) * 128])
                    S_.tt(t1[:], pK1[:, 0:128], masknegD[:], ALU.add)
                    S_.act(Dm[:], t1[:], AF.Exp, bias=negG_col)
                    S_.mm(pS[:, :], k_s[:, lkv * S + c * 128:lkv * S + (c + 1) * 128],
                          qh_all[:, hl * S + c * 128:hl * S + (c + 1) * 128])
                    S_.tt(A_T[:], pS[:, :], Dm[:], ALU.mult)
                    S_.ts(xc_tm[:], vt_tm[:, lkv * 128:(lkv + 1) * 128],
                          dtg_s[:, c * 16 + hl:c * 16 + hl + 1], None, ALU.mult)
                    S_.ts(kdec[:], kt_tm[:, lkv * 128:(lkv + 1) * 128], edec[:, 0:1], None, ALU.mult)
                    S_.mm(pC[:, :], kdec[:], xc_tm[:])
                    S_.mm(pD_[:, :], A_T[:], xc_tm[:])
                    S_.mm(pE[:, :], qh_all[:, hl * S + c * 128:hl * S + (c + 1) * 128], Stb)
                    S_.ts(t2[:], pE[:, :], eG[:, 0:1], None, ALU.mult)
                    S_.tt(y_s[:], pD_[:, :], t2[:], ALU.add)
                    S_.act(jnk[:], y_s[:], AF.Square, accum_out=ss_col[:, 0:1])
                    S_.act(rstd_c[:], ss_col[:], AF.Ln, scale=1.0 / 128, bias=c_eps5[:, 0:1])
                    S_.act(rstd_c[:], rstd_c[:], AF.Exp, scale=-0.5)
                    S_.stt(y_n[:], y_s[:], rstd_c[:, 0:1], gnorm_t[:], ALU.mult, ALU.mult)
                    S_.tr(pT[:, 0:128], y_n[:], ident_b[:])
                    S_.vcopy(y1tb[:], pT[:, 0:128])
                    S_.dma(y1_l3[hl * 128:(hl + 1) * 128, c // 8, (c % 8) * 128:(c % 8 + 1) * 128],
                           y1tb[:])
                    S_.ts(Sn, Sc, cd[:, 0:1], None, ALU.mult)
                    S_.tt(Sn, Sn, pC[:, :], ALU.add)
                    S_.vcopy(Stb, Sn)

        S_.barrier()
        coll("AllGather", GROUPS4, y1_l, y1_g)
        if _PHASE_LIMIT <= 5:
            S_.replay(nc, serialize=_SERIALIZE)
            return nc

        S_.serialize_recording = True
        with contextlib.ExitStack() as st:
            sb = lambda n, s, d: st.enter_context(nc.sbuf_tensor(n, s, d))
            ps = lambda n, s, d: st.enter_context(nc.psum_tensor(n, s, d))
            TC = 512
            y1b = sb("f_y1b", [128, 32 * TC], BF16)
            x2b = sb("f_x2b", [128, 32 * TC], BF16)
            ygb = y1b
            wcol = sb("f_wcol", [128, 2 * 32 * 128], BF16)
            wsel = [0]
            xf32 = sb("f_xf32", [128, TC], F32)
            r16 = sb("f_r16", [128, 32 * TC], F16)
            x3b = sb("f_x3b", [128, 32 * TC], BF16)
            x3h = sb("f_x3h", [128, 32 * TC], F16)
            sq = sb("f_sq", [128, TC], F16)
            sg = sb("f_sg", [128, TC], BF16)
            onecol = sb("f_onecol", [128, 1], F16)
            onerow = sb("f_onerow", [1, 128], F32)
            tmp32 = sb("f_tmp32", [128, TC], F32)
            rowa = sb("f_rowa", [1, TC], F32)
            rowb = sb("f_rowb", [1, TC], F32)
            rowc = sb("f_rowc", [1, TC], F32)
            m_bc = sb("f_m_bc", [128, TC], F32)
            rstd_bc = sb("f_rstd_bc", [128, TC], F32)
            c_eps5 = sb("f_c_eps5", [1, 1], F32)
            lg_f = sb("f_lg_f", [28, TC], F32)
            lt = sb("f_lt", [128, 28], F32)
            msk1 = sb("f_msk1", [128, 28], F32)
            msk2 = sb("f_msk2", [128, 28], F32)
            l2t = sb("f_l2t", [128, 28], F32)
            tb = sb("f_tb", [128, 28], F32)
            ga_t = sb("f_ga_t", [128, 28], F32)
            gb_t = sb("f_gb_t", [128, 28], F32)
            gtm = sb("f_gtm", [128, 28], F32)
            m1c = sb("f_m1c", [128, 1], F32)
            m2c = sb("f_m2c", [128, 1], F32)
            dgc = sb("f_dgc", [128, 1], F32)
            p1c = sb("f_p1c", [128, 1], F32)
            p2c = sb("f_p2c", [128, 1], F32)
            gate_fm = sb("f_gate_fm", [28, TC], F32)
            gsc1 = sb("f_gsc1", [128, TC], F32)
            gsc2 = sb("f_gsc2", [96, TC], F32)
            hg1 = sb("f_hg1", [128, TC], BF16)
            hg2 = sb("f_hg2", [96, TC], BF16)
            lb1 = sb("f_lb1", [128, 2 * 128], BF16)
            lb2 = sb("f_lb2", [96, 2 * 128], BF16)
            R1s = sb("f_R1s", [28, 128], F32)
            R2s = sb("f_R2s", [28, 96], F32)
            pA = ps("f_pA", [128, TC], F32)
            pR = ps("f_pR", [1, 2 * TC], F32)
            pB = ps("f_pB", [128, TC], F32)
            pL = ps("f_pL", [28, TC], F32)
            pTr = ps("f_pTr", [128, 128], F32)
            pH = ps("f_pH", [128, TC], F32)

            S_.memset(onecol[:], 1.0)
            S_.memset(onerow[:], 1.0)
            S_.memset(c_eps5[:], 1e-5)
            S_.dma(R1s[:], R1_in)
            S_.dma(R2s[:], R2_in)

            def wload_f(dst, wT_dram, ocol0, owidth, nk, cast=True):
                b0 = wsel[0] * 4096
                wsel[0] ^= 1
                S_.dma(dst[:, b0:b0 + nk * owidth].rearrange("p (k o) -> p k o", k=nk),
                       wT_dram[:, ocol0:ocol0 + owidth].rearrange("(k p) o -> p k o", p=128),
                       cast=cast)
                return b0

            def layer_norm(src16, dst_bf, dst_f16, out_dram=None, tc0=0):
                S_.act(rowa[:], pR[0:1, 0:TC], AF.Copy, scale=1.0 / D)
                S_.act(rowb[:], pR[0:1, TC:2 * TC], AF.Copy, scale=1.0 / D)
                S_.tt(rowc[:], rowa[:], rowa[:], ALU.mult)
                S_.tt(rowb[:], rowb[:], rowc[:], ALU.subtract)
                S_.act(rowb[:], rowb[:], AF.Ln, bias=c_eps5[0:1, 0:1])
                S_.act(rowb[:], rowb[:], AF.Exp, scale=-0.5)
                S_.mm(pB[:, :], onerow[:], rowa[0:1, :])
                S_.vcopy(m_bc[:], pB[:, :])
                S_.mm(pB[:, :], onerow[:], rowb[0:1, :])
                S_.vcopy(rstd_bc[:], pB[:, :])
                for ot in range(32):
                    sl = src16[:, ot * TC:(ot + 1) * TC]
                    S_.tt(tmp32[:], sl, m_bc[:], ALU.subtract)
                    if dst_bf is not None:
                        S_.tt(dst_bf[:, ot * TC:(ot + 1) * TC], tmp32[:], rstd_bc[:], ALU.mult)
                    if dst_f16 is not None:
                        S_.tt(dst_f16[:, ot * TC:(ot + 1) * TC], tmp32[:], rstd_bc[:], ALU.mult)
                    if out_dram is not None:
                        S_.tt(tmp32[:], tmp32[:], rstd_bc[:], ALU.mult)
                        S_.dma(out_dram[ot * 128:(ot + 1) * 128, tc0:tc0 + TC], tmp32[:])

            for tci, tc in enumerate(range(0, T, TC)):
                if tci >= _F_TC_LIMIT:
                    break
                for t in range(32):
                    idma(y1b[:, t * TC:(t + 1) * TC], y1_gf, xY[:, t * 4 + 2 * tci:t * 4 + 2 * tci + 1])
                    S_.dma(x2b[:, t * TC:(t + 1) * TC],
                           x2_l.ap()[t * 128:(t + 1) * 128, tc:tc + TC], cast=True)
                if _F_SEC < 2:
                    continue
                S_.barrier()
                for ot in range(32):
                    S_.barrier()
                    b0 = wload_f(wcol, WgT, ot * 128, 128, 32, True)
                    for k in range(32):
                        S_.mm(pA[:, :], wcol[:, b0 + k * 128:b0 + (k + 1) * 128],
                              x2b[:, k * TC:(k + 1) * TC],
                              start=(k == 0), stop=(k == 31))
                    S_.act(sg[:], pA[:, :], AF.Silu)
                    S_.tt(ygb[:, ot * TC:(ot + 1) * TC], y1b[:, ot * TC:(ot + 1) * TC],
                          sg[:], ALU.mult)
                if _F_SEC < 3:
                    continue
                S_.barrier()
                for ot in range(32):
                    S_.barrier()
                    b0 = wload_f(wcol, WoT1, ot * 128, 128, 32, True)
                    for k in range(32):
                        S_.mm(pA[:, :], wcol[:, b0 + k * 128:b0 + (k + 1) * 128],
                              ygb[:, k * TC:(k + 1) * TC],
                              start=(k == 0), stop=(k == 31))
                    S_.dma(xf32[:, 0:TC], x2_l.ap()[ot * 128:(ot + 1) * 128, tc:tc + TC])
                    S_.tt(r16[:, ot * TC:(ot + 1) * TC], pA[:, :], xf32[:, 0:TC], ALU.add)
                    S_.act(sq[:, 0:TC], r16[:, ot * TC:(ot + 1) * TC], AF.Square)
                    S_.mm(pR[0:1, 0:TC], onecol[:], r16[:, ot * TC:(ot + 1) * TC],
                          start=(ot == 0), stop=(ot == 31))
                    S_.mm(pR[0:1, TC:2 * TC], onecol[:], sq[:, 0:TC],
                          start=(ot == 0), stop=(ot == 31))
                S_.barrier()
                layer_norm(r16, x3b, x3h)
                if _F_SEC < 4:
                    continue
                S_.barrier()
                b0 = wload_f(wcol, gateT, 0, 28, 32, True)
                for k in range(32):
                    S_.mm(pL[0:28, :], wcol[:, b0 + k * 28:b0 + (k + 1) * 28],
                          x3b[:, k * TC:(k + 1) * TC],
                          start=(k == 0), stop=(k == 31))
                S_.vcopy(lg_f[:], pL[0:28, :])
                for j in range(TC // 128):
                    S_.barrier()
                    S_.tr(pTr[0:128, 0:28], lg_f[0:28, j * 128:(j + 1) * 128], identf[0:28, 0:28])
                    S_.vcopy(lt[:], pTr[0:128, 0:28])
                    S_.rmax(m1c[:], lt[:])
                    S_.ts(msk1[:], lt[:], m1c[:, 0:1], None, ALU.is_equal)
                    S_.ts(tb[:], msk1[:], 1e30, None, ALU.mult)
                    S_.tt(l2t[:], lt[:], tb[:], ALU.subtract)
                    S_.rmax(m2c[:], l2t[:])
                    S_.tt(dgc[:], m1c[:], m2c[:], ALU.subtract)
                    S_.act(p1c[:], dgc[:], AF.Sigmoid)
                    S_.ts(p2c[:], p1c[:], -1.0, 1.0, ALU.mult, ALU.add)
                    S_.ts(msk2[:], l2t[:], m2c[:, 0:1], None, ALU.is_equal)
                    S_.ts(ga_t[:], msk1[:], p1c[:, 0:1], None, ALU.mult)
                    S_.ts(gb_t[:], msk2[:], p2c[:, 0:1], None, ALU.mult)
                    S_.tt(gtm[:], ga_t[:], gb_t[:], ALU.add)
                    S_.tr(pTr[0:28, 0:128], gtm[0:128, 0:28], identf[:])
                    S_.vcopy(gate_fm[0:28, j * 128:(j + 1) * 128], pTr[0:28, 0:128])
                if _F_SEC < 5:
                    continue
                S_.barrier()
                S_.mm(pB[0:128, :], R1s[:], gate_fm[0:28, :])
                S_.vcopy(gsc1[:], pB[0:128, :])
                S_.mm(pB[0:96, :], R2s[:], gate_fm[0:28, :])
                S_.vcopy(gsc2[:], pB[0:96, :])
                b0 = wload_f(wcol, loraAT, 0, 128, 32, True)
                for k in range(32):
                    S_.mm(pH[:, :], wcol[:, b0 + k * 128:b0 + (k + 1) * 128],
                          x3b[:, k * TC:(k + 1) * TC],
                          start=(k == 0), stop=(k == 31))
                S_.stt(hg1[:], pH[:, :], 2.0, gsc1[:], ALU.mult, ALU.mult)
                b0 = wload_f(wcol, loraAT, 128, 96, 32, True)
                for k in range(32):
                    S_.mm(pH[0:96, :], wcol[:, b0 + k * 96:b0 + (k + 1) * 96],
                          x3b[:, k * TC:(k + 1) * TC],
                          start=(k == 0), stop=(k == 31))
                S_.stt(hg2[:], pH[0:96, :], 2.0, gsc2[:], ALU.mult, ALU.mult)
                if _F_SEC < 6:
                    continue
                S_.barrier()
                for ot in range(32):
                    S_.barrier()
                    b0 = wload_f(wcol, baseT, ot * 128, 128, 32, True)
                    for k in range(32):
                        S_.mm(pA[:, :], wcol[:, b0 + k * 128:b0 + (k + 1) * 128],
                              x3b[:, k * TC:(k + 1) * TC],
                              start=(k == 0), stop=False)
                    lbb = (ot % 2) * 128
                    S_.dma(lb1[:, lbb:lbb + 128], loraBf[0:128, ot * 128:(ot + 1) * 128], cast=True)
                    S_.dma(lb2[:, lbb:lbb + 128], loraBf[128:224, ot * 128:(ot + 1) * 128], cast=True)
                    S_.mm(pA[:, :], lb1[:, lbb:lbb + 128], hg1[:], start=False, stop=False)
                    S_.mm(pA[:, :], lb2[0:96, lbb:lbb + 128], hg2[0:96, :], start=False, stop=True)
                    S_.tt(r16[:, ot * TC:(ot + 1) * TC], pA[:, :], x3h[:, ot * TC:(ot + 1) * TC],
                          ALU.add)
                    S_.act(sq[:, 0:TC], r16[:, ot * TC:(ot + 1) * TC], AF.Square)
                    S_.mm(pR[0:1, 0:TC], onecol[:], r16[:, ot * TC:(ot + 1) * TC],
                          start=(ot == 0), stop=(ot == 31))
                    S_.mm(pR[0:1, TC:2 * TC], onecol[:], sq[:, 0:TC],
                          start=(ot == 0), stop=(ot == 31))
                S_.barrier()
                layer_norm(r16, None, None, out_dram=o_x4, tc0=tc)

        S_.replay(nc, serialize=_SERIALIZE)
    return nc


def host_prep_fused(inputs):
    x = np.asarray(inputs["L0_attn_x"], np.float32)
    wq_a = np.asarray(inputs["L0_attn_wq_a"], np.float32)
    q_norm_w = np.asarray(inputs["L0_attn_q_norm_w"], np.float32)
    wq_b = np.asarray(inputs["L0_attn_wq_b"], np.float32)
    wkv_a = np.asarray(inputs["L0_attn_wkv_a"], np.float32)
    kv_norm_w = np.asarray(inputs["L0_attn_kv_norm_w"], np.float32)
    wkv_b = np.asarray(inputs["L0_attn_wkv_b"], np.float32)
    wo = np.asarray(inputs["L0_attn_wo"], np.float32)
    qw = np.asarray(inputs["L0_ffn_q_weights"], np.int32)
    sc = np.asarray(inputs["L0_ffn_scales"], np.float32)
    zr = np.asarray(inputs["L0_ffn_zeros"], np.float32)
    bias = np.asarray(inputs["L0_ffn_bias"], np.float32)
    Wq = np.asarray(inputs["L1_attn_Wq"], np.float32)
    Wk = np.asarray(inputs["L1_attn_Wk"], np.float32)
    Wv = np.asarray(inputs["L1_attn_Wv"], np.float32)
    Wo1 = np.asarray(inputs["L1_attn_Wo"], np.float32)
    Wg = np.asarray(inputs["L1_attn_Wg"], np.float32)
    Win = np.asarray(inputs["L1_attn_Win"], np.float32)
    dtb = np.asarray(inputs["L1_attn_dt_bias"], np.float32).reshape(32, 1)
    alog = np.asarray(inputs["L1_attn_A_log_bias"], np.float32).reshape(32, 1)
    gnw = np.asarray(inputs["L1_attn_gnorm_w"], np.float32)
    baseW = np.asarray(inputs["L1_ffn_base_W"], np.float32)
    gateW = np.asarray(inputs["L1_ffn_gate_W"], np.float32)
    loraA = np.asarray(inputs["L1_ffn_lora_A"], np.float32)
    loraB = np.asarray(inputs["L1_ffn_lora_B"], np.float32)

    perm = _deint_perm()
    wqb = wq_b.reshape(32, 192, 1536)
    wqbnT = _t(wqb[:, :128, :].reshape(4096, 1536))
    wqbpT = _t(wqb[:, 128:, :][:, perm, :].reshape(2048, 1536))
    wqaT = _t(wq_a)
    wkva_p = wkv_a.copy()
    wkva_p[512:576] = wkv_a[512:576][perm]
    wkvaT = _t(wkva_p)
    qnw_t = np.ascontiguousarray(q_norm_w.reshape(12, 128).T)
    kvnw_t = np.ascontiguousarray(kv_norm_w.reshape(4, 128).T)
    qwT = np.ascontiguousarray(qw.T)
    scT, zrT = _t(sc), _t(zr)
    ident = np.eye(128, dtype=np.float32)
    maskneg = np.where(np.arange(128)[None, :] <= np.arange(128)[:, None],
                       0.0, -1e30).astype(np.float32)
    wkvb = wkv_b.reshape(32, 256, 512)
    woT = _t(wo)
    bias_t = np.ascontiguousarray(bias.reshape(32, 128).T)
    WqT, WkT, WvT, WinT = _t(Wq), _t(Wk), _t(Wv), _t(Win)
    triu = np.triu(np.ones((128, 128), np.float32))
    masknegD = np.where(np.arange(128)[None, :] >= np.arange(128)[:, None],
                        0.0, -1e30).astype(np.float32)
    gnt = np.tile(gnw[None, :], (128, 1)).astype(np.float32)
    WgT, WoT1, baseT, gateT = _t(Wg), _t(Wo1), _t(baseW), _t(gateW)
    loraAT = _t(loraA.reshape(224, 4096))
    loraBf = np.ascontiguousarray(loraB.transpose(0, 2, 1).reshape(224, 4096))
    R = np.zeros((28, 224), np.float32)
    for e in range(28):
        R[e, 8 * e:8 * e + 8] = 1.0

    p128 = np.arange(128)
    p64 = np.arange(64)
    maps = []
    for c in range(NCORES):
        b, t0, h0, blk = c // 4, (c % 4) * T, 8 * (c % 4), c % 4
        pos = np.arange(t0, t0 + T)
        cosp, sinp = _rope_tabs_l0(pos)
        cos1, sin1 = _rope_tabs_l1(pos)
        idxB_qn = np.empty((128, 32), np.int32)
        idxB_qp = np.empty((64, 32), np.int32)
        for hl in range(8):
            for r in range(4):
                idxB_qn[:, hl * 4 + r] = (r * 4096 + (h0 + hl) * 128 + p128) * 1024
                idxB_qp[:, hl * 4 + r] = (r * 2048 + (h0 + hl) * 64 + p64) * 1024
        idxY = np.empty((128, 128), np.int32)
        for ht in range(32):
            for tci in range(4):
                idxY[:, ht * 4 + tci] = (((ht // 8) * 4096 + ((ht % 8) * 128 + p128) * 4
                                          + blk) * 1024 + tci * 256)
        idxE_kv = np.empty((128, 8), np.int32)
        for ti in range(2):
            for r in range(4):
                idxE_kv[:, ti * 4 + r] = (r * 1024 + 256 * blk + ti * 128 + p128) * 1024
        idxE_dtg = np.empty((128, 32), np.int32)
        for cc in range(32):
            idxE_dtg[:, cc] = ((cc // 8) * 4096 + (128 * (cc % 8) + p128) * 4 + blk) * 16
        wk = np.concatenate([wkvb[h0 + i, :128, :].T for i in range(8)], axis=1)
        wv = np.concatenate([wkvb[h0 + i, 128:, :].T for i in range(8)], axis=1)
        maps.append({
            "xT": np.ascontiguousarray(x[b, t0:t0 + T, :].T),
            "wqaT": wqaT, "wqbnT": wqbnT, "wqbpT": wqbpT,
            "qnw": qnw_t, "wkvaT": wkvaT, "kvnw": kvnw_t,
            "cosp": cosp, "sinp": sinp,
            "qwT": np.ascontiguousarray(qwT[512 * c:512 * (c + 1), :]),
            "scT": np.ascontiguousarray(scT[512 * c:512 * (c + 1), :]),
            "zrT": np.ascontiguousarray(zrT[512 * c:512 * (c + 1), :]),
            "identf_in": ident, "maskneg_in": maskneg,
            "wkvkT": np.ascontiguousarray(wk, dtype=np.float32),
            "wkvvT": np.ascontiguousarray(wv, dtype=np.float32),
            "woT": woT, "bias_t": bias_t,
            "WqT": WqT, "WkT": WkT, "WvT": WvT, "WinT": WinT,
            "dtb": dtb, "alog": alog, "cos1": cos1, "sin1": sin1,
            "gnt": gnt, "triuf_in": triu, "masknegD_in": masknegD,
            "WgT": WgT, "WoT1": WoT1, "baseT": baseT, "gateT": gateT,
            "loraAT": loraAT, "loraBf": loraBf,
            "R1_in": np.ascontiguousarray(R[:, 0:128]),
            "R2_in": np.ascontiguousarray(R[:, 128:224]),
            "idxB_qn": idxB_qn, "idxB_qp": idxB_qp, "idxY": idxY,
            "idxE_kv": idxE_kv, "idxE_dtg": idxE_dtg,
        })
    return maps


def kernel(**inputs):
    res = run_bass_kernel_spmd(build_fused(), host_prep_fused(inputs),
                               core_ids=list(range(NCORES))).results
    out = np.empty((B, S, D), np.float32)
    for c in range(NCORES):
        b, t0 = c // 4, (c % 4) * T
        out[b, t0:t0 + T, :] = res[c]["o_x4"].T
    return out


# version 08
# speedup vs baseline: 2.0036x; 1.0488x over previous
import contextlib
import numpy as np
import ml_dtypes
import concourse.bass as bass
import concourse.mybir as mybir
from concourse.bass_utils import run_bass_kernel_spmd

F32 = mybir.dt.float32
F16 = mybir.dt.float16
BF16 = mybir.dt.bfloat16
I32 = mybir.dt.int32
AF = mybir.ActivationFunctionType
ALU = mybir.AluOpType
AX = mybir.AxisListType.X

BF = ml_dtypes.bfloat16

NCORES = 8
T = 1024
S = 4096
B = 2
D = 4096



class Sched:

    ENGS = ('tensor', 'vector', 'scalar', 'gpsimd', 'sync')

    def __init__(self):
        self.items = []
        self.serialize_recording = False
        self.chain_compute_recording = False

    @staticmethod
    def _rng(ap):
        try:
            t = ap.tensor
            name = t.name
        except AttributeError:
            return None
        off = ap.offset
        kind = type(t).__name__
        if kind == 'SBTensorHandle':
            W = 1
            for d in list(t.shape)[1:]:
                W *= int(d)
            lo = off % W if W else 0
            ext = 1
            ok = True
            for pair in ap.ap:
                step, count = abs(pair[0]), pair[1]
                if step == W and W > 1:
                    continue
                if step > W:
                    ok = False
                    break
                ext += (count - 1) * step
            if ok:
                hi = lo + ext
                if hi <= W:
                    return (name, lo, hi)
            return (name, 0, W)
        ext = 1
        for pair in ap.ap:
            step, count = pair[0], pair[1]
            ext += (count - 1) * abs(step)
        return (name, off, off + ext)

    def _op(self, eng, inc, fn, reads=(), writes=(), is_coll=False):
        rd = [r for r in (self._rng(a) for a in reads if a is not None) if r]
        wr = [r for r in (self._rng(a) for a in writes if a is not None) if r]
        self.items.append((eng, fn, inc, rd, wr, is_coll, self.chain_compute_recording))
        if self.serialize_recording:
            self.items.append(('BARRIER', None, 0, (), (), (), False))

    def barrier(self):
        self.items.append(('BARRIER', None, 0, (), (), False, False))

    def dma(self, dst, src, cast=False, slow=False):
        eng = 'gpsimd' if cast else 'sync'
        kw = {}
        if slow:
            kw['allow_slow_non_contiguous'] = True
        self._op(eng, 16, lambda e, d=dst, s=src, kw=kw: e.dma_start(out=d, in_=s, **kw),
                 reads=[src], writes=[dst])

    def mm(self, out, lhsT, rhs, start=True, stop=True):
        self._op('tensor', 1,
                 lambda e, o=out, l=lhsT, r=rhs, st=start, sp=stop: e.matmul(o, l, r, start=st, stop=sp),
                 reads=[lhsT, rhs], writes=[out])

    def tr(self, out, in_, ident):
        self._op('tensor', 1, lambda e, o=out, i=in_, d=ident: e.transpose(o, i, d),
                 reads=[in_, ident], writes=[out])

    def act(self, out, in_, func, bias=0.0, scale=1.0, accum_out=None):
        aps_r = [in_] + [x for x in (bias, scale) if not isinstance(x, (int, float))]
        aps_w = [out] + ([accum_out] if accum_out is not None else [])
        self._op('scalar', 1,
                 lambda e, o=out, i=in_, f=func, b=bias, s=scale, a=accum_out:
                 e.activation(o, i, f, bias=b, scale=s, accum_out=a),
                 reads=aps_r, writes=aps_w)

    def scopy(self, out, in_):
        self._op('scalar', 1, lambda e, o=out, i=in_: e.copy(o, i),
                 reads=[in_], writes=[out])

    def tt(self, out, a, b, op):
        self._op('vector', 1, lambda e, o=out, x=a, y=b, p=op: e.tensor_tensor(o, x, y, p),
                 reads=[a, b], writes=[out])

    def ts(self, out, in0, s1, s2, op0, op1=ALU.bypass, accum_out=None):
        aps_r = [in0] + [x for x in (s1, s2) if x is not None and not isinstance(x, (int, float))]
        aps_w = [out] + ([accum_out] if accum_out is not None else [])
        self._op('vector', 1,
                 lambda e, o=out, i=in0, a=s1, b=s2, p0=op0, p1=op1, ac=accum_out:
                 e.tensor_scalar(o, i, a, b, p0, p1, accum_out=ac),
                 reads=aps_r, writes=aps_w)

    def stt(self, out, in0, scalar, in1, op0, op1):
        aps_r = [in0, in1] + ([scalar] if not isinstance(scalar, (int, float)) else [])
        self._op('vector', 1,
                 lambda e, o=out, i=in0, s=scalar, j=in1, p0=op0, p1=op1:
                 e.scalar_tensor_tensor(o, i, s, j, p0, p1),
                 reads=aps_r, writes=[out])

    def vcopy(self, out, in_):
        self._op('vector', 1, lambda e, o=out, i=in_: e.tensor_copy(o, i),
                 reads=[in_], writes=[out])

    def rmax(self, out, in_):
        self._op('vector', 1, lambda e, o=out, i=in_: e.reduce_max(o, i, axis=AX),
                 reads=[in_], writes=[out])

    def rsum(self, out, in_):
        self._op('vector', 1, lambda e, o=out, i=in_: e.reduce_sum(o, i, axis=AX),
                 reads=[in_], writes=[out])

    def recip(self, out, in_):
        self._op('vector', 1, lambda e, o=out, i=in_: e.reciprocal(o, i),
                 reads=[in_], writes=[out])

    def memset(self, ap, val):
        self._op('vector', 1, lambda e, a=ap, v=val: e.memset(a, v), writes=[ap])

    def replay(self, nc, serialize=False):
        if serialize and serialize not in ('dveact', 'dvegp', 'split'):
            items2 = []
            for it in self.items:
                if it[0] != 'BARRIER':
                    items2.append(it)
                    if serialize is True:
                        items2.append(('BARRIER', None, 0, (), (), False, False))
                    elif serialize == 'dma' and it[2] >= 16:
                        items2.append(('BARRIER', None, 0, (), (), False, False))
                    elif serialize == 'compute' and it[2] < 16:
                        items2.append(('BARRIER', None, 0, (), (), False, False))
            self.items = items2
        n = len(self.items)

        HIST_MAX = 96
        hist = {}
        dep_of = [0] * n
        deps = [[] for _ in range(n)]
        eng_of = [it[0] for it in self.items]
        barrier_dep = -1

        def conflicts(name, lo, hi, want_writes, want_reads):
            out = []
            h = hist.get(name)
            if not h:
                return out
            for (l, h2, idx, isw) in h:
                if l < hi and lo < h2:
                    if (isw and want_writes) or ((not isw) and want_reads):
                        out.append(idx)
            return out

        last_before_barrier = -1
        prev_dveact = -1
        prev_act = [-1]
        prev_ch3 = [-1]
        last_dma = {'sync': -1, 'gpsimd': -1}
        for i, (eng, fn, inc, rd, wr, is_coll, ch3) in enumerate(self.items):
            if eng == 'BARRIER':
                last_before_barrier = i - 1
                hist.clear()
                continue
            dd = set()
            if last_before_barrier >= 0:
                dd.add(('ALL', last_before_barrier))
            for (name, lo, hi) in rd:
                for j in conflicts(name, lo, hi, True, False):
                    dd.add(('I', j))
            for (name, lo, hi) in wr:
                for j in conflicts(name, lo, hi, True, True):
                    dd.add(('I', j))
            _sergrp = {'dveact': ('vector', 'scalar'),
                       'dvegp': ('vector', 'scalar', 'gpsimd')}.get(serialize)
            prev_ch3_target = -1
            if ch3 and eng in ('tensor', 'vector', 'scalar'):
                if prev_ch3[0] >= 0:
                    dd.add(('I', prev_ch3[0]))
                    prev_ch3_target = prev_ch3[0]
                prev_ch3[0] = i
                if last_dma['sync'] >= 0:
                    dd.add(('I', last_dma['sync']))
                if last_dma['gpsimd'] >= 0:
                    dd.add(('I', last_dma['gpsimd']))
            if eng in ('sync', 'gpsimd') and inc >= 16 and not is_coll:
                last_dma[eng] = i
            elif _sergrp and eng in _sergrp:
                if prev_dveact >= 0:
                    dd.add(('I', prev_dveact))
                prev_dveact = i
            elif serialize == 'split':
                if eng == 'vector':
                    if prev_dveact >= 0:
                        dd.add(('I', prev_dveact))
                    prev_dveact = i
                elif eng == 'scalar':
                    if prev_act[0] >= 0:
                        dd.add(('I', prev_act[0]))
                    prev_act[0] = i
            pruned = []
            for kind, j in dd:
                if (kind == 'I' and eng == 'tensor' and eng_of[j] == 'tensor'
                        and j != prev_ch3_target):
                    continue
                pruned.append((kind, j))
            deps[i] = pruned
            for (name, lo, hi) in rd:
                hist.setdefault(name, []).append((lo, hi, i, False))
            for (name, lo, hi) in wr:
                hist.setdefault(name, []).append((lo, hi, i, True))
            for name in (set(x[0] for x in rd) | set(x[0] for x in wr)):
                h = hist[name]
                if len(h) > HIST_MAX:
                    cut = h[:HIST_MAX // 2]
                    keep = h[HIST_MAX // 2:]
                    coll_entries = []
                    cr = [x[2] for x in cut if not x[3]]
                    cw = [x[2] for x in cut if x[3]]
                    if cr:
                        coll_entries.append((-1 << 40, 1 << 40, max(cr), False))
                    if cw:
                        coll_entries.append((-1 << 40, 1 << 40, max(cw), True))
                    hist[name] = coll_entries + keep

        depset = set()
        for i in range(n):
            for kind, j in deps[i]:
                if kind == 'I':
                    depset.add(j)
        last_of = {}
        for i, it in enumerate(self.items):
            eng = it[0]
            if eng == 'BARRIER':
                depset.update(last_of.values())
            else:
                key = 'cc' if it[5] else eng
                last_of[key] = i
        depset.update(last_of.values())
        for i, it in enumerate(self.items):
            if it[5]:
                depset.add(i)

        att = [0] * n
        pend = {e: 0 for e in self.ENGS}
        pend_cc = 0
        cum = {e: [0] * n for e in self.ENGS}
        cum_cc = [0] * n
        run = {e: 0 for e in self.ENGS}
        run_cc = 0
        for i, (eng, fn, inc, rd, wr, is_coll, ch3) in enumerate(self.items):
            if eng != 'BARRIER':
                if is_coll:
                    pend_cc += inc
                    if i in depset:
                        att[i] = pend_cc
                        run_cc += pend_cc
                        pend_cc = 0
                else:
                    pend[eng] += inc
                    if i in depset or inc >= 16:
                        att[i] = pend[eng]
                        run[eng] += pend[eng]
                        pend[eng] = 0
            for e in self.ENGS:
                cum[e][i] = run[e]
            cum_cc[i] = run_cc

        with contextlib.ExitStack() as st:
            sems = {e: st.enter_context(nc.semaphore(f"sem_{e}")) for e in self.ENGS}
            sem_cc = st.enter_context(nc.semaphore("sem_cc"))
            block = st.enter_context(nc.Block())
            final_targets = {e: run[e] for e in self.ENGS}
            final_cc = run_cc

            def make_body(name):
                def body(eng):
                    maxw = {e: 0 for e in self.ENGS}
                    maxw_cc = 0
                    for i, (e_, fn, inc, rd, wr, is_coll, ch3) in enumerate(self.items):
                        if e_ != name:
                            continue
                        want = {}
                        want_cc = 0
                        for kind, j in deps[i]:
                            if kind == 'ALL':
                                for e2 in self.ENGS:
                                    want[e2] = max(want.get(e2, 0), cum[e2][j])
                                want_cc = max(want_cc, cum_cc[j])
                            else:
                                ej = eng_of[j]
                                if self.items[j][5]:
                                    want_cc = max(want_cc, cum_cc[j])
                                else:
                                    want[ej] = max(want.get(ej, 0), cum[ej][j])
                        for e2, v in want.items():
                            if v > maxw[e2]:
                                eng.wait_ge(sems[e2], v)
                                maxw[e2] = v
                        if want_cc > maxw_cc:
                            eng.wait_ge(sem_cc, want_cc)
                            maxw_cc = want_cc
                        ins = fn(eng)
                        if att[i]:
                            ins.then_inc(sem_cc if is_coll else sems[name], att[i])
                    if name == 'sync':
                        for e2 in self.ENGS:
                            if final_targets[e2] > maxw[e2]:
                                eng.wait_ge(sems[e2], final_targets[e2])
                        if final_cc > maxw_cc:
                            eng.wait_ge(sem_cc, final_cc)
                return body

            for name in self.ENGS:
                getattr(block, name)(make_body(name))



def _bf(x):
    return np.ascontiguousarray(x, dtype=np.float32)


def _t(x):
    return np.ascontiguousarray(np.asarray(x, dtype=np.float32).T)


def _rope_tabs_l0(pos):
    freqs = 1.0 / 10000.0 ** (np.arange(0, 64, 2, dtype=np.float32) / 64)
    ang = np.outer(pos.astype(np.float32), freqs)
    return np.ascontiguousarray(np.cos(ang).T), np.ascontiguousarray(np.sin(ang).T)


def _rope_tabs_l1(pos):
    inv = 1.0 / 10000.0 ** (np.arange(0, 128, 2, dtype=np.float32) / 128)
    ang = np.outer(pos.astype(np.float32), inv)
    return np.ascontiguousarray(np.cos(ang).T), np.ascontiguousarray(np.sin(ang).T)


def _deint_perm():
    return np.concatenate([np.arange(0, 64, 2), np.arange(1, 64, 2)])



def build_launch_a():
    nc = bass.Bass("TRN2", target_bir_lowering=False, debug=False)
    dr = lambda n, s, d, k: nc.dram_tensor(n, s, d, kind=k).ap()

    xT = dr("xT", [D, T], F32, "ExternalInput")
    wqaT = dr("wqaT", [D, 1536], F32, "ExternalInput")
    wqbnT = dr("wqbnT", [1536, 4096], F32, "ExternalInput")
    wqbpT = dr("wqbpT", [1536, 2048], F32, "ExternalInput")
    qnw = dr("qnw", [128, 12], F32, "ExternalInput")
    wkvaT = dr("wkvaT", [D, 576], F32, "ExternalInput")
    kvnw = dr("kvnw", [128, 4], F32, "ExternalInput")
    cosp = dr("cosp", [32, T], F32, "ExternalInput")
    sinp = dr("sinp", [32, T], F32, "ExternalInput")
    qwT = dr("qwT", [512, 512], I32, "ExternalInput")
    scT = dr("scT", [512, 32], F32, "ExternalInput")
    zrT = dr("zrT", [512, 32], F32, "ExternalInput")
    identf_in = dr("identf_in", [128, 128], F32, "ExternalInput")

    o_qn = dr("o_qn", [4096, T], BF16, "ExternalOutput")
    o_qp = dr("o_qp", [2048, T], BF16, "ExternalOutput")
    o_ckv = dr("o_ckv", [512, T], BF16, "ExternalOutput")
    o_kpe = dr("o_kpe", [64, T], BF16, "ExternalOutput")
    o_wdq = dr("o_wdq", [4096, 512], BF16, "ExternalOutput")

    S_ = Sched()
    with contextlib.ExitStack() as st:
        sb = lambda n, s, d: st.enter_context(nc.sbuf_tensor(n, s, d))
        ps = lambda n, s, d: st.enter_context(nc.psum_tensor(n, s, d))

        x_bf = sb("x_bf", [128, 32 * T], BF16)
        wcol = sb("wcol", [128, 32 * 128], BF16)
        qlr = sb("qlr", [128, 12 * T], F16)
        qln = sb("qln", [128, 12 * T], BF16)
        kvr = sb("kvr", [128, 5 * T], F16)
        sq = sb("sq", [128, T], F16)
        row = sb("row", [1, T], F32)
        row2 = sb("row2", [1, T], F32)
        bc = sb("bc", [128, T], F32)
        onecol = sb("onecol", [128, 1], F16)
        onerow = sb("onerow", [1, 128], F32)
        c_eps6 = sb("c_eps6", [128, 1], F32)
        qnw_s = sb("qnw_s", [128, 12], F32)
        kvnw_s = sb("kvnw_s", [128, 4], F32)
        trig = sb("trig", [64, 2 * T], F32)
        rp = [sb(f"rp{i}", [32, T], F32) for i in range(4)]
        ro = [sb(f"ro{i}", [32, T], BF16) for i in range(2)]
        x2c = sb("x2c", [32, T], F32)
        otile = sb("otile", [128, T], BF16)
        qw_s = sb("qw_s", [128, 4 * 512], I32)
        qtmp = sb("qtmp", [128, 512], I32)
        wun = sb("wun", [128, 4096], F16)
        waf = sb("waf", [128, 4096], BF16)
        sc_s = sb("sc_s", [128, 4 * 32], F32)
        zr_s = sb("zr_s", [128, 4 * 32], F32)
        wtr = sb("wtr", [128, 512], BF16)
        identf = sb("identf", [128, 128], F32)
        ident_b = sb("ident_b", [128, 128], BF16)

        pQ = ps("pQ", [128, T], F32)
        pR = ps("pR", [1, T], F32)
        pB = ps("pB", [128, T], F32)
        pT = ps("pT", [128, 512], BF16)

        for t in range(32):
            S_.dma(x_bf[:, t * T:(t + 1) * T], xT[t * 128:(t + 1) * 128, :], cast=True)
        S_.dma(qnw_s[:], qnw)
        S_.dma(kvnw_s[:], kvnw)
        S_.dma(trig[0:32, 0:T], cosp)
        S_.dma(trig[0:32, T:2 * T], sinp)
        S_.dma(identf[:], identf_in)
        S_.dma(ident_b[:], identf_in, cast=True)
        S_.memset(onecol[:], 1.0)
        S_.memset(onerow[:], 1.0)
        S_.memset(c_eps6[:], 1e-6)

        def mm_otile(psum, wT_dram, ocol0, owidth, rhs_sb, nk, extra_wcols=0):
            S_.dma(wcol[:, 0:nk * owidth].rearrange("p (k o) -> p k o", k=nk),
                   wT_dram[:, ocol0:ocol0 + owidth].rearrange("(k p) o -> p k o", p=128),
                   cast=True)
            for k in range(nk):
                l = wcol[0:128, k * owidth:(k + 1) * owidth]
                for h in range(0, T, 512):
                    S_.mm(psum[0:owidth, h:h + 512], l, rhs_sb[:, k * T + h:k * T + h + 512],
                          start=(k == 0), stop=(k == nk - 1))

        for ot in range(12):
            mm_otile(pQ, wqaT, ot * 128, 128, x_bf, 32)
            S_.vcopy(qlr[:, ot * T:(ot + 1) * T], pQ[:, :])
            S_.act(sq[:], pQ[:, :], AF.Square)
            for h in range(0, T, 512):
                S_.mm(pR[0:1, h:h + 512], onecol[:], sq[:, h:h + 512],
                      start=(ot == 0), stop=(ot == 11))
        S_.act(row[:], pR[0:1, :], AF.Ln, bias=c_eps6[0:1, 0:1], scale=1.0 / 1536)
        S_.act(row[:], row[:], AF.Exp, scale=-0.5)
        for h in range(0, T, 512):
            S_.mm(pB[:, h:h + 512], onerow[:], row[0:1, h:h + 512])
        S_.vcopy(bc[:], pB[:, :])
        for ot in range(12):
            S_.stt(qln[:, ot * T:(ot + 1) * T], qlr[:, ot * T:(ot + 1) * T],
                   qnw_s[:, ot:ot + 1], bc[:], ALU.mult, ALU.mult)

        for ot in range(4):
            mm_otile(pQ, wkvaT, ot * 128, 128, x_bf, 32)
            S_.vcopy(kvr[:, ot * T:(ot + 1) * T], pQ[:, :])
            S_.act(sq[:], pQ[:, :], AF.Square)
            for h in range(0, T, 512):
                S_.mm(pR[0:1, h:h + 512], onecol[:], sq[:, h:h + 512],
                      start=(ot == 0), stop=(ot == 3))
        mm_otile(pQ, wkvaT, 512, 64, x_bf, 32)
        S_.vcopy(kvr[0:64, 4 * T:5 * T], pQ[0:64, :])
        S_.act(row[:], pR[0:1, :], AF.Ln, bias=c_eps6[0:1, 0:1], scale=1.0 / 512)
        S_.act(row[:], row[:], AF.Exp, scale=-0.5)
        for h in range(0, T, 512):
            S_.mm(pB[:, h:h + 512], onerow[:], row[0:1, h:h + 512])
        S_.vcopy(bc[:], pB[:, :])
        for ot in range(4):
            S_.stt(otile[:, :], kvr[:, ot * T:(ot + 1) * T],
                   kvnw_s[:, ot:ot + 1], bc[:], ALU.mult, ALU.mult)
            S_.dma(o_ckv[ot * 128:(ot + 1) * 128, :], otile[:, :])

        def rope_deint(src32_hi_getter, x1ap, x2ap):
            S_.vcopy(x2c[:, :], x2ap)
            cos_t = trig[0:32, 0:T]
            sin_t = trig[0:32, T:2 * T]
            S_.tt(rp[0][:, :], x1ap, cos_t, ALU.mult)
            S_.tt(rp[1][:, :], x2c[:, :], sin_t, ALU.mult)
            S_.tt(rp[2][:, :], x1ap, sin_t, ALU.mult)
            S_.tt(rp[3][:, :], x2c[:, :], cos_t, ALU.mult)
            S_.tt(ro[0][:, :], rp[0][:, :], rp[1][:, :], ALU.subtract)
            S_.tt(ro[1][:, :], rp[2][:, :], rp[3][:, :], ALU.add)

        rope_deint(None, kvr[0:32, 4 * T:5 * T], kvr[32:64, 4 * T:5 * T])
        S_.dma(o_kpe[0:32, :], ro[0][:, :])
        S_.dma(o_kpe[32:64, :], ro[1][:, :])

        for ot in range(32):
            mm_otile(pQ, wqbnT, ot * 128, 128, qln, 12)
            S_.vcopy(otile[:, :], pQ[:, :])
            S_.dma(o_qn[ot * 128:(ot + 1) * 128, :], otile[:, :])

        for hh in range(32):
            mm_otile(pQ, wqbpT, hh * 64, 64, qln, 12)
            S_.vcopy(kvr[0:64, 4 * T:5 * T], pQ[0:64, :])
            rope_deint(None, kvr[0:32, 4 * T:5 * T], kvr[32:64, 4 * T:5 * T])
            S_.dma(o_qp[hh * 64:hh * 64 + 32, :], ro[0][:, :])
            S_.dma(o_qp[hh * 64 + 32:hh * 64 + 64, :], ro[1][:, :])

        for t in range(4):
            S_.dma(qw_s[:, t * 512:(t + 1) * 512], qwT[t * 128:(t + 1) * 128, :])
            S_.dma(sc_s[:, t * 32:(t + 1) * 32], scT[t * 128:(t + 1) * 128, :])
            S_.dma(zr_s[:, t * 32:(t + 1) * 32], zrT[t * 128:(t + 1) * 128, :])
        for t in range(4):
            q = qw_s[:, t * 512:(t + 1) * 512]
            for j in range(8):
                S_.ts(qtmp[:], q, float(4 * j), 15.0,
                      ALU.logical_shift_right, ALU.bitwise_and)
                S_.vcopy(wun[:, j::8], qtmp[:])
            for g in range(32):
                S_.ts(waf[:, g * 128:(g + 1) * 128], wun[:, g * 128:(g + 1) * 128],
                      sc_s[:, t * 32 + g:t * 32 + g + 1],
                      zr_s[:, t * 32 + g:t * 32 + g + 1],
                      ALU.mult, ALU.subtract)
            for g in range(0, 32, 4):
                for gg in range(4):
                    S_.tr(pT[:, gg * 128:(gg + 1) * 128],
                          waf[:, (g + gg) * 128:(g + gg + 1) * 128], ident_b[:])
                S_.vcopy(wtr[:, :], pT[:, :])
                for gg in range(4):
                    S_.dma(o_wdq[(g + gg) * 128:(g + gg + 1) * 128, t * 128:(t + 1) * 128],
                           wtr[:, gg * 128:(gg + 1) * 128])

        S_.replay(nc)
    return nc


def host_prep_a(inputs):
    x = np.asarray(inputs["L0_attn_x"], np.float32)
    wq_a = np.asarray(inputs["L0_attn_wq_a"], np.float32)
    q_norm_w = np.asarray(inputs["L0_attn_q_norm_w"], np.float32)
    wq_b = np.asarray(inputs["L0_attn_wq_b"], np.float32)
    wkv_a = np.asarray(inputs["L0_attn_wkv_a"], np.float32)
    kv_norm_w = np.asarray(inputs["L0_attn_kv_norm_w"], np.float32)
    qw = np.asarray(inputs["L0_ffn_q_weights"], np.int32)
    sc = np.asarray(inputs["L0_ffn_scales"], np.float32)
    zr = np.asarray(inputs["L0_ffn_zeros"], np.float32)

    perm = _deint_perm()
    wqb = wq_b.reshape(32, 192, 1536)
    wq_b_nope = wqb[:, :128, :].reshape(4096, 1536)
    wq_b_pe = wqb[:, 128:, :][:, perm, :].reshape(2048, 1536)
    wqbnT = _t(wq_b_nope)
    wqbpT = _t(wq_b_pe)
    wqaT = _t(wq_a)
    wkva_p = wkv_a.copy()
    wkva_p[512:576] = wkv_a[512:576][perm]
    wkvaT = _t(wkva_p)
    qnw_t = np.ascontiguousarray(q_norm_w.reshape(12, 128).T)
    kvnw_t = np.ascontiguousarray(kv_norm_w.reshape(4, 128).T)
    qwT = _t(qw.astype(np.float64)).astype(np.int32)
    qwT = np.ascontiguousarray(qw.T)
    scT = _t(sc)
    zrT = _t(zr)
    ident = np.eye(128, dtype=np.float32)

    maps = []
    for c in range(NCORES):
        b, t0 = c // 4, (c % 4) * T
        pos = np.arange(t0, t0 + T)
        cosp, sinp = _rope_tabs_l0(pos)
        maps.append({
            "xT": np.ascontiguousarray(x[b, t0:t0 + T, :].T),
            "wqaT": wqaT, "wqbnT": wqbnT, "wqbpT": wqbpT,
            "qnw": qnw_t, "wkvaT": wkvaT, "kvnw": kvnw_t,
            "cosp": cosp, "sinp": sinp,
            "qwT": np.ascontiguousarray(qwT[512 * c:512 * (c + 1), :]),
            "scT": np.ascontiguousarray(scT[512 * c:512 * (c + 1), :]),
            "zrT": np.ascontiguousarray(zrT[512 * c:512 * (c + 1), :]),
            "identf_in": ident,
        })
    return maps





L0_SCALE = 1.0 / float(np.sqrt(192.0))


def build_launch_b():
    nc = bass.Bass("TRN2", target_bir_lowering=False, debug=False)
    dr = lambda n, s, d, k: nc.dram_tensor(n, s, d, kind=k).ap()

    qn_h = dr("qn_h", [1024, S], BF16, "ExternalInput")
    qp_h = dr("qp_h", [512, S], BF16, "ExternalInput")
    ckv = dr("ckv", [512, S], BF16, "ExternalInput")
    kpe = dr("kpe", [64, S], BF16, "ExternalInput")
    wkvkT = dr("wkvkT", [512, 1024], F32, "ExternalInput")
    wkvvT = dr("wkvvT", [512, 1024], F32, "ExternalInput")
    identf_in = dr("identf_in", [128, 128], F32, "ExternalInput")
    maskneg_in = dr("maskneg_in", [128, 128], F32, "ExternalInput")

    o_y = dr("o_y", [S, 1024], BF16, "ExternalOutput")

    S_ = Sched()
    with contextlib.ExitStack() as st:
        sb = lambda n, s, d: st.enter_context(nc.sbuf_tensor(n, s, d))
        ps = lambda n, s, d: st.enter_context(nc.psum_tensor(n, s, d))

        ckv_s = sb("ckv_s", [128, 4 * S], BF16)
        kpe_s = sb("kpe_s", [64, S], BF16)
        wkvk_s = sb("wkvk_s", [128, 4 * 1024], BF16)
        wkvv_s = sb("wkvv_s", [128, 4 * 1024], BF16)
        qn_s = sb("qn_s", [128, S], BF16)
        qp_s = sb("qp_s", [64, S], BF16)
        kh_s = sb("kh_s", [128, S], BF16)
        vtm_s = sb("vtm_s", [128, S], BF16)
        Ssb = sb("Ssb", [128, S], F32)
        Pb = sb("Pb", [128, S], BF16)
        PT_s = sb("PT_s", [128, S], BF16)
        junk = sb("junk", [128, 512], F32)
        mparts = sb("mparts", [128, 8], F32)
        rsparts = sb("rsparts", [128, 8], F32)
        m_col = sb("m_col", [128, 1], F32)
        negm = sb("negm", [128, 1], F32)
        rs_col = sb("rs_col", [128, 1], F32)
        rinv = sb("rinv", [128, 1], F32)
        ysb = sb("ysb", [128, 128], BF16)
        ident_b = sb("ident_b", [128, 128], BF16)
        maskneg = sb("maskneg", [128, 128], F32)

        pS = ps("pS", [128, 512], F32)
        pV = ps("pV", [128, 128], F32)
        pTt = ps("pTt", [128, 512], BF16)
        pAV = ps("pAV", [128, 128], F32)

        for t in range(4):
            S_.dma(ckv_s[:, t * S:(t + 1) * S], ckv[t * 128:(t + 1) * 128, :])
            S_.dma(wkvk_s[:, t * 1024:(t + 1) * 1024], wkvkT[t * 128:(t + 1) * 128, :], cast=True)
            S_.dma(wkvv_s[:, t * 1024:(t + 1) * 1024], wkvvT[t * 128:(t + 1) * 128, :], cast=True)
        S_.dma(kpe_s[:, :], kpe)
        S_.dma(ident_b[:], identf_in, cast=True)
        S_.dma(maskneg[:], maskneg_in)

        for hl in range(8):
            S_.dma(qn_s[:, :], qn_h[hl * 128:(hl + 1) * 128, :])
            S_.dma(qp_s[:, :], qp_h[hl * 64:(hl + 1) * 64, :])
            for tc in range(8):
                for kc in range(4):
                    S_.mm(pS[:, :], wkvk_s[:, kc * 1024 + hl * 128:kc * 1024 + (hl + 1) * 128],
                          ckv_s[:, kc * S + tc * 512:kc * S + (tc + 1) * 512],
                          start=(kc == 0), stop=(kc == 3))
                S_.vcopy(kh_s[:, tc * 512:(tc + 1) * 512], pS[:, :])
            for tt in range(32):
                for kc in range(4):
                    S_.mm(pV[:, :], ckv_s[:, kc * S + tt * 128:kc * S + (tt + 1) * 128],
                          wkvv_s[:, kc * 1024 + hl * 128:kc * 1024 + (hl + 1) * 128],
                          start=(kc == 0), stop=(kc == 3))
                S_.vcopy(vtm_s[:, tt * 128:(tt + 1) * 128], pV[:, :])
            for qt in range(32):
                nk = qt + 1
                kw = nk * 128
                nkc = (kw + 511) // 512
                for ci in range(nkc):
                    w = min(512, kw - ci * 512)
                    S_.mm(pS[0:128, 0:w], qn_s[:, qt * 128:(qt + 1) * 128],
                          kh_s[:, ci * 512:ci * 512 + w], start=True, stop=False)
                    S_.mm(pS[0:128, 0:w], qp_s[0:64, qt * 128:(qt + 1) * 128],
                          kpe_s[0:64, ci * 512:ci * 512 + w], start=False, stop=True)
                    last = (ci == nkc - 1)
                    wnd = w - 128 if last else w
                    if wnd > 0:
                        S_.vcopy(Ssb[:, ci * 512:ci * 512 + wnd], pS[0:128, 0:wnd])
                    if last:
                        S_.tt(Ssb[:, kw - 128:kw], pS[0:128, w - 128:w], maskneg[:], ALU.add)
                    S_.ts(junk[:, 0:w], Ssb[:, ci * 512:ci * 512 + w], 1.0, None,
                          ALU.mult, ALU.max, accum_out=mparts[:, ci:ci + 1])
                S_.rmax(m_col[:], mparts[:, 0:nkc])
                S_.ts(negm[:], m_col[:], -L0_SCALE, None, ALU.mult)
                for ci in range(nkc):
                    w = min(512, kw - ci * 512)
                    S_.act(Pb[:, ci * 512:ci * 512 + w], Ssb[:, ci * 512:ci * 512 + w],
                           AF.Exp, bias=negm[:, 0:1], scale=L0_SCALE,
                           accum_out=rsparts[:, ci:ci + 1])
                S_.rsum(rs_col[:], rsparts[:, 0:nkc])
                S_.recip(rinv[:], rs_col[:])
                for g in range(0, nk, 4):
                    ng = min(4, nk - g)
                    for kk in range(ng):
                        S_.tr(pTt[:, kk * 128:(kk + 1) * 128],
                              Pb[:, (g + kk) * 128:(g + kk + 1) * 128], ident_b[:])
                    S_.vcopy(PT_s[:, g * 128:(g + ng) * 128], pTt[:, 0:ng * 128])
                for kt in range(nk):
                    S_.mm(pAV[:, :], PT_s[:, kt * 128:(kt + 1) * 128],
                          vtm_s[:, kt * 128:(kt + 1) * 128],
                          start=(kt == 0), stop=(kt == nk - 1))
                S_.act(ysb[:], pAV[:, :], AF.Copy, scale=rinv[:, 0:1])
                S_.dma(o_y[qt * 128:(qt + 1) * 128, hl * 128:(hl + 1) * 128], ysb[:])

        S_.replay(nc)
    return nc


def host_prep_b(inputs, res_a):
    wkv_b = np.asarray(inputs["L0_attn_wkv_b"], np.float32)
    qn_full, qp_full, ckv_full, kpe_full = [], [], [], []
    for b in range(2):
        qn_full.append(np.concatenate([res_a[4 * b + i]["o_qn"] for i in range(4)], axis=1))
        qp_full.append(np.concatenate([res_a[4 * b + i]["o_qp"] for i in range(4)], axis=1))
        ckv_full.append(np.concatenate([res_a[4 * b + i]["o_ckv"] for i in range(4)], axis=1))
        kpe_full.append(np.concatenate([res_a[4 * b + i]["o_kpe"] for i in range(4)], axis=1))
    maskneg = np.where(np.arange(128)[None, :] <= np.arange(128)[:, None],
                       0.0, -1e30).astype(np.float32)
    ident = np.eye(128, dtype=np.float32)
    maps = []
    for c in range(NCORES):
        b, h0 = c // 4, 8 * (c % 4)
        wkvb = wkv_b.reshape(32, 256, 512)
        wk = np.concatenate([wkvb[h0 + i, :128, :].T for i in range(8)], axis=1)
        wv = np.concatenate([wkvb[h0 + i, 128:, :].T for i in range(8)], axis=1)
        maps.append({
            "qn_h": np.ascontiguousarray(qn_full[b][h0 * 128:(h0 + 8) * 128, :]),
            "qp_h": np.ascontiguousarray(qp_full[b][h0 * 64:(h0 + 8) * 64, :]),
            "ckv": np.ascontiguousarray(ckv_full[b]),
            "kpe": np.ascontiguousarray(kpe_full[b]),
            "wkvkT": np.ascontiguousarray(wk, dtype=np.float32),
            "wkvvT": np.ascontiguousarray(wv, dtype=np.float32),
            "identf_in": ident,
            "maskneg_in": maskneg,
        })
    return maps



def build_launch_c():
    nc = bass.Bass("TRN2", target_bir_lowering=False, debug=False)
    dr = lambda n, s, d, k: nc.dram_tensor(n, s, d, kind=k).ap()

    yT = dr("yT", [D, T], BF16, "ExternalInput")
    xT = dr("xT", [D, T], F32, "ExternalInput")
    woT = dr("woT", [D, D], F32, "ExternalInput")
    wdq = dr("wdq", [D, D], BF16, "ExternalInput")
    bias_t = dr("bias_t", [128, 32], F32, "ExternalInput")
    o_x2 = dr("o_x2", [D, T], F32, "ExternalOutput")

    S_ = Sched()
    TC = 512
    with contextlib.ExitStack() as st:
        sb = lambda n, s, d: st.enter_context(nc.sbuf_tensor(n, s, d))
        ps = lambda n, s, d: st.enter_context(nc.psum_tensor(n, s, d))

        ybf = sb("ybf", [128, 32 * TC], BF16)
        wcol = sb("wcol", [128, 32 * 128], BF16)
        xf32 = sb("xf32", [128, TC], F32)
        r16 = sb("r16", [128, 32 * TC], F16)
        x1b = sb("x1b", [128, 32 * TC], BF16)
        x1h = sb("x1h", [128, 32 * TC], F16)
        sq = sb("sq", [128, TC], F16)
        onecol = sb("onecol", [128, 1], F16)
        onerow = sb("onerow", [1, 128], F32)
        bias_s = sb("bias_s", [128, 32], F32)
        tmp32 = sb("tmp32", [128, TC], F32)
        rowa = sb("rowa", [1, TC], F32)
        rowb = sb("rowb", [1, TC], F32)
        rowc = sb("rowc", [1, TC], F32)
        m_bc = sb("m_bc", [128, TC], F32)
        rstd_bc = sb("rstd_bc", [128, TC], F32)
        c_eps5 = sb("c_eps5", [1, 1], F32)

        pA = ps("pA", [128, TC], F32)
        pR = ps("pR", [1, 2 * TC], F32)
        pB = ps("pB", [128, TC], F32)

        S_.memset(onecol[:], 1.0)
        S_.memset(onerow[:], 1.0)
        S_.memset(c_eps5[:], 1e-5)
        S_.dma(bias_s[:], bias_t)

        def wload(dst, wT_dram, ocol0, owidth, nk, cast):
            S_.dma(dst[:, 0:nk * owidth].rearrange("p (k o) -> p k o", k=nk),
                   wT_dram[:, ocol0:ocol0 + owidth].rearrange("(k p) o -> p k o", p=128),
                   cast=cast)

        def layer_norm(src16, dst_bf, dst_f16, out_dram=None, tc0=0):
            S_.act(rowa[:], pR[0:1, 0:TC], AF.Copy, scale=1.0 / D)
            S_.act(rowb[:], pR[0:1, TC:2 * TC], AF.Copy, scale=1.0 / D)
            S_.tt(rowc[:], rowa[:], rowa[:], ALU.mult)
            S_.tt(rowb[:], rowb[:], rowc[:], ALU.subtract)
            S_.act(rowb[:], rowb[:], AF.Ln, bias=c_eps5[0:1, 0:1])
            S_.act(rowb[:], rowb[:], AF.Exp, scale=-0.5)
            S_.mm(pB[:, :], onerow[:], rowa[0:1, :])
            S_.vcopy(m_bc[:], pB[:, :])
            S_.mm(pB[:, :], onerow[:], rowb[0:1, :])
            S_.vcopy(rstd_bc[:], pB[:, :])
            for ot in range(32):
                sl = src16[:, ot * TC:(ot + 1) * TC]
                S_.tt(tmp32[:], sl, m_bc[:], ALU.subtract)
                if dst_bf is not None:
                    S_.tt(dst_bf[:, ot * TC:(ot + 1) * TC], tmp32[:], rstd_bc[:], ALU.mult)
                if dst_f16 is not None:
                    S_.tt(dst_f16[:, ot * TC:(ot + 1) * TC], tmp32[:], rstd_bc[:], ALU.mult)
                if out_dram is not None:
                    S_.tt(tmp32[:], tmp32[:], rstd_bc[:], ALU.mult)
                    S_.dma(out_dram[ot * 128:(ot + 1) * 128, tc0:tc0 + TC], tmp32[:])

        for tc in range(0, T, TC):
            for t in range(32):
                S_.dma(ybf[:, t * TC:(t + 1) * TC], yT[t * 128:(t + 1) * 128, tc:tc + TC])
            for ot in range(32):
                wload(wcol, woT, ot * 128, 128, 32, True)
                for k in range(32):
                    S_.mm(pA[:, :], wcol[:, k * 128:(k + 1) * 128], ybf[:, k * TC:(k + 1) * TC],
                          start=(k == 0), stop=(k == 31))
                S_.dma(xf32[:], xT[ot * 128:(ot + 1) * 128, tc:tc + TC])
                S_.tt(r16[:, ot * TC:(ot + 1) * TC], pA[:, :], xf32[:], ALU.add)
                S_.act(sq[:], r16[:, ot * TC:(ot + 1) * TC], AF.Square)
                S_.mm(pR[0:1, 0:TC], onecol[:], r16[:, ot * TC:(ot + 1) * TC],
                      start=(ot == 0), stop=(ot == 31))
                S_.mm(pR[0:1, TC:2 * TC], onecol[:], sq[:], start=(ot == 0), stop=(ot == 31))
            layer_norm(r16, x1b, x1h)
            for ot in range(32):
                wload(wcol, wdq, ot * 128, 128, 32, False)
                for k in range(32):
                    S_.mm(pA[:, :], wcol[:, k * 128:(k + 1) * 128], x1b[:, k * TC:(k + 1) * TC],
                          start=(k == 0), stop=(k == 31))
                S_.stt(r16[:, ot * TC:(ot + 1) * TC], pA[:, :], bias_s[:, ot:ot + 1],
                       x1h[:, ot * TC:(ot + 1) * TC], ALU.add, ALU.add)
                S_.act(sq[:], r16[:, ot * TC:(ot + 1) * TC], AF.Square)
                S_.mm(pR[0:1, 0:TC], onecol[:], r16[:, ot * TC:(ot + 1) * TC],
                      start=(ot == 0), stop=(ot == 31))
                S_.mm(pR[0:1, TC:2 * TC], onecol[:], sq[:], start=(ot == 0), stop=(ot == 31))
            layer_norm(r16, None, None, out_dram=o_x2, tc0=tc)

        S_.replay(nc)
    return nc


def host_prep_c(inputs, res_a, res_b):
    wo = np.asarray(inputs["L0_attn_wo"], np.float32)
    bias = np.asarray(inputs["L0_ffn_bias"], np.float32)
    x = np.asarray(inputs["L0_attn_x"], np.float32)
    wdq_full = np.concatenate([res_a[c]["o_wdq"] for c in range(8)], axis=1)
    y_fm = []
    for b in range(2):
        ytm = np.concatenate([res_b[4 * b + i]["o_y"] for i in range(4)], axis=1)
        y_fm.append(np.ascontiguousarray(ytm.T))
    woT = _t(wo)
    bias_t = np.ascontiguousarray(bias.reshape(32, 128).T)
    maps = []
    for c in range(NCORES):
        b, t0 = c // 4, (c % 4) * T
        maps.append({
            "yT": np.ascontiguousarray(y_fm[b][:, t0:t0 + T]),
            "xT": np.ascontiguousarray(x[b, t0:t0 + T, :].T),
            "woT": woT, "wdq": wdq_full, "bias_t": bias_t,
        })
    return maps



def build_launch_d():
    nc = bass.Bass("TRN2", target_bir_lowering=False, debug=False)
    dr = lambda n, s, d, k: nc.dram_tensor(n, s, d, kind=k).ap()

    x2T = dr("x2T", [D, T], F32, "ExternalInput")
    WqT = dr("WqT", [D, D], F32, "ExternalInput")
    WkT = dr("WkT", [D, 1024], F32, "ExternalInput")
    WvT = dr("WvT", [D, 1024], F32, "ExternalInput")
    WinT = dr("WinT", [D, 32], F32, "ExternalInput")
    dtb = dr("dtb", [32, 1], F32, "ExternalInput")
    alog = dr("alog", [32, 1], F32, "ExternalInput")
    cos1 = dr("cos1", [64, T], F32, "ExternalInput")
    sin1 = dr("sin1", [64, T], F32, "ExternalInput")

    o_qr = dr("o_qr", [D, T], BF16, "ExternalOutput")
    o_kr = dr("o_kr", [1024, T], BF16, "ExternalOutput")
    o_v = dr("o_v", [1024, T], BF16, "ExternalOutput")
    o_dt = dr("o_dt", [32, T], F32, "ExternalOutput")
    o_g = dr("o_g", [32, T], F32, "ExternalOutput")

    S_ = Sched()
    with contextlib.ExitStack() as st:
        sb = lambda n, s, d: st.enter_context(nc.sbuf_tensor(n, s, d))
        ps = lambda n, s, d: st.enter_context(nc.psum_tensor(n, s, d))

        xb = sb("xb", [128, 32 * T], BF16)
        wcol = sb("wcol", [128, 32 * 128], BF16)
        qf = sb("qf", [128, T], F32)
        xh0 = sb("xh0", [64, T], F32)
        pr1 = sb("pr1", [64, T], F32)
        pr2 = sb("pr2", [64, T], F32)
        ro1 = sb("ro1", [64, T], BF16)
        ro2 = sb("ro2", [64, T], BF16)
        trig_c = sb("trig_c", [64, T], F32)
        trig_s = sb("trig_s", [64, T], F32)
        otile = sb("otile", [128, T], BF16)
        dtb_s = sb("dtb_s", [32, 1], F32)
        alog_s = sb("alog_s", [32, 1], F32)
        a1 = sb("a1", [32, T], F32)
        l1 = sb("l1", [32, T], F32)
        r1v = sb("r1v", [32, T], F32)
        dt_s = sb("dt_s", [32, T], F32)
        g_s = sb("g_s", [32, T], F32)
        nega = sb("nega", [32, 1], F32)
        c_one = sb("c_one", [32, 1], F32)

        pA = ps("pA", [128, T], F32)
        pD = ps("pD", [32, T], F32)

        for t in range(32):
            S_.dma(xb[:, t * T:(t + 1) * T], x2T[t * 128:(t + 1) * 128, :], cast=True)
        S_.dma(trig_c[:], cos1)
        S_.dma(trig_s[:], sin1)
        S_.dma(dtb_s[:], dtb)
        S_.dma(alog_s[:], alog)
        S_.memset(c_one[:], 1.0)

        def wload(dst, wT_dram, ocol0, owidth, nk, cast=True):
            S_.dma(dst[:, 0:nk * owidth].rearrange("p (k o) -> p k o", k=nk),
                   wT_dram[:, ocol0:ocol0 + owidth].rearrange("(k p) o -> p k o", p=128),
                   cast=cast)

        def proj_tile(psum, wT_dram, ocol0, owidth):
            wload(wcol, wT_dram, ocol0, owidth, 32)
            for k in range(32):
                l = wcol[0:128, k * owidth:(k + 1) * owidth]
                for h in range(0, T, 512):
                    S_.mm(psum[0:owidth, h:h + 512], l, xb[:, k * T + h:k * T + h + 512],
                          start=(k == 0), stop=(k == 31))

        def rope1(dst_dram, row0):
            S_.vcopy(xh0[:], qf[64:128, :])
            S_.tt(pr1[:], qf[0:64, :], trig_c[:], ALU.mult)
            S_.tt(pr2[:], xh0[:], trig_s[:], ALU.mult)
            S_.tt(pr1[:], pr1[:], pr2[:], ALU.subtract)
            S_.vcopy(ro1[:], pr1[:])
            S_.tt(pr1[:], xh0[:], trig_c[:], ALU.mult)
            S_.tt(pr2[:], qf[0:64, :], trig_s[:], ALU.mult)
            S_.tt(pr1[:], pr1[:], pr2[:], ALU.add)
            S_.vcopy(ro2[:], pr1[:])
            S_.dma(dst_dram[row0:row0 + 64, :], ro1[:])
            S_.dma(dst_dram[row0 + 64:row0 + 128, :], ro2[:])

        for hh in range(32):
            proj_tile(pA, WqT, hh * 128, 128)
            S_.vcopy(qf[:], pA[:, :])
            rope1(o_qr, hh * 128)
        for hh in range(8):
            proj_tile(pA, WkT, hh * 128, 128)
            S_.vcopy(qf[:], pA[:, :])
            rope1(o_kr, hh * 128)
        for hh in range(8):
            proj_tile(pA, WvT, hh * 128, 128)
            S_.vcopy(otile[:], pA[:, :])
            S_.dma(o_v[hh * 128:(hh + 1) * 128, :], otile[:])
        proj_tile(pD, WinT, 0, 32)
        S_.act(a1[:], pD[0:32, :], AF.Abs, bias=dtb_s[:, 0:1])
        S_.act(a1[:], a1[:], AF.Exp, scale=-1.0)
        S_.act(l1[:], a1[:], AF.Ln, bias=c_one[:, 0:1])
        S_.act(r1v[:], pD[0:32, :], AF.Relu, bias=dtb_s[:, 0:1])
        S_.tt(dt_s[:], l1[:], r1v[:], ALU.add)
        S_.dma(o_dt, dt_s[:])
        S_.act(nega[:], alog_s[:], AF.Exp)
        S_.ts(nega[:], nega[:], -1.0, None, ALU.mult)
        S_.ts(g_s[:], dt_s[:], nega[:, 0:1], None, ALU.mult)
        S_.dma(o_g, g_s[:])

        S_.replay(nc)
    return nc


def host_prep_d(inputs, x2_by_core):
    Wq = np.asarray(inputs["L1_attn_Wq"], np.float32)
    Wk = np.asarray(inputs["L1_attn_Wk"], np.float32)
    Wv = np.asarray(inputs["L1_attn_Wv"], np.float32)
    Win = np.asarray(inputs["L1_attn_Win"], np.float32)
    dtb = np.asarray(inputs["L1_attn_dt_bias"], np.float32).reshape(32, 1)
    alog = np.asarray(inputs["L1_attn_A_log_bias"], np.float32).reshape(32, 1)
    WqT, WkT, WvT, WinT = _t(Wq), _t(Wk), _t(Wv), _t(Win)
    maps = []
    for c in range(NCORES):
        t0 = (c % 4) * T
        cos1, sin1 = _rope_tabs_l1(np.arange(t0, t0 + T))
        maps.append({
            "x2T": x2_by_core[c], "WqT": WqT, "WkT": WkT, "WvT": WvT, "WinT": WinT,
            "dtb": dtb, "alog": alog, "cos1": cos1, "sin1": sin1,
        })
    return maps



def build_launch_e():
    nc = bass.Bass("TRN2", target_bir_lowering=False, debug=False)
    dr = lambda n, s, d, k: nc.dram_tensor(n, s, d, kind=k).ap()

    q_h = dr("q_h", [1024, S], BF16, "ExternalInput")
    k_h = dr("k_h", [256, S], BF16, "ExternalInput")
    v_h = dr("v_h", [256, S], BF16, "ExternalInput")
    dt_tm = dr("dt_tm", [S, 8], F32, "ExternalInput")
    g_ch = dr("g_ch", [128, 256], F32, "ExternalInput")
    gnt = dr("gnt", [128, 128], F32, "ExternalInput")
    identf_in = dr("identf_in", [128, 128], F32, "ExternalInput")
    triuf_in = dr("triuf_in", [128, 128], F32, "ExternalInput")
    masknegD_in = dr("masknegD_in", [128, 128], F32, "ExternalInput")

    o_y1 = dr("o_y1", [S, 1024], BF16, "ExternalOutput")
    grows = nc.dram_tensor("grows", [256, 128], F32, kind="Internal")

    S_ = Sched()
    with contextlib.ExitStack() as st:
        sb = lambda n, s, d: st.enter_context(nc.sbuf_tensor(n, s, d))
        ps = lambda n, s, d: st.enter_context(nc.psum_tensor(n, s, d))

        qh_s = sb("qh_s", [128, S], BF16)
        k_s = sb("k_s", [128, 2 * S], BF16)
        v_s = sb("v_s", [128, 2 * S], BF16)
        dt_s = sb("dt_s", [128, 32 * 8], F32)
        g_s = sb("g_s", [128, 256], F32)
        G_all = sb("G_all", [128, 256], F32)
        negG = sb("negG", [128, 256], F32)
        bGl = sb("bGl", [128, 256], F32)
        GT_s = sb("GT_s", [128, 256], F32)
        Gl_row = sb("Gl_row", [1, 256], F32)
        grow_c = sb("grow_c", [1, 4096], F32)
        onecol = sb("onecol", [128, 1], F32)
        onerow = sb("onerow", [1, 128], F32)
        S_a = sb("S_a", [128, 128], F32)
        S_b = sb("S_b", [128, 128], F32)
        St_bf = sb("St_bf", [128, 128], BF16)
        kt_tm = sb("kt_tm", [128, 128], BF16)
        vt_tm = sb("vt_tm", [128, 128], BF16)
        xc_tm = sb("xc_tm", [128, 128], BF16)
        kdec = sb("kdec", [128, 128], BF16)
        A_T = sb("A_T", [128, 128], BF16)
        Dm = sb("Dm", [128, 128], F32)
        t1 = sb("t1", [128, 128], F32)
        t2 = sb("t2", [128, 128], F32)
        y_s = sb("y_s", [128, 128], F32)
        y_n = sb("y_n", [128, 128], BF16)
        jnk = sb("jnk", [128, 128], F32)
        eG = sb("eG", [128, 1], F32)
        edec = sb("edec", [128, 1], F32)
        cd = sb("cd", [128, 1], F32)
        ss_col = sb("ss_col", [128, 1], F32)
        rstd_c = sb("rstd_c", [128, 1], F32)
        identf = sb("identf", [128, 128], F32)
        ident_b = sb("ident_b", [128, 128], BF16)
        triuf = sb("triuf", [128, 128], F32)
        masknegD = sb("masknegD", [128, 128], F32)
        gnorm_t = sb("gnorm_t", [128, 128], F32)
        c_eps5 = sb("c_eps5", [128, 1], F32)

        pS = ps("pS", [128, 128], F32)
        pK1 = ps("pK1", [128, 256], F32)
        pC = ps("pC", [128, 128], F32)
        pD = ps("pD", [128, 128], F32)
        pE = ps("pE", [128, 128], F32)
        pT = ps("pT", [128, 256], BF16)
        pRow = ps("pRow", [1, 256], F32)
        pG = ps("pG", [128, 256], F32)

        for t in range(2):
            S_.dma(k_s[:, t * S:(t + 1) * S], k_h[t * 128:(t + 1) * 128, :])
            S_.dma(v_s[:, t * S:(t + 1) * S], v_h[t * 128:(t + 1) * 128, :])
        S_.dma(dt_s[:, :].rearrange("p (c h) -> p c h", c=32),
               dt_tm.rearrange("(c p) h -> p c h", p=128))
        S_.dma(g_s[:], g_ch)
        S_.dma(identf[:], identf_in)
        S_.dma(ident_b[:], identf_in, cast=True)
        S_.dma(triuf[:], triuf_in)
        S_.dma(masknegD[:], masknegD_in)
        S_.dma(gnorm_t[:], gnt)
        S_.memset(onecol[:], 1.0)
        S_.memset(onerow[:], 1.0)
        S_.memset(c_eps5[:], 1e-5)

        S_.mm(pG[:, :], triuf[:], g_s[:])
        S_.vcopy(G_all[:], pG[:, :])
        S_.ts(negG[:], G_all[:], -1.0, None, ALU.mult)
        S_.mm(pRow[0:1, :], onecol[:], g_s[:])
        S_.vcopy(Gl_row[:], pRow[0:1, :])
        S_.mm(pK1[:, :], onerow[:], Gl_row[0:1, :])
        S_.vcopy(bGl[:], pK1[:, :])
        S_.tr(pG[:, 0:128], G_all[:, 0:128], identf[:])
        S_.tr(pG[:, 128:256], G_all[:, 128:256], identf[:])
        S_.vcopy(GT_s[:], pG[:, :])
        S_.dma(grows.ap()[0:128, :], GT_s[:, 0:128])
        S_.dma(grows.ap()[128:256, :], GT_s[:, 128:256])

        grows_flat = grows.ap().rearrange("u w -> (u w)")

        for hl in range(8):
            lkv = hl // 4
            S_.dma(qh_s[:, :], q_h[hl * 128:(hl + 1) * 128, :])
            S_.dma(grow_c[0:1, :], grows_flat[None, hl * 4096:(hl + 1) * 4096])
            S_.memset(S_a[:], 0.0)
            S_.memset(St_bf[:], 0.0)
            for c in range(32):
                u = hl * 32 + c
                G_col = G_all[:, u:u + 1]
                negG_col = negG[:, u:u + 1]
                bGl_col = bGl[:, u:u + 1]
                Sc, Sn = (S_a, S_b) if c % 2 == 0 else (S_b, S_a)
                S_.tr(pT[:, 0:128], k_s[:, lkv * S + c * 128:lkv * S + (c + 1) * 128], ident_b[:])
                S_.tr(pT[:, 128:256], v_s[:, lkv * S + c * 128:lkv * S + (c + 1) * 128], ident_b[:])
                S_.vcopy(kt_tm[:], pT[:, 0:128])
                S_.vcopy(vt_tm[:], pT[:, 128:256])
                S_.act(eG[:], G_col, AF.Exp)
                S_.act(edec[:], G_col, AF.Exp, scale=-1.0, bias=bGl_col)
                S_.act(cd[:], bGl_col, AF.Exp)
                S_.mm(pK1[:, 0:128], onerow[:], grow_c[0:1, c * 128:(c + 1) * 128])
                S_.tt(t1[:], pK1[:, 0:128], masknegD[:], ALU.add)
                S_.act(Dm[:], t1[:], AF.Exp, bias=negG_col)
                S_.mm(pS[:, :], k_s[:, lkv * S + c * 128:lkv * S + (c + 1) * 128],
                      qh_s[:, c * 128:(c + 1) * 128])
                S_.tt(A_T[:], pS[:, :], Dm[:], ALU.mult)
                S_.ts(xc_tm[:], vt_tm[:], dt_s[:, c * 8 + hl:c * 8 + hl + 1], None, ALU.mult)
                S_.ts(kdec[:], kt_tm[:], edec[:, 0:1], None, ALU.mult)
                S_.mm(pC[:, :], kdec[:], xc_tm[:])
                S_.mm(pD[:, :], A_T[:], xc_tm[:])
                S_.mm(pE[:, :], qh_s[:, c * 128:(c + 1) * 128], St_bf[:])
                S_.ts(t2[:], pE[:, :], eG[:, 0:1], None, ALU.mult)
                S_.tt(y_s[:], pD[:, :], t2[:], ALU.add)
                S_.act(jnk[:], y_s[:], AF.Square, accum_out=ss_col[:, 0:1])
                S_.act(rstd_c[:], ss_col[:], AF.Ln, scale=1.0 / 128, bias=c_eps5[:, 0:1])
                S_.act(rstd_c[:], rstd_c[:], AF.Exp, scale=-0.5)
                S_.stt(y_n[:], y_s[:], rstd_c[:, 0:1], gnorm_t[:], ALU.mult, ALU.mult)
                S_.dma(o_y1[c * 128:(c + 1) * 128, hl * 128:(hl + 1) * 128], y_n[:])
                S_.ts(Sn[:], Sc[:], cd[:, 0:1], None, ALU.mult)
                S_.tt(Sn[:], Sn[:], pC[:, :], ALU.add)
                S_.vcopy(St_bf[:], Sn[:])

        S_.replay(nc)
    return nc


def host_prep_e(inputs, res_d):
    qr_full, kr_full, v_full, dt_full, g_full = [], [], [], [], []
    for b in range(2):
        qr_full.append(np.concatenate([res_d[4 * b + i]["o_qr"] for i in range(4)], axis=1))
        kr_full.append(np.concatenate([res_d[4 * b + i]["o_kr"] for i in range(4)], axis=1))
        v_full.append(np.concatenate([res_d[4 * b + i]["o_v"] for i in range(4)], axis=1))
        dt_full.append(np.concatenate([res_d[4 * b + i]["o_dt"] for i in range(4)], axis=1))
        g_full.append(np.concatenate([res_d[4 * b + i]["o_g"] for i in range(4)], axis=1))
    ident = np.eye(128, dtype=np.float32)
    triu = np.triu(np.ones((128, 128), np.float32))
    masknegD = np.where(np.arange(128)[None, :] >= np.arange(128)[:, None],
                        0.0, -1e30).astype(np.float32)
    gnw = np.asarray(inputs["L1_attn_gnorm_w"], np.float32)
    gnt = np.tile(gnw[None, :], (128, 1)).astype(np.float32)
    maps = []
    for c in range(NCORES):
        b, h0 = c // 4, 8 * (c % 4)
        kvh0 = h0 // 4
        dt_sl = dt_full[b][h0:h0 + 8, :]
        g_sl = g_full[b][h0:h0 + 8, :]
        g_ch = np.ascontiguousarray(
            g_sl.reshape(8, 32, 128).transpose(2, 0, 1).reshape(128, 256))
        maps.append({
            "q_h": np.ascontiguousarray(qr_full[b][h0 * 128:(h0 + 8) * 128, :]),
            "k_h": np.ascontiguousarray(kr_full[b][kvh0 * 128:(kvh0 + 2) * 128, :]),
            "v_h": np.ascontiguousarray(v_full[b][kvh0 * 128:(kvh0 + 2) * 128, :]),
            "dt_tm": np.ascontiguousarray(dt_sl.T),
            "g_ch": g_ch,
            "gnt": gnt, "identf_in": ident, "triuf_in": triu, "masknegD_in": masknegD,
        })
    return maps



def build_launch_f():
    nc = bass.Bass("TRN2", target_bir_lowering=False, debug=False)
    dr = lambda n, s, d, k: nc.dram_tensor(n, s, d, kind=k).ap()

    y1T = dr("y1T", [D, T], BF16, "ExternalInput")
    x2T = dr("x2T", [D, T], F32, "ExternalInput")
    WgT = dr("WgT", [D, D], F32, "ExternalInput")
    WoT = dr("WoT", [D, D], F32, "ExternalInput")
    baseT = dr("baseT", [D, D], F32, "ExternalInput")
    gateT = dr("gateT", [D, 28], F32, "ExternalInput")
    loraAT = dr("loraAT", [D, 224], F32, "ExternalInput")
    loraBf = dr("loraBf", [224, D], F32, "ExternalInput")
    R1_in = dr("R1_in", [28, 128], F32, "ExternalInput")
    R2_in = dr("R2_in", [28, 96], F32, "ExternalInput")
    identf_in = dr("identf_in", [128, 128], F32, "ExternalInput")
    o_x4 = dr("o_x4", [D, T], F32, "ExternalOutput")

    S_ = Sched()
    TC = 256
    with contextlib.ExitStack() as st:
        sb = lambda n, s, d: st.enter_context(nc.sbuf_tensor(n, s, d))
        ps = lambda n, s, d: st.enter_context(nc.psum_tensor(n, s, d))

        y1b = sb("y1b", [128, 32 * TC], BF16)
        x2b = sb("x2b", [128, 32 * TC], BF16)
        ygb = sb("ygb", [128, 32 * TC], BF16)
        wcol = sb("wcol", [128, 32 * 128], BF16)
        xf32 = sb("xf32", [128, TC], F32)
        r16 = sb("r16", [128, 32 * TC], F16)
        x3b = sb("x3b", [128, 32 * TC], BF16)
        x3h = sb("x3h", [128, 32 * TC], F16)
        sq = sb("sq", [128, TC], F16)
        sg = sb("sg", [128, TC], BF16)
        onecol = sb("onecol", [128, 1], F16)
        onerow = sb("onerow", [1, 128], F32)
        tmp32 = sb("tmp32", [128, TC], F32)
        rowa = sb("rowa", [1, TC], F32)
        rowb = sb("rowb", [1, TC], F32)
        rowc = sb("rowc", [1, TC], F32)
        m_bc = sb("m_bc", [128, TC], F32)
        rstd_bc = sb("rstd_bc", [128, TC], F32)
        c_eps5 = sb("c_eps5", [1, 1], F32)
        lg_f = sb("lg_f", [28, TC], F32)
        lt = sb("lt", [128, 28], F32)
        msk1 = sb("msk1", [128, 28], F32)
        msk2 = sb("msk2", [128, 28], F32)
        l2t = sb("l2t", [128, 28], F32)
        tb = sb("tb", [128, 28], F32)
        ga_t = sb("ga_t", [128, 28], F32)
        gb_t = sb("gb_t", [128, 28], F32)
        gtm = sb("gtm", [128, 28], F32)
        m1c = sb("m1c", [128, 1], F32)
        m2c = sb("m2c", [128, 1], F32)
        dgc = sb("dgc", [128, 1], F32)
        p1c = sb("p1c", [128, 1], F32)
        p2c = sb("p2c", [128, 1], F32)
        gate_fm = sb("gate_fm", [28, TC], F32)
        gsc1 = sb("gsc1", [128, TC], F32)
        gsc2 = sb("gsc2", [96, TC], F32)
        hg1 = sb("hg1", [128, TC], BF16)
        hg2 = sb("hg2", [96, TC], BF16)
        lb1 = sb("lb1", [128, 128], BF16)
        lb2 = sb("lb2", [96, 128], BF16)
        identf = sb("identf", [128, 128], F32)
        R1s = sb("R1s", [28, 128], F32)
        R2s = sb("R2s", [28, 96], F32)

        pA = ps("pA", [128, TC], F32)
        pR = ps("pR", [1, 2 * TC], F32)
        pB = ps("pB", [128, TC], F32)
        pL = ps("pL", [28, TC], F32)
        pTr = ps("pTr", [128, 128], F32)
        pH = ps("pH", [128, TC], F32)

        S_.memset(onecol[:], 1.0)
        S_.memset(onerow[:], 1.0)
        S_.memset(c_eps5[:], 1e-5)
        S_.dma(identf[:], identf_in)
        S_.dma(R1s[:], R1_in)
        S_.dma(R2s[:], R2_in)

        def wload(dst, wT_dram, ocol0, owidth, nk, cast=True):
            S_.dma(dst[:, 0:nk * owidth].rearrange("p (k o) -> p k o", k=nk),
                   wT_dram[:, ocol0:ocol0 + owidth].rearrange("(k p) o -> p k o", p=128),
                   cast=cast)

        def layer_norm(src16, dst_bf, dst_f16, out_dram=None, tc0=0):
            S_.act(rowa[:], pR[0:1, 0:TC], AF.Copy, scale=1.0 / D)
            S_.act(rowb[:], pR[0:1, TC:2 * TC], AF.Copy, scale=1.0 / D)
            S_.tt(rowc[:], rowa[:], rowa[:], ALU.mult)
            S_.tt(rowb[:], rowb[:], rowc[:], ALU.subtract)
            S_.act(rowb[:], rowb[:], AF.Ln, bias=c_eps5[0:1, 0:1])
            S_.act(rowb[:], rowb[:], AF.Exp, scale=-0.5)
            S_.mm(pB[:, :], onerow[:], rowa[0:1, :])
            S_.vcopy(m_bc[:], pB[:, :])
            S_.mm(pB[:, :], onerow[:], rowb[0:1, :])
            S_.vcopy(rstd_bc[:], pB[:, :])
            for ot in range(32):
                sl = src16[:, ot * TC:(ot + 1) * TC]
                S_.tt(tmp32[:], sl, m_bc[:], ALU.subtract)
                if dst_bf is not None:
                    S_.tt(dst_bf[:, ot * TC:(ot + 1) * TC], tmp32[:], rstd_bc[:], ALU.mult)
                if dst_f16 is not None:
                    S_.tt(dst_f16[:, ot * TC:(ot + 1) * TC], tmp32[:], rstd_bc[:], ALU.mult)
                if out_dram is not None:
                    S_.tt(tmp32[:], tmp32[:], rstd_bc[:], ALU.mult)
                    S_.dma(out_dram[ot * 128:(ot + 1) * 128, tc0:tc0 + TC], tmp32[:])

        for tc in range(0, T, TC):
            for t in range(32):
                S_.dma(y1b[:, t * TC:(t + 1) * TC], y1T[t * 128:(t + 1) * 128, tc:tc + TC])
                S_.dma(x2b[:, t * TC:(t + 1) * TC], x2T[t * 128:(t + 1) * 128, tc:tc + TC],
                       cast=True)
            for ot in range(32):
                wload(wcol, WgT, ot * 128, 128, 32, True)
                for k in range(32):
                    S_.mm(pA[:, :], wcol[:, k * 128:(k + 1) * 128], x2b[:, k * TC:(k + 1) * TC],
                          start=(k == 0), stop=(k == 31))
                S_.act(sg[:], pA[:, :], AF.Silu)
                S_.tt(ygb[:, ot * TC:(ot + 1) * TC], y1b[:, ot * TC:(ot + 1) * TC],
                      sg[:], ALU.mult)
            for ot in range(32):
                wload(wcol, WoT, ot * 128, 128, 32, True)
                for k in range(32):
                    S_.mm(pA[:, :], wcol[:, k * 128:(k + 1) * 128], ygb[:, k * TC:(k + 1) * TC],
                          start=(k == 0), stop=(k == 31))
                S_.dma(xf32[:], x2T[ot * 128:(ot + 1) * 128, tc:tc + TC])
                S_.tt(r16[:, ot * TC:(ot + 1) * TC], pA[:, :], xf32[:], ALU.add)
                S_.act(sq[:], r16[:, ot * TC:(ot + 1) * TC], AF.Square)
                S_.mm(pR[0:1, 0:TC], onecol[:], r16[:, ot * TC:(ot + 1) * TC],
                      start=(ot == 0), stop=(ot == 31))
                S_.mm(pR[0:1, TC:2 * TC], onecol[:], sq[:], start=(ot == 0), stop=(ot == 31))
            layer_norm(r16, x3b, x3h)
            wload(wcol, gateT, 0, 28, 32, True)
            for k in range(32):
                S_.mm(pL[0:28, :], wcol[:, k * 28:(k + 1) * 28], x3b[:, k * TC:(k + 1) * TC],
                      start=(k == 0), stop=(k == 31))
            S_.vcopy(lg_f[:], pL[0:28, :])
            for j in range(TC // 128):
                S_.tr(pTr[0:128, 0:28], lg_f[0:28, j * 128:(j + 1) * 128], identf[0:28, 0:28])
                S_.vcopy(lt[:], pTr[0:128, 0:28])
                S_.rmax(m1c[:], lt[:])
                S_.ts(msk1[:], lt[:], m1c[:, 0:1], None, ALU.is_equal)
                S_.ts(tb[:], msk1[:], 1e30, None, ALU.mult)
                S_.tt(l2t[:], lt[:], tb[:], ALU.subtract)
                S_.rmax(m2c[:], l2t[:])
                S_.tt(dgc[:], m1c[:], m2c[:], ALU.subtract)
                S_.act(p1c[:], dgc[:], AF.Sigmoid)
                S_.ts(p2c[:], p1c[:], -1.0, 1.0, ALU.mult, ALU.add)
                S_.ts(msk2[:], l2t[:], m2c[:, 0:1], None, ALU.is_equal)
                S_.ts(ga_t[:], msk1[:], p1c[:, 0:1], None, ALU.mult)
                S_.ts(gb_t[:], msk2[:], p2c[:, 0:1], None, ALU.mult)
                S_.tt(gtm[:], ga_t[:], gb_t[:], ALU.add)
                S_.tr(pTr[0:28, 0:128], gtm[0:128, 0:28], identf[:])
                S_.vcopy(gate_fm[0:28, j * 128:(j + 1) * 128], pTr[0:28, 0:128])
            S_.mm(pB[0:128, :], R1s[:], gate_fm[0:28, :])
            S_.vcopy(gsc1[:], pB[0:128, :])
            S_.mm(pB[0:96, :], R2s[:], gate_fm[0:28, :])
            S_.vcopy(gsc2[:], pB[0:96, :])
            wload(wcol, loraAT, 0, 128, 32, True)
            for k in range(32):
                S_.mm(pH[:, :], wcol[:, k * 128:(k + 1) * 128], x3b[:, k * TC:(k + 1) * TC],
                      start=(k == 0), stop=(k == 31))
            S_.stt(hg1[:], pH[:, :], 2.0, gsc1[:], ALU.mult, ALU.mult)
            wload(wcol, loraAT, 128, 96, 32, True)
            for k in range(32):
                S_.mm(pH[0:96, :], wcol[:, k * 96:(k + 1) * 96], x3b[:, k * TC:(k + 1) * TC],
                      start=(k == 0), stop=(k == 31))
            S_.stt(hg2[:], pH[0:96, :], 2.0, gsc2[:], ALU.mult, ALU.mult)
            for ot in range(32):
                wload(wcol, baseT, ot * 128, 128, 32, True)
                for k in range(32):
                    S_.mm(pA[:, :], wcol[:, k * 128:(k + 1) * 128], x3b[:, k * TC:(k + 1) * TC],
                          start=(k == 0), stop=False)
                S_.dma(lb1[:], loraBf[0:128, ot * 128:(ot + 1) * 128], cast=True)
                S_.dma(lb2[:], loraBf[128:224, ot * 128:(ot + 1) * 128], cast=True)
                S_.mm(pA[:, :], lb1[:], hg1[:], start=False, stop=False)
                S_.mm(pA[:, :], lb2[0:96, :], hg2[0:96, :], start=False, stop=True)
                S_.tt(r16[:, ot * TC:(ot + 1) * TC], pA[:, :], x3h[:, ot * TC:(ot + 1) * TC],
                      ALU.add)
                S_.act(sq[:], r16[:, ot * TC:(ot + 1) * TC], AF.Square)
                S_.mm(pR[0:1, 0:TC], onecol[:], r16[:, ot * TC:(ot + 1) * TC],
                      start=(ot == 0), stop=(ot == 31))
                S_.mm(pR[0:1, TC:2 * TC], onecol[:], sq[:], start=(ot == 0), stop=(ot == 31))
            layer_norm(r16, None, None, out_dram=o_x4, tc0=tc)

        S_.replay(nc)
    return nc


def host_prep_f(inputs, res_e, x2_by_core):
    Wg = np.asarray(inputs["L1_attn_Wg"], np.float32)
    Wo = np.asarray(inputs["L1_attn_Wo"], np.float32)
    baseW = np.asarray(inputs["L1_ffn_base_W"], np.float32)
    gateW = np.asarray(inputs["L1_ffn_gate_W"], np.float32)
    loraA = np.asarray(inputs["L1_ffn_lora_A"], np.float32)
    loraB = np.asarray(inputs["L1_ffn_lora_B"], np.float32)
    WgT, WoT, baseT, gateT = _t(Wg), _t(Wo), _t(baseW), _t(gateW)
    loraAT = _t(loraA.reshape(224, 4096))
    loraBf = np.ascontiguousarray(loraB.transpose(0, 2, 1).reshape(224, 4096))
    R = np.zeros((28, 224), np.float32)
    for e in range(28):
        R[e, 8 * e:8 * e + 8] = 1.0
    ident = np.eye(128, dtype=np.float32)
    y1_fm = []
    for b in range(2):
        ytm = np.concatenate([res_e[4 * b + i]["o_y1"] for i in range(4)], axis=1)
        y1_fm.append(np.ascontiguousarray(ytm.T))
    maps = []
    for c in range(NCORES):
        b, t0 = c // 4, (c % 4) * T
        maps.append({
            "y1T": np.ascontiguousarray(y1_fm[b][:, t0:t0 + T]),
            "x2T": x2_by_core[c],
            "WgT": WgT, "WoT": WoT, "baseT": baseT, "gateT": gateT,
            "loraAT": loraAT, "loraBf": loraBf,
            "R1_in": np.ascontiguousarray(R[:, 0:128]),
            "R2_in": np.ascontiguousarray(R[:, 128:224]),
            "identf_in": ident,
        })
    return maps




IOA = bass.IndirectOffsetOnAxis
GROUPS4 = [[0, 1, 2, 3], [4, 5, 6, 7]]
GROUPS8 = [[0, 1, 2, 3, 4, 5, 6, 7]]


def _flat(dram_handle):
    return dram_handle.ap().rearrange("r c -> (r c)")[:, None]


_PHASE_LIMIT = 99
_F_TC_LIMIT = 99
_F_SEC = 99
_SERIALIZE = 'dveact'


def build_fused():
    nc = bass.Bass("TRN2", target_bir_lowering=False, debug=False)
    dr = lambda n, s, d, k: nc.dram_tensor(n, s, d, kind=k).ap()
    dri = lambda n, s, d: nc.dram_tensor(n, s, d, kind="Internal")

    xT = dr("xT", [D, T], F32, "ExternalInput")
    wqaT = dr("wqaT", [D, 1536], F32, "ExternalInput")
    wqbnT = dr("wqbnT", [1536, 4096], F32, "ExternalInput")
    wqbpT = dr("wqbpT", [1536, 2048], F32, "ExternalInput")
    qnw = dr("qnw", [128, 12], F32, "ExternalInput")
    wkvaT = dr("wkvaT", [D, 576], F32, "ExternalInput")
    kvnw = dr("kvnw", [128, 4], F32, "ExternalInput")
    cosp = dr("cosp", [32, T], F32, "ExternalInput")
    sinp = dr("sinp", [32, T], F32, "ExternalInput")
    qwT = dr("qwT", [512, 512], I32, "ExternalInput")
    scT = dr("scT", [512, 32], F32, "ExternalInput")
    zrT = dr("zrT", [512, 32], F32, "ExternalInput")
    identf_in = dr("identf_in", [128, 128], F32, "ExternalInput")
    maskneg_in = dr("maskneg_in", [128, 128], F32, "ExternalInput")
    wkvkT = dr("wkvkT", [512, 1024], F32, "ExternalInput")
    wkvvT = dr("wkvvT", [512, 1024], F32, "ExternalInput")
    woT = dr("woT", [D, D], F32, "ExternalInput")
    bias_t = dr("bias_t", [128, 32], F32, "ExternalInput")
    WqT = dr("WqT", [D, D], F32, "ExternalInput")
    WkT = dr("WkT", [D, 1024], F32, "ExternalInput")
    WvT = dr("WvT", [D, 1024], F32, "ExternalInput")
    WinT = dr("WinT", [D, 32], F32, "ExternalInput")
    dtb = dr("dtb", [32, 1], F32, "ExternalInput")
    alog = dr("alog", [32, 1], F32, "ExternalInput")
    cos1 = dr("cos1", [64, T], F32, "ExternalInput")
    sin1 = dr("sin1", [64, T], F32, "ExternalInput")
    gnt = dr("gnt", [128, 128], F32, "ExternalInput")
    triuf_in = dr("triuf_in", [128, 128], F32, "ExternalInput")
    masknegD_in = dr("masknegD_in", [128, 128], F32, "ExternalInput")
    WgT = dr("WgT", [D, D], F32, "ExternalInput")
    WoT1 = dr("WoT1", [D, D], F32, "ExternalInput")
    baseT = dr("baseT", [D, D], F32, "ExternalInput")
    gateT = dr("gateT", [D, 28], F32, "ExternalInput")
    loraAT = dr("loraAT", [D, 224], F32, "ExternalInput")
    loraBf = dr("loraBf", [224, D], F32, "ExternalInput")
    R1_in = dr("R1_in", [28, 128], F32, "ExternalInput")
    R2_in = dr("R2_in", [28, 96], F32, "ExternalInput")
    idxB_qn = dr("idxB_qn", [128, 32], I32, "ExternalInput")
    idxB_qp = dr("idxB_qp", [64, 32], I32, "ExternalInput")
    idxY = dr("idxY", [128, 128], I32, "ExternalInput")
    idxE_kv = dr("idxE_kv", [128, 8], I32, "ExternalInput")
    idxE_dtg = dr("idxE_dtg", [128, 32], I32, "ExternalInput")

    o_x4 = dr("o_x4", [D, T], F32, "ExternalOutput")

    qn_l = dri("qn_l", [4096, T], BF16); qn_g = dri("qn_g", [4 * 4096, T], BF16)
    qp_l = dri("qp_l", [2048, T], BF16); qp_g = dri("qp_g", [4 * 2048, T], BF16)
    ckv_l = dri("ckv_l", [512, T], BF16); ckv_g = dri("ckv_g", [4 * 512, T], BF16)
    kpe_l = dri("kpe_l", [64, T], BF16); kpe_g = dri("kpe_g", [4 * 64, T], BF16)
    wdq_l = dri("wdq_l", [4096, 512], BF16); wdq_g = dri("wdq_g", [8 * 4096, 512], BF16)
    y_l = dri("y_l", [4096, T], BF16); y_g = dri("y_g", [4 * 4096, T], BF16)
    x2_l = dri("x2_l", [D, T], F32)
    qr_l = dri("qr_l", [4096, T], BF16); qr_g = dri("qr_g", [4 * 4096, T], BF16)
    kr_l = dri("kr_l", [1024, T], BF16); kr_g = dri("kr_g", [4 * 1024, T], BF16)
    v_l = dri("v_l", [1024, T], BF16); v_g = dri("v_g", [4 * 1024, T], BF16)
    dtg_l = dri("dtg_l", [4096, 16], F32); dtg_g = dri("dtg_g", [4 * 4096, 16], F32)
    y1_l = dri("y1_l", [4096, T], BF16); y1_g = dri("y1_g", [4 * 4096, T], BF16)
    grows = dri("grows", [256, 128], F32)

    S_ = Sched()
    gst = contextlib.ExitStack()
    with gst:
        gsb = lambda n, s, d: gst.enter_context(nc.sbuf_tensor(n, s, d))
        identf = gsb("identf", [128, 128], F32)
        ident_b = gsb("ident_b", [128, 128], BF16)
        xB_qn = gsb("xB_qn", [128, 32], I32)
        xB_qp = gsb("xB_qp", [64, 32], I32)
        xY = gsb("xY", [128, 128], I32)
        xE_kv = gsb("xE_kv", [128, 8], I32)
        xE_dtg = gsb("xE_dtg", [128, 32], I32)

        S_.dma(identf[:], identf_in)
        S_.dma(ident_b[:], identf_in, cast=True)
        S_.dma(xB_qn[:], idxB_qn)
        S_.dma(xB_qp[:], idxB_qp)
        S_.dma(xY[:], idxY)
        S_.dma(xE_kv[:], idxE_kv)
        S_.dma(xE_dtg[:], idxE_dtg)

        def coll(kind, groups, in_h, out_h):
            S_._op('gpsimd', 1,
                   lambda e, k=kind, g=groups, i=in_h, o=out_h:
                   e.collective_compute(k, ALU.bypass, replica_groups=g,
                                        ins=[i.ap().opt()], outs=[o.ap().opt()]),
                   reads=[in_h.ap()], writes=[out_h.ap()], is_coll=True)

        def idma(out_ap, flat_src, idx_col):
            S_._op('gpsimd', 16,
                   lambda e, o=out_ap, s=flat_src, x=idx_col:
                   e.indirect_dma_start(o, None, s, IOA(x, 0)),
                   reads=[flat_src, idx_col], writes=[out_ap])

        with contextlib.ExitStack() as st:
            sb = lambda n, s, d: st.enter_context(nc.sbuf_tensor(n, s, d))
            ps = lambda n, s, d: st.enter_context(nc.psum_tensor(n, s, d))
            x_bf = sb("a_x_bf", [128, 32 * T], BF16)
            wcol = sb("a_wcol", [128, 2 * 32 * 128], BF16)
            wsel = [0]
            qlr = sb("a_qlr", [128, 12 * T], BF16)
            qln = sb("a_qln", [128, 12 * T], BF16)
            kvr = sb("a_kvr", [128, 5 * T], BF16)
            sq = sb("a_sq", [128, T], F16)
            row = sb("a_row", [1, T], F32)
            bc = sb("a_bc", [128, T], F32)
            onecol = sb("a_onecol", [128, 1], F16)
            onerow = sb("a_onerow", [1, 128], F32)
            c_eps6 = sb("a_c_eps6", [128, 1], F32)
            qnw_s = sb("a_qnw_s", [128, 12], F32)
            kvnw_s = sb("a_kvnw_s", [128, 4], F32)
            trig = sb("a_trig", [64, 2 * T], F32)
            rp = [sb(f"a_rp{i}", [32, T], F16) for i in range(4)]
            ro = [sb(f"a_ro{i}", [32, T], BF16) for i in range(2)]
            x2c = sb("a_x2c", [32, T], F16)
            otile = sb("a_otile", [128, T], BF16)
            qw_s = sb("a_qw_s", [128, 4 * 512], I32)
            qtmp = sb("a_qtmp", [128, 512], I32)
            wun = sb("a_wun", [128, 4096], F16)
            waf = sb("a_waf", [128, 4096], BF16)
            sc_s = sb("a_sc_s", [128, 4 * 32], F32)
            zr_s = sb("a_zr_s", [128, 4 * 32], F32)
            wtr = sb("a_wtr", [128, 512], BF16)
            pQ = ps("a_pQ", [128, T], F32)
            pR = ps("a_pR", [1, T], F32)
            pB = ps("a_pB", [128, T], F32)
            pT = ps("a_pT", [128, 512], BF16)

            for t in range(32):
                S_.dma(x_bf[:, t * T:(t + 1) * T], xT[t * 128:(t + 1) * 128, :], cast=True)
            S_.dma(qnw_s[:], qnw)
            S_.dma(kvnw_s[:], kvnw)
            S_.dma(trig[0:32, 0:T], cosp)
            S_.dma(trig[0:32, T:2 * T], sinp)
            S_.memset(onecol[:], 1.0)
            S_.memset(onerow[:], 1.0)
            S_.memset(c_eps6[:], 1e-6)

            def mm_otile(psum, wT_dram, ocol0, owidth, rhs_sb, nk):
                b0 = wsel[0] * 4096
                wsel[0] ^= 1
                S_.dma(wcol[:, b0:b0 + nk * owidth].rearrange("p (k o) -> p k o", k=nk),
                       wT_dram[:, ocol0:ocol0 + owidth].rearrange("(k p) o -> p k o", p=128),
                       cast=True)
                for k in range(nk):
                    l = wcol[0:128, b0 + k * owidth:b0 + (k + 1) * owidth]
                    for h in range(0, T, 512):
                        S_.mm(psum[0:owidth, h:h + 512], l, rhs_sb[:, k * T + h:k * T + h + 512],
                              start=(k == 0), stop=(k == nk - 1))

            for ot in range(12):
                mm_otile(pQ, wqaT, ot * 128, 128, x_bf, 32)
                S_.vcopy(qlr[:, ot * T:(ot + 1) * T], pQ[:, :])
                S_.act(sq[:], pQ[:, :], AF.Square)
                for h in range(0, T, 512):
                    S_.mm(pR[0:1, h:h + 512], onecol[:], sq[:, h:h + 512],
                          start=(ot == 0), stop=(ot == 11))
            S_.act(row[:], pR[0:1, :], AF.Ln, bias=c_eps6[0:1, 0:1], scale=1.0 / 1536)
            S_.act(row[:], row[:], AF.Exp, scale=-0.5)
            for h in range(0, T, 512):
                S_.mm(pB[:, h:h + 512], onerow[:], row[0:1, h:h + 512])
            S_.vcopy(bc[:], pB[:, :])
            for ot in range(12):
                S_.stt(qln[:, ot * T:(ot + 1) * T], qlr[:, ot * T:(ot + 1) * T],
                       qnw_s[:, ot:ot + 1], bc[:], ALU.mult, ALU.mult)

            for ot in range(4):
                mm_otile(pQ, wkvaT, ot * 128, 128, x_bf, 32)
                S_.vcopy(kvr[:, ot * T:(ot + 1) * T], pQ[:, :])
                S_.act(sq[:], pQ[:, :], AF.Square)
                for h in range(0, T, 512):
                    S_.mm(pR[0:1, h:h + 512], onecol[:], sq[:, h:h + 512],
                          start=(ot == 0), stop=(ot == 3))
            mm_otile(pQ, wkvaT, 512, 64, x_bf, 32)
            S_.vcopy(kvr[0:64, 4 * T:5 * T], pQ[0:64, :])
            S_.act(row[:], pR[0:1, :], AF.Ln, bias=c_eps6[0:1, 0:1], scale=1.0 / 512)
            S_.act(row[:], row[:], AF.Exp, scale=-0.5)
            for h in range(0, T, 512):
                S_.mm(pB[:, h:h + 512], onerow[:], row[0:1, h:h + 512])
            S_.vcopy(bc[:], pB[:, :])
            for ot in range(4):
                S_.stt(otile[:, :], kvr[:, ot * T:(ot + 1) * T],
                       kvnw_s[:, ot:ot + 1], bc[:], ALU.mult, ALU.mult)
                S_.dma(ckv_l.ap()[ot * 128:(ot + 1) * 128, :], otile[:, :])

            def rope_deint(x1ap, x2ap):
                S_.vcopy(x2c[:, :], x2ap)
                cos_t = trig[0:32, 0:T]
                sin_t = trig[0:32, T:2 * T]
                S_.tt(rp[0][:, :], x1ap, cos_t, ALU.mult)
                S_.tt(rp[1][:, :], x2c[:, :], sin_t, ALU.mult)
                S_.tt(rp[2][:, :], x1ap, sin_t, ALU.mult)
                S_.tt(rp[3][:, :], x2c[:, :], cos_t, ALU.mult)
                S_.tt(ro[0][:, :], rp[0][:, :], rp[1][:, :], ALU.subtract)
                S_.tt(ro[1][:, :], rp[2][:, :], rp[3][:, :], ALU.add)

            rope_deint(kvr[0:32, 4 * T:5 * T], kvr[32:64, 4 * T:5 * T])
            S_.dma(kpe_l.ap()[0:32, :], ro[0][:, :])
            S_.dma(kpe_l.ap()[32:64, :], ro[1][:, :])

            for ot in range(32):
                mm_otile(pQ, wqbnT, ot * 128, 128, qln, 12)
                S_.vcopy(otile[:, :], pQ[:, :])
                S_.dma(qn_l.ap()[ot * 128:(ot + 1) * 128, :], otile[:, :])

            for hh in range(32):
                mm_otile(pQ, wqbpT, hh * 64, 64, qln, 12)
                S_.vcopy(kvr[0:64, 4 * T:5 * T], pQ[0:64, :])
                rope_deint(kvr[0:32, 4 * T:5 * T], kvr[32:64, 4 * T:5 * T])
                S_.dma(qp_l.ap()[hh * 64:hh * 64 + 32, :], ro[0][:, :])
                S_.dma(qp_l.ap()[hh * 64 + 32:hh * 64 + 64, :], ro[1][:, :])

            for t in range(4):
                S_.dma(qw_s[:, t * 512:(t + 1) * 512], qwT[t * 128:(t + 1) * 128, :])
                S_.dma(sc_s[:, t * 32:(t + 1) * 32], scT[t * 128:(t + 1) * 128, :])
                S_.dma(zr_s[:, t * 32:(t + 1) * 32], zrT[t * 128:(t + 1) * 128, :])
            for t in range(4):
                q = qw_s[:, t * 512:(t + 1) * 512]
                for j in range(8):
                    S_.ts(qtmp[:], q, float(4 * j), 15.0,
                          ALU.logical_shift_right, ALU.bitwise_and)
                    S_.vcopy(wun[:, j::8], qtmp[:])
                for g in range(32):
                    S_.ts(waf[:, g * 128:(g + 1) * 128], wun[:, g * 128:(g + 1) * 128],
                          sc_s[:, t * 32 + g:t * 32 + g + 1],
                          zr_s[:, t * 32 + g:t * 32 + g + 1],
                          ALU.mult, ALU.subtract)
                for g in range(0, 32, 4):
                    for gg in range(4):
                        S_.tr(pT[:, gg * 128:(gg + 1) * 128],
                              waf[:, (g + gg) * 128:(g + gg + 1) * 128], ident_b[:])
                    S_.vcopy(wtr[:, :], pT[:, :])
                    for gg in range(4):
                        S_.dma(wdq_l.ap()[(g + gg) * 128:(g + gg + 1) * 128,
                                          t * 128:(t + 1) * 128],
                               wtr[:, gg * 128:(gg + 1) * 128])

        S_.barrier()
        coll("AllGather", GROUPS4, qn_l, qn_g)
        coll("AllGather", GROUPS4, qp_l, qp_g)
        coll("AllGather", GROUPS4, ckv_l, ckv_g)
        coll("AllGather", GROUPS4, kpe_l, kpe_g)
        coll("AllGather", GROUPS8, wdq_l, wdq_g)

        if _PHASE_LIMIT <= 1:
            S_.replay(nc, serialize=_SERIALIZE)
            return nc
        qn_gf = _flat(qn_g)
        qp_gf = _flat(qp_g)
        y_gf = _flat(y_g)
        qr_gf = _flat(qr_g)
        kr_gf = _flat(kr_g)
        v_gf = _flat(v_g)
        dtg_gf = _flat(dtg_g)
        y1_gf = _flat(y1_g)

        with contextlib.ExitStack() as st:
            sb = lambda n, s, d: st.enter_context(nc.sbuf_tensor(n, s, d))
            ps = lambda n, s, d: st.enter_context(nc.psum_tensor(n, s, d))
            ckv_s = sb("b_ckv_s", [128, 4 * S], BF16)
            kpe_s = sb("b_kpe_s", [64, S], BF16)
            wkvk_s = sb("b_wkvk_s", [128, 4 * 1024], BF16)
            wkvv_s = sb("b_wkvv_s", [128, 4 * 1024], BF16)
            qn_s = sb("b_qn_s", [128, 2 * S], BF16)
            qp_s = sb("b_qp_s", [64, 2 * S], BF16)
            kh_s = sb("b_kh_s", [128, S], BF16)
            vtm_s = sb("b_vtm_s", [128, S], BF16)
            Ssb = sb("b_Ssb", [128, 2 * S], F32)
            Pb = sb("b_Pb", [128, 2 * S], BF16)
            PT_s = sb("b_PT_s", [128, 2 * S], BF16)
            mparts = sb("b_mparts", [128, 18], F32)
            jnk128 = sb("b_jnk128", [128, 128], F32)
            rsparts = sb("b_rsparts", [128, 16], F32)
            m_col = sb("b_m_col", [128, 2], F32)
            negm = sb("b_negm", [128, 2], F32)
            rs_col = sb("b_rs_col", [128, 2], F32)
            rinv = sb("b_rinv", [128, 2], F32)
            ysb = sb("b_ysb", [128, 2 * 128], BF16)
            ytb = sb("b_ytb", [128, 2 * 128], BF16)
            maskneg_b = sb("b_maskneg_b", [128, 128], BF16)
            pS = ps("b_pS", [128, 2048], F32)
            pV = ps("b_pV", [128, 128], F32)
            pTt = ps("b_pTt", [128, 512], BF16)
            pAV = ps("b_pAV", [128, 2 * 128], F32)
            pYt = ps("b_pYt", [128, 2 * 128], BF16)

            for r in range(4):
                for t in range(4):
                    S_.dma(ckv_s[:, t * S + r * 1024:t * S + (r + 1) * 1024],
                           ckv_g.ap()[r * 512 + t * 128:r * 512 + (t + 1) * 128, :])
                S_.dma(kpe_s[:, r * 1024:(r + 1) * 1024],
                       kpe_g.ap()[r * 64:(r + 1) * 64, :])
            for t in range(4):
                S_.dma(wkvk_s[:, t * 1024:(t + 1) * 1024], wkvkT[t * 128:(t + 1) * 128, :], cast=True)
                S_.dma(wkvv_s[:, t * 1024:(t + 1) * 1024], wkvvT[t * 128:(t + 1) * 128, :], cast=True)
            S_.dma(maskneg_b[:], maskneg_in, cast=True)

            y_l3 = y_l.ap().rearrange("(r b) t -> r b t", b=4)

            for hl in range(8):
                hb = (hl % 2) * S
                for r in range(4):
                    idma(qn_s[:, hb + r * 1024:hb + (r + 1) * 1024], qn_gf,
                         xB_qn[:, hl * 4 + r:hl * 4 + r + 1])
                    idma(qp_s[:, hb + r * 1024:hb + (r + 1) * 1024], qp_gf,
                         xB_qp[:, hl * 4 + r:hl * 4 + r + 1])
                for tc in range(8):
                    ps0 = (tc % 4) * 512
                    for kc in range(4):
                        S_.mm(pS[:, ps0:ps0 + 512],
                              wkvk_s[:, kc * 1024 + hl * 128:kc * 1024 + (hl + 1) * 128],
                              ckv_s[:, kc * S + tc * 512:kc * S + (tc + 1) * 512],
                              start=(kc == 0), stop=(kc == 3))
                    S_.vcopy(kh_s[:, tc * 512:(tc + 1) * 512], pS[:, ps0:ps0 + 512])
                for tt in range(32):
                    for kc in range(4):
                        S_.mm(pV[:, :], ckv_s[:, kc * S + tt * 128:kc * S + (tt + 1) * 128],
                              wkvv_s[:, kc * 1024 + hl * 128:kc * 1024 + (hl + 1) * 128],
                              start=(kc == 0), stop=(kc == 3))
                    S_.vcopy(vtm_s[:, tt * 128:(tt + 1) * 128], pV[:, :])
                for qt in range(32):
                    nk = qt + 1
                    kw = nk * 128
                    nkc = (kw + 511) // 512
                    pa = qt % 2
                    sb0 = pa * S
                    ng2 = (kw + 2047) // 2048
                    for ci in range(nkc):
                        w = min(512, kw - ci * 512)
                        ps0 = (ci % 4) * 512
                        last = (ci == nkc - 1)
                        S_.mm(pS[0:128, ps0:ps0 + w], qn_s[:, hb + qt * 128:hb + (qt + 1) * 128],
                              kh_s[:, ci * 512:ci * 512 + w], start=True, stop=False)
                        S_.mm(pS[0:128, ps0:ps0 + w], qp_s[0:64, hb + qt * 128:hb + (qt + 1) * 128],
                              kpe_s[0:64, ci * 512:ci * 512 + w], start=False, stop=not last)
                        if last:
                            S_.mm(pS[0:128, ps0 + w - 128:ps0 + w], ident_b[:], maskneg_b[:],
                                  start=False, stop=True)
                        if ci % 4 == 3 or last:
                            g2 = ci // 4
                            w4 = (ci % 4) * 512 + w
                            S_.ts(Ssb[:, sb0 + g2 * 2048:sb0 + g2 * 2048 + w4],
                                  pS[0:128, 0:w4], 1.0, None, ALU.mult, ALU.max,
                                  accum_out=mparts[:, pa * 9 + g2:pa * 9 + g2 + 1])
                    S_.rmax(m_col[:, pa:pa + 1], mparts[:, pa * 9:pa * 9 + ng2])
                    S_.ts(negm[:, pa:pa + 1], m_col[:, pa:pa + 1], -L0_SCALE, None, ALU.mult)
                    S_.act(Pb[:, sb0:sb0 + kw], Ssb[:, sb0:sb0 + kw],
                           AF.Exp, bias=negm[:, pa:pa + 1], scale=L0_SCALE,
                           accum_out=rs_col[:, pa:pa + 1])
                    S_.recip(rinv[:, pa:pa + 1], rs_col[:, pa:pa + 1])
                    for g in range(0, nk, 4):
                        ng = min(4, nk - g)
                        for kk in range(ng):
                            S_.tr(pTt[:, kk * 128:(kk + 1) * 128],
                                  Pb[:, sb0 + (g + kk) * 128:sb0 + (g + kk + 1) * 128], ident_b[:])
                        S_.vcopy(PT_s[:, sb0 + g * 128:sb0 + (g + ng) * 128],
                                 pTt[:, 0:ng * 128])
                    for kt in range(nk):
                        S_.mm(pAV[:, pa * 128:(pa + 1) * 128],
                              PT_s[:, sb0 + kt * 128:sb0 + (kt + 1) * 128],
                              vtm_s[:, kt * 128:(kt + 1) * 128],
                              start=(kt == 0), stop=(kt == nk - 1))
                    S_.act(ysb[:, pa * 128:(pa + 1) * 128], pAV[:, pa * 128:(pa + 1) * 128],
                           AF.Copy, scale=rinv[:, pa:pa + 1])
                    S_.tr(pYt[:, pa * 128:(pa + 1) * 128], ysb[:, pa * 128:(pa + 1) * 128],
                          ident_b[:])
                    S_.vcopy(ytb[:, pa * 128:(pa + 1) * 128], pYt[:, pa * 128:(pa + 1) * 128])
                    S_.dma(y_l3[hl * 128:(hl + 1) * 128, qt // 8, (qt % 8) * 128:(qt % 8 + 1) * 128],
                           ytb[:, pa * 128:(pa + 1) * 128])

        S_.barrier()
        coll("AllGather", GROUPS4, y_l, y_g)
        if _PHASE_LIMIT <= 2:
            S_.replay(nc, serialize=_SERIALIZE)
            return nc

        with contextlib.ExitStack() as st:
            sb = lambda n, s, d: st.enter_context(nc.sbuf_tensor(n, s, d))
            ps = lambda n, s, d: st.enter_context(nc.psum_tensor(n, s, d))
            TC = 512
            ybf = sb("c_ybf", [128, 32 * TC], BF16)
            wcol = sb("c_wcol", [128, 2 * 32 * 128], BF16)
            wsel = [0]
            xf32 = sb("c_xf32", [128, 2 * TC], F32)
            sqb = [0]
            r16 = sb("c_r16", [128, 32 * TC], F16)
            x1b = sb("c_x1b", [128, 32 * TC], BF16)
            x1h = sb("c_x1h", [128, 32 * TC], F16)
            sq = sb("c_sq", [128, 2 * TC], F16)
            onecol = sb("c_onecol", [128, 1], F16)
            onerow = sb("c_onerow", [1, 128], F32)
            bias_s = sb("c_bias_s", [128, 32], F32)
            tmp32 = sb("c_tmp32", [128, TC], F32)
            rowa = sb("c_rowa", [1, TC], F32)
            rowb = sb("c_rowb", [1, TC], F32)
            rowc = sb("c_rowc", [1, TC], F32)
            m_bc = sb("c_m_bc", [128, TC], F32)
            rstd_bc = sb("c_rstd_bc", [128, TC], F32)
            c_eps5 = sb("c_c_eps5", [1, 1], F32)
            pA = ps("c_pA", [128, 2 * TC], F32)
            pR = ps("c_pR", [1, 2 * TC], F32)
            pB = ps("c_pB", [128, TC], F32)

            S_.memset(onecol[:], 1.0)
            S_.memset(onerow[:], 1.0)
            S_.memset(c_eps5[:], 1e-5)
            S_.dma(bias_s[:], bias_t)

            def wload_c(dst, wT_dram, ocol0, owidth, nk, cast):
                b0 = wsel[0] * 4096
                wsel[0] ^= 1
                S_.dma(dst[:, b0:b0 + nk * owidth].rearrange("p (k o) -> p k o", k=nk),
                       wT_dram[:, ocol0:ocol0 + owidth].rearrange("(k p) o -> p k o", p=128),
                       cast=cast)
                return b0

            def wload_wdq(dst, ot):
                b0 = wsel[0] * 4096
                wsel[0] ^= 1
                r, c0 = ot // 4, (ot % 4) * 128
                S_.dma(dst[:, b0:b0 + 32 * 128].rearrange("p (k o) -> p k o", k=32),
                       wdq_g.ap()[r * 4096:(r + 1) * 4096, c0:c0 + 128]
                       .rearrange("(k p) o -> p k o", p=128))
                return b0

            def layer_norm(src16, dst_bf, dst_f16, out_dram=None, tc0=0):
                S_.act(rowa[:], pR[0:1, 0:TC], AF.Copy, scale=1.0 / D)
                S_.act(rowb[:], pR[0:1, TC:2 * TC], AF.Copy, scale=1.0 / D)
                S_.tt(rowc[:], rowa[:], rowa[:], ALU.mult)
                S_.tt(rowb[:], rowb[:], rowc[:], ALU.subtract)
                S_.act(rowb[:], rowb[:], AF.Ln, bias=c_eps5[0:1, 0:1])
                S_.act(rowb[:], rowb[:], AF.Exp, scale=-0.5)
                S_.mm(pB[:, :], onerow[:], rowa[0:1, :])
                S_.vcopy(m_bc[:], pB[:, :])
                S_.mm(pB[:, :], onerow[:], rowb[0:1, :])
                S_.vcopy(rstd_bc[:], pB[:, :])
                for ot in range(32):
                    sl = src16[:, ot * TC:(ot + 1) * TC]
                    S_.tt(tmp32[:], sl, m_bc[:], ALU.subtract)
                    if dst_bf is not None:
                        S_.tt(dst_bf[:, ot * TC:(ot + 1) * TC], tmp32[:], rstd_bc[:], ALU.mult)
                    if dst_f16 is not None:
                        S_.tt(dst_f16[:, ot * TC:(ot + 1) * TC], tmp32[:], rstd_bc[:], ALU.mult)
                    if out_dram is not None:
                        S_.tt(tmp32[:], tmp32[:], rstd_bc[:], ALU.mult)
                        S_.dma(out_dram[ot * 128:(ot + 1) * 128, tc0:tc0 + TC], tmp32[:])

            for tci, tc in enumerate(range(0, T, TC)):
                for t in range(32):
                    idma(ybf[:, t * TC:(t + 1) * TC], y_gf, xY[:, t * 4 + 2 * tci:t * 4 + 2 * tci + 1])
                for ot in range(32):
                    b0 = wload_c(wcol, woT, ot * 128, 128, 32, True)
                    pa0 = (ot % 2) * TC
                    for k in range(32):
                        S_.mm(pA[:, pa0:pa0 + TC], wcol[:, b0 + k * 128:b0 + (k + 1) * 128],
                              ybf[:, k * TC:(k + 1) * TC],
                              start=(k == 0), stop=(k == 31))
                    xb0 = (ot % 2) * TC
                    S_.dma(xf32[:, xb0:xb0 + TC], xT[ot * 128:(ot + 1) * 128, tc:tc + TC])
                    S_.tt(r16[:, ot * TC:(ot + 1) * TC], pA[:, pa0:pa0 + TC],
                          xf32[:, xb0:xb0 + TC], ALU.add)
                    sb0 = (ot % 2) * TC
                    S_.act(sq[:, sb0:sb0 + TC], r16[:, ot * TC:(ot + 1) * TC], AF.Square)
                    S_.mm(pR[0:1, 0:TC], onecol[:], r16[:, ot * TC:(ot + 1) * TC],
                          start=(ot == 0), stop=(ot == 31))
                    S_.mm(pR[0:1, TC:2 * TC], onecol[:], sq[:, sb0:sb0 + TC],
                          start=(ot == 0), stop=(ot == 31))
                layer_norm(r16, x1b, x1h)
                for ot in range(32):
                    b0 = wload_wdq(wcol, ot)
                    pa0 = (ot % 2) * TC
                    for k in range(32):
                        S_.mm(pA[:, pa0:pa0 + TC], wcol[:, b0 + k * 128:b0 + (k + 1) * 128],
                              x1b[:, k * TC:(k + 1) * TC],
                              start=(k == 0), stop=(k == 31))
                    S_.stt(r16[:, ot * TC:(ot + 1) * TC], pA[:, pa0:pa0 + TC],
                           bias_s[:, ot:ot + 1],
                           x1h[:, ot * TC:(ot + 1) * TC], ALU.add, ALU.add)
                    sb0 = (ot % 2) * TC
                    S_.act(sq[:, sb0:sb0 + TC], r16[:, ot * TC:(ot + 1) * TC], AF.Square)
                    S_.mm(pR[0:1, 0:TC], onecol[:], r16[:, ot * TC:(ot + 1) * TC],
                          start=(ot == 0), stop=(ot == 31))
                    S_.mm(pR[0:1, TC:2 * TC], onecol[:], sq[:, sb0:sb0 + TC],
                          start=(ot == 0), stop=(ot == 31))
                layer_norm(r16, None, None, out_dram=x2_l.ap(), tc0=tc)

        if _PHASE_LIMIT <= 3:
            S_.replay(nc, serialize=_SERIALIZE)
            return nc
        S_.barrier()
        with contextlib.ExitStack() as st:
            sb = lambda n, s, d: st.enter_context(nc.sbuf_tensor(n, s, d))
            ps = lambda n, s, d: st.enter_context(nc.psum_tensor(n, s, d))
            xb = sb("d_xb", [128, 32 * T], BF16)
            wcol = sb("d_wcol", [128, 2 * 32 * 128], BF16)
            wsel = [0]
            qf = sb("d_qf", [128, T], F32)
            xh0 = sb("d_xh0", [64, T], F32)
            pr1 = sb("d_pr1", [64, T], F32)
            pr2 = sb("d_pr2", [64, T], F32)
            ro1 = sb("d_ro1", [64, T], BF16)
            ro2 = sb("d_ro2", [64, T], BF16)
            trig_c = sb("d_trig_c", [64, T], F32)
            trig_s = sb("d_trig_s", [64, T], F32)
            otile = sb("d_otile", [128, T], BF16)
            dtb_s = sb("d_dtb_s", [32, 1], F32)
            alog_s = sb("d_alog_s", [32, 1], F32)
            a1 = sb("d_a1", [32, T], F32)
            l1 = sb("d_l1", [32, T], F32)
            r1v = sb("d_r1v", [32, T], F32)
            dt_s = sb("d_dt_s", [32, T], F32)
            g_s = sb("d_g_s", [32, T], F32)
            nega = sb("d_nega", [32, 1], F32)
            c_one = sb("d_c_one", [32, 1], F32)
            dgt = sb("d_dgt", [128, 64], F32)
            pA = ps("d_pA", [128, 2 * T], F32)
            pacnt = [0]
            pD = ps("d_pD", [32, T], F32)
            pTr = ps("d_pTr", [128, 64], F32)

            for t in range(32):
                S_.dma(xb[:, t * T:(t + 1) * T], x2_l.ap()[t * 128:(t + 1) * 128, :], cast=True)
            S_.dma(trig_c[:], cos1)
            S_.dma(trig_s[:], sin1)
            S_.dma(dtb_s[:], dtb)
            S_.dma(alog_s[:], alog)
            S_.memset(c_one[:], 1.0)

            def proj_tile(psum, wT_dram, ocol0, owidth):
                b0 = wsel[0] * 4096
                wsel[0] ^= 1
                pa0 = (pacnt[0] % 2) * T if psum is pA else 0
                pacnt[0] += 1
                S_.dma(wcol[:, b0:b0 + 32 * owidth].rearrange("p (k o) -> p k o", k=32),
                       wT_dram[:, ocol0:ocol0 + owidth].rearrange("(k p) o -> p k o", p=128),
                       cast=True)
                for k in range(32):
                    l = wcol[0:128, b0 + k * owidth:b0 + (k + 1) * owidth]
                    for h in range(0, T, 512):
                        S_.mm(psum[0:owidth, pa0 + h:pa0 + h + 512],
                              l, xb[:, k * T + h:k * T + h + 512],
                              start=(k == 0), stop=(k == 31))
                return pa0

            def rope1(dst_dram, row0):
                S_.vcopy(xh0[:], qf[64:128, :])
                S_.tt(pr1[:], qf[0:64, :], trig_c[:], ALU.mult)
                S_.tt(pr2[:], xh0[:], trig_s[:], ALU.mult)
                S_.tt(pr1[:], pr1[:], pr2[:], ALU.subtract)
                S_.vcopy(ro1[:], pr1[:])
                S_.tt(pr1[:], xh0[:], trig_c[:], ALU.mult)
                S_.tt(pr2[:], qf[0:64, :], trig_s[:], ALU.mult)
                S_.tt(pr1[:], pr1[:], pr2[:], ALU.add)
                S_.vcopy(ro2[:], pr1[:])
                S_.dma(dst_dram[row0:row0 + 64, :], ro1[:])
                S_.dma(dst_dram[row0 + 64:row0 + 128, :], ro2[:])

            for hh in range(32):
                pa0 = proj_tile(pA, WqT, hh * 128, 128)
                S_.vcopy(qf[:], pA[:, pa0:pa0 + T])
                rope1(qr_l.ap(), hh * 128)
            for hh in range(8):
                pa0 = proj_tile(pA, WkT, hh * 128, 128)
                S_.vcopy(qf[:], pA[:, pa0:pa0 + T])
                rope1(kr_l.ap(), hh * 128)
            for hh in range(8):
                pa0 = proj_tile(pA, WvT, hh * 128, 128)
                S_.vcopy(otile[:], pA[:, pa0:pa0 + T])
                S_.dma(v_l.ap()[hh * 128:(hh + 1) * 128, :], otile[:])
            proj_tile(pD, WinT, 0, 32)
            S_.act(a1[:], pD[0:32, :], AF.Abs, bias=dtb_s[:, 0:1])
            S_.act(a1[:], a1[:], AF.Exp, scale=-1.0)
            S_.act(l1[:], a1[:], AF.Ln, bias=c_one[:, 0:1])
            S_.act(r1v[:], pD[0:32, :], AF.Relu, bias=dtb_s[:, 0:1])
            S_.tt(dt_s[:], l1[:], r1v[:], ALU.add)
            S_.act(nega[:], alog_s[:], AF.Exp)
            S_.ts(nega[:], nega[:], -1.0, None, ALU.mult)
            S_.ts(g_s[:], dt_s[:], nega[:, 0:1], None, ALU.mult)
            dtg_l3 = dtg_l.ap().rearrange("(t b) c -> t b c", b=4)
            for tt in range(8):
                S_.tr(pTr[:, 0:32], dt_s[0:32, tt * 128:(tt + 1) * 128], identf[0:32, 0:32])
                S_.tr(pTr[:, 32:64], g_s[0:32, tt * 128:(tt + 1) * 128], identf[0:32, 0:32])
                S_.vcopy(dgt[:], pTr[:, :])
                for hg in range(4):
                    S_.dma(dtg_l3[tt * 128:(tt + 1) * 128, hg, 0:8],
                           dgt[:, hg * 8:(hg + 1) * 8])
                    S_.dma(dtg_l3[tt * 128:(tt + 1) * 128, hg, 8:16],
                           dgt[:, 32 + hg * 8:32 + (hg + 1) * 8])

        S_.barrier()
        coll("AllGather", GROUPS4, qr_l, qr_g)
        coll("AllGather", GROUPS4, kr_l, kr_g)
        coll("AllGather", GROUPS4, v_l, v_g)
        coll("AllGather", GROUPS4, dtg_l, dtg_g)

        if _PHASE_LIMIT <= 4:
            S_.replay(nc, serialize=_SERIALIZE)
            return nc
        with contextlib.ExitStack() as st:
            sb = lambda n, s, d: st.enter_context(nc.sbuf_tensor(n, s, d))
            ps = lambda n, s, d: st.enter_context(nc.psum_tensor(n, s, d))
            qh_all = sb("e_qh_all", [128, 8 * S], BF16)
            k_s = sb("e_k_s", [128, 2 * S], BF16)
            v_s = sb("e_v_s", [128, 2 * S], BF16)
            dtg_s = sb("e_dtg_s", [128, 32 * 16], F32)
            g_s = sb("e_g_s", [128, 256], F32)
            G_all = sb("e_G_all", [128, 256], F32)
            negG = sb("e_negG", [128, 256], F32)
            bGl = sb("e_bGl", [128, 256], F32)
            GT_s = sb("e_GT_s", [128, 256], F32)
            Gl_row = sb("e_Gl_row", [1, 256], F32)
            grow_c = sb("e_grow_c", [1, 1024], F32)
            onecol = sb("e_onecol", [128, 1], F32)
            onerow = sb("e_onerow", [1, 128], F32)
            S_a = sb("e_S_a", [128, 8 * 128], F32)
            S_b = sb("e_S_b", [128, 8 * 128], F32)
            St_bf = sb("e_St_bf", [128, 8 * 128], BF16)
            kt_tm = sb("e_kt_tm", [128, 2 * 128], BF16)
            vt_tm = sb("e_vt_tm", [128, 2 * 128], BF16)
            xc_tm = sb("e_xc_tm", [128, 128], BF16)
            kdec = sb("e_kdec", [128, 128], BF16)
            A_T = sb("e_A_T", [128, 128], BF16)
            Dm = sb("e_Dm", [128, 128], F32)
            t1 = sb("e_t1", [128, 128], F32)
            t2 = sb("e_t2", [128, 128], F32)
            y_s = sb("e_y_s", [128, 128], F32)
            y_n = sb("e_y_n", [128, 128], BF16)
            y1tb = sb("e_y1tb", [128, 128], BF16)
            jnk = sb("e_jnk", [128, 128], F32)
            eG = sb("e_eG", [128, 1], F32)
            edec = sb("e_edec", [128, 1], F32)
            cd = sb("e_cd", [128, 1], F32)
            ss_col = sb("e_ss_col", [128, 1], F32)
            rstd_c = sb("e_rstd_c", [128, 1], F32)
            triuf = sb("e_triuf", [128, 128], F32)
            masknegD = sb("e_masknegD", [128, 128], F32)
            gnorm_t = sb("e_gnorm_t", [128, 128], F32)
            c_eps5 = sb("e_c_eps5", [128, 1], F32)
            pS = ps("e_pS", [128, 128], F32)
            pK1 = ps("e_pK1", [128, 256], F32)
            pC = ps("e_pC", [128, 128], F32)
            pD_ = ps("e_pD", [128, 128], F32)
            pE = ps("e_pE", [128, 128], F32)
            pT = ps("e_pT", [128, 256], BF16)
            pRow = ps("e_pRow", [1, 256], F32)
            pG = ps("e_pG", [128, 256], F32)

            for hl in range(8):
                for r in range(4):
                    idma(qh_all[:, hl * S + r * 1024:hl * S + (r + 1) * 1024],
                         qr_gf, xB_qn[:, hl * 4 + r:hl * 4 + r + 1])
            for ti in range(2):
                for r in range(4):
                    idma(k_s[:, ti * S + r * 1024:ti * S + (r + 1) * 1024],
                         kr_gf, xE_kv[:, ti * 4 + r:ti * 4 + r + 1])
                    idma(v_s[:, ti * S + r * 1024:ti * S + (r + 1) * 1024],
                         v_gf, xE_kv[:, ti * 4 + r:ti * 4 + r + 1])
            for c in range(32):
                idma(dtg_s[:, c * 16:(c + 1) * 16], dtg_gf, xE_dtg[:, c:c + 1])
            for c in range(32):
                S_.vcopy(g_s[:, c * 8:(c + 1) * 8], dtg_s[:, c * 16 + 8:(c + 1) * 16])
            S_.dma(triuf[:], triuf_in)
            S_.dma(masknegD[:], masknegD_in)
            S_.dma(gnorm_t[:], gnt)
            S_.memset(onecol[:], 1.0)
            S_.memset(onerow[:], 1.0)
            S_.memset(c_eps5[:], 1e-5)

            S_.mm(pG[:, :], triuf[:], g_s[:])
            S_.vcopy(G_all[:], pG[:, :])
            S_.ts(negG[:], G_all[:], -1.0, None, ALU.mult)
            S_.mm(pRow[0:1, :], onecol[:], g_s[:])
            S_.vcopy(Gl_row[:], pRow[0:1, :])
            S_.mm(pK1[:, :], onerow[:], Gl_row[0:1, :])
            S_.vcopy(bGl[:], pK1[:, :])
            S_.tr(pG[:, 0:128], G_all[:, 0:128], identf[:])
            S_.tr(pG[:, 128:256], G_all[:, 128:256], identf[:])
            S_.vcopy(GT_s[:], pG[:, :])
            S_.dma(grows.ap()[0:128, :], GT_s[:, 0:128])
            S_.dma(grows.ap()[128:256, :], GT_s[:, 128:256])
            grows_flat = grows.ap().rearrange("u w -> (u w)")
            y1_l3 = y1_l.ap().rearrange("(r b) t -> r b t", b=4)

            S_.memset(S_a[:], 0.0)
            S_.memset(St_bf[:], 0.0)
            for c in range(32):
                S_.dma(grow_c[0:1, :], grows_flat[None, c * 1024:(c + 1) * 1024])
                for lkv in range(2):
                    S_.tr(pT[:, 0:128], k_s[:, lkv * S + c * 128:lkv * S + (c + 1) * 128], ident_b[:])
                    S_.tr(pT[:, 128:256], v_s[:, lkv * S + c * 128:lkv * S + (c + 1) * 128], ident_b[:])
                    S_.vcopy(kt_tm[:, lkv * 128:(lkv + 1) * 128], pT[:, 0:128])
                    S_.vcopy(vt_tm[:, lkv * 128:(lkv + 1) * 128], pT[:, 128:256])
                for hl in range(8):
                    u = c * 8 + hl
                    lkv = hl // 4
                    G_col = G_all[:, u:u + 1]
                    negG_col = negG[:, u:u + 1]
                    bGl_col = bGl[:, u:u + 1]
                    Sc = S_a[:, hl * 128:(hl + 1) * 128] if c % 2 == 0 else S_b[:, hl * 128:(hl + 1) * 128]
                    Sn = S_b[:, hl * 128:(hl + 1) * 128] if c % 2 == 0 else S_a[:, hl * 128:(hl + 1) * 128]
                    Stb = St_bf[:, hl * 128:(hl + 1) * 128]
                    S_.act(eG[:], G_col, AF.Exp)
                    S_.act(edec[:], G_col, AF.Exp, scale=-1.0, bias=bGl_col)
                    S_.act(cd[:], bGl_col, AF.Exp)
                    S_.mm(pK1[:, 0:128], onerow[:], grow_c[0:1, hl * 128:(hl + 1) * 128])
                    S_.tt(t1[:], pK1[:, 0:128], masknegD[:], ALU.add)
                    S_.act(Dm[:], t1[:], AF.Exp, bias=negG_col)
                    S_.mm(pS[:, :], k_s[:, lkv * S + c * 128:lkv * S + (c + 1) * 128],
                          qh_all[:, hl * S + c * 128:hl * S + (c + 1) * 128])
                    S_.tt(A_T[:], pS[:, :], Dm[:], ALU.mult)
                    S_.ts(xc_tm[:], vt_tm[:, lkv * 128:(lkv + 1) * 128],
                          dtg_s[:, c * 16 + hl:c * 16 + hl + 1], None, ALU.mult)
                    S_.ts(kdec[:], kt_tm[:, lkv * 128:(lkv + 1) * 128], edec[:, 0:1], None, ALU.mult)
                    S_.mm(pC[:, :], kdec[:], xc_tm[:])
                    S_.mm(pD_[:, :], A_T[:], xc_tm[:])
                    S_.mm(pE[:, :], qh_all[:, hl * S + c * 128:hl * S + (c + 1) * 128], Stb)
                    S_.ts(t2[:], pE[:, :], eG[:, 0:1], None, ALU.mult)
                    S_.tt(y_s[:], pD_[:, :], t2[:], ALU.add)
                    S_.act(jnk[:], y_s[:], AF.Square, accum_out=ss_col[:, 0:1])
                    S_.act(rstd_c[:], ss_col[:], AF.Ln, scale=1.0 / 128, bias=c_eps5[:, 0:1])
                    S_.act(rstd_c[:], rstd_c[:], AF.Exp, scale=-0.5)
                    S_.stt(y_n[:], y_s[:], rstd_c[:, 0:1], gnorm_t[:], ALU.mult, ALU.mult)
                    S_.tr(pT[:, 0:128], y_n[:], ident_b[:])
                    S_.vcopy(y1tb[:], pT[:, 0:128])
                    S_.dma(y1_l3[hl * 128:(hl + 1) * 128, c // 8, (c % 8) * 128:(c % 8 + 1) * 128],
                           y1tb[:])
                    S_.ts(Sn, Sc, cd[:, 0:1], None, ALU.mult)
                    S_.tt(Sn, Sn, pC[:, :], ALU.add)
                    S_.vcopy(Stb, Sn)

        S_.barrier()
        coll("AllGather", GROUPS4, y1_l, y1_g)
        if _PHASE_LIMIT <= 5:
            S_.replay(nc, serialize=_SERIALIZE)
            return nc

        with contextlib.ExitStack() as st:
            sb = lambda n, s, d: st.enter_context(nc.sbuf_tensor(n, s, d))
            ps = lambda n, s, d: st.enter_context(nc.psum_tensor(n, s, d))
            TC = 512
            y1b = sb("f_y1b", [128, 32 * TC], BF16)
            x2b = sb("f_x2b", [128, 32 * TC], BF16)
            ygb = y1b
            wcol = sb("f_wcol", [128, 2 * 32 * 128], BF16)
            wsel = [0]
            xf32 = sb("f_xf32", [128, TC], F32)
            r16 = sb("f_r16", [128, 32 * TC], F16)
            x3b = sb("f_x3b", [128, 32 * TC], BF16)
            x3h = sb("f_x3h", [128, 32 * TC], F16)
            sq = sb("f_sq", [128, TC], F16)
            sg = sb("f_sg", [128, TC], BF16)
            onecol = sb("f_onecol", [128, 1], F16)
            onerow = sb("f_onerow", [1, 128], F32)
            tmp32 = sb("f_tmp32", [128, TC], F32)
            rowa = sb("f_rowa", [1, TC], F32)
            rowb = sb("f_rowb", [1, TC], F32)
            rowc = sb("f_rowc", [1, TC], F32)
            m_bc = sb("f_m_bc", [128, TC], F32)
            rstd_bc = sb("f_rstd_bc", [128, TC], F32)
            c_eps5 = sb("f_c_eps5", [1, 1], F32)
            lg_f = sb("f_lg_f", [28, TC], F32)
            lt = sb("f_lt", [128, 28], F32)
            msk1 = sb("f_msk1", [128, 28], F32)
            msk2 = sb("f_msk2", [128, 28], F32)
            l2t = sb("f_l2t", [128, 28], F32)
            tb = sb("f_tb", [128, 28], F32)
            ga_t = sb("f_ga_t", [128, 28], F32)
            gb_t = sb("f_gb_t", [128, 28], F32)
            gtm = sb("f_gtm", [128, 28], F32)
            m1c = sb("f_m1c", [128, 1], F32)
            m2c = sb("f_m2c", [128, 1], F32)
            dgc = sb("f_dgc", [128, 1], F32)
            p1c = sb("f_p1c", [128, 1], F32)
            p2c = sb("f_p2c", [128, 1], F32)
            gate_fm = sb("f_gate_fm", [28, TC], F32)
            gsc1 = sb("f_gsc1", [128, TC], F32)
            gsc2 = sb("f_gsc2", [96, TC], F32)
            hg1 = sb("f_hg1", [128, TC], BF16)
            hg2 = sb("f_hg2", [96, TC], BF16)
            lb1 = sb("f_lb1", [128, 2 * 128], BF16)
            lb2 = sb("f_lb2", [96, 2 * 128], BF16)
            R1s = sb("f_R1s", [28, 128], F32)
            R2s = sb("f_R2s", [28, 96], F32)
            pA = ps("f_pA", [128, TC], F32)
            pR = ps("f_pR", [1, 2 * TC], F32)
            pB = ps("f_pB", [128, TC], F32)
            pL = ps("f_pL", [28, TC], F32)
            pTr = ps("f_pTr", [128, 128], F32)
            pH = ps("f_pH", [128, TC], F32)

            S_.memset(onecol[:], 1.0)
            S_.memset(onerow[:], 1.0)
            S_.memset(c_eps5[:], 1e-5)
            S_.dma(R1s[:], R1_in)
            S_.dma(R2s[:], R2_in)

            def wload_f(dst, wT_dram, ocol0, owidth, nk, cast=True):
                b0 = wsel[0] * 4096
                wsel[0] ^= 1
                S_.dma(dst[:, b0:b0 + nk * owidth].rearrange("p (k o) -> p k o", k=nk),
                       wT_dram[:, ocol0:ocol0 + owidth].rearrange("(k p) o -> p k o", p=128),
                       cast=cast)
                return b0

            def layer_norm(src16, dst_bf, dst_f16, out_dram=None, tc0=0):
                S_.act(rowa[:], pR[0:1, 0:TC], AF.Copy, scale=1.0 / D)
                S_.act(rowb[:], pR[0:1, TC:2 * TC], AF.Copy, scale=1.0 / D)
                S_.tt(rowc[:], rowa[:], rowa[:], ALU.mult)
                S_.tt(rowb[:], rowb[:], rowc[:], ALU.subtract)
                S_.act(rowb[:], rowb[:], AF.Ln, bias=c_eps5[0:1, 0:1])
                S_.act(rowb[:], rowb[:], AF.Exp, scale=-0.5)
                S_.mm(pB[:, :], onerow[:], rowa[0:1, :])
                S_.vcopy(m_bc[:], pB[:, :])
                S_.mm(pB[:, :], onerow[:], rowb[0:1, :])
                S_.vcopy(rstd_bc[:], pB[:, :])
                for ot in range(32):
                    sl = src16[:, ot * TC:(ot + 1) * TC]
                    S_.tt(tmp32[:], sl, m_bc[:], ALU.subtract)
                    if dst_bf is not None:
                        S_.tt(dst_bf[:, ot * TC:(ot + 1) * TC], tmp32[:], rstd_bc[:], ALU.mult)
                    if dst_f16 is not None:
                        S_.tt(dst_f16[:, ot * TC:(ot + 1) * TC], tmp32[:], rstd_bc[:], ALU.mult)
                    if out_dram is not None:
                        S_.tt(tmp32[:], tmp32[:], rstd_bc[:], ALU.mult)
                        S_.dma(out_dram[ot * 128:(ot + 1) * 128, tc0:tc0 + TC], tmp32[:])

            for tci, tc in enumerate(range(0, T, TC)):
                if tci >= _F_TC_LIMIT:
                    break
                for t in range(32):
                    idma(y1b[:, t * TC:(t + 1) * TC], y1_gf, xY[:, t * 4 + 2 * tci:t * 4 + 2 * tci + 1])
                    S_.dma(x2b[:, t * TC:(t + 1) * TC],
                           x2_l.ap()[t * 128:(t + 1) * 128, tc:tc + TC], cast=True)
                if _F_SEC < 2:
                    continue
                for ot in range(32):
                    b0 = wload_f(wcol, WgT, ot * 128, 128, 32, True)
                    for k in range(32):
                        S_.mm(pA[:, :], wcol[:, b0 + k * 128:b0 + (k + 1) * 128],
                              x2b[:, k * TC:(k + 1) * TC],
                              start=(k == 0), stop=(k == 31))
                    S_.act(sg[:], pA[:, :], AF.Silu)
                    S_.tt(ygb[:, ot * TC:(ot + 1) * TC], y1b[:, ot * TC:(ot + 1) * TC],
                          sg[:], ALU.mult)
                if _F_SEC < 3:
                    continue
                S_.barrier()
                S_.serialize_recording = True
                for ot in range(32):
                    b0 = wload_f(wcol, WoT1, ot * 128, 128, 32, True)
                    for k in range(32):
                        S_.mm(pA[:, :], wcol[:, b0 + k * 128:b0 + (k + 1) * 128],
                              ygb[:, k * TC:(k + 1) * TC],
                              start=(k == 0), stop=(k == 31))
                    S_.dma(xf32[:, 0:TC], x2_l.ap()[ot * 128:(ot + 1) * 128, tc:tc + TC])
                    S_.tt(r16[:, ot * TC:(ot + 1) * TC], pA[:, :], xf32[:, 0:TC], ALU.add)
                    S_.act(sq[:, 0:TC], r16[:, ot * TC:(ot + 1) * TC], AF.Square)
                    S_.mm(pR[0:1, 0:TC], onecol[:], r16[:, ot * TC:(ot + 1) * TC],
                          start=(ot == 0), stop=(ot == 31))
                    S_.mm(pR[0:1, TC:2 * TC], onecol[:], sq[:, 0:TC],
                          start=(ot == 0), stop=(ot == 31))
                S_.barrier()
                layer_norm(r16, x3b, x3h)
                if _F_SEC < 4:
                    continue
                S_.barrier()
                b0 = wload_f(wcol, gateT, 0, 28, 32, True)
                for k in range(32):
                    S_.mm(pL[0:28, :], wcol[:, b0 + k * 28:b0 + (k + 1) * 28],
                          x3b[:, k * TC:(k + 1) * TC],
                          start=(k == 0), stop=(k == 31))
                S_.vcopy(lg_f[:], pL[0:28, :])
                for j in range(TC // 128):
                    S_.barrier()
                    S_.tr(pTr[0:128, 0:28], lg_f[0:28, j * 128:(j + 1) * 128], identf[0:28, 0:28])
                    S_.vcopy(lt[:], pTr[0:128, 0:28])
                    S_.rmax(m1c[:], lt[:])
                    S_.ts(msk1[:], lt[:], m1c[:, 0:1], None, ALU.is_equal)
                    S_.ts(tb[:], msk1[:], 1e30, None, ALU.mult)
                    S_.tt(l2t[:], lt[:], tb[:], ALU.subtract)
                    S_.rmax(m2c[:], l2t[:])
                    S_.tt(dgc[:], m1c[:], m2c[:], ALU.subtract)
                    S_.act(p1c[:], dgc[:], AF.Sigmoid)
                    S_.ts(p2c[:], p1c[:], -1.0, 1.0, ALU.mult, ALU.add)
                    S_.ts(msk2[:], l2t[:], m2c[:, 0:1], None, ALU.is_equal)
                    S_.ts(ga_t[:], msk1[:], p1c[:, 0:1], None, ALU.mult)
                    S_.ts(gb_t[:], msk2[:], p2c[:, 0:1], None, ALU.mult)
                    S_.tt(gtm[:], ga_t[:], gb_t[:], ALU.add)
                    S_.tr(pTr[0:28, 0:128], gtm[0:128, 0:28], identf[:])
                    S_.vcopy(gate_fm[0:28, j * 128:(j + 1) * 128], pTr[0:28, 0:128])
                if _F_SEC < 5:
                    continue
                S_.barrier()
                S_.mm(pB[0:128, :], R1s[:], gate_fm[0:28, :])
                S_.vcopy(gsc1[:], pB[0:128, :])
                S_.mm(pB[0:96, :], R2s[:], gate_fm[0:28, :])
                S_.vcopy(gsc2[:], pB[0:96, :])
                b0 = wload_f(wcol, loraAT, 0, 128, 32, True)
                for k in range(32):
                    S_.mm(pH[:, :], wcol[:, b0 + k * 128:b0 + (k + 1) * 128],
                          x3b[:, k * TC:(k + 1) * TC],
                          start=(k == 0), stop=(k == 31))
                S_.stt(hg1[:], pH[:, :], 2.0, gsc1[:], ALU.mult, ALU.mult)
                b0 = wload_f(wcol, loraAT, 128, 96, 32, True)
                for k in range(32):
                    S_.mm(pH[0:96, :], wcol[:, b0 + k * 96:b0 + (k + 1) * 96],
                          x3b[:, k * TC:(k + 1) * TC],
                          start=(k == 0), stop=(k == 31))
                S_.stt(hg2[:], pH[0:96, :], 2.0, gsc2[:], ALU.mult, ALU.mult)
                if _F_SEC < 6:
                    continue
                S_.barrier()
                for ot in range(32):
                    S_.barrier()
                    b0 = wload_f(wcol, baseT, ot * 128, 128, 32, True)
                    for k in range(32):
                        S_.mm(pA[:, :], wcol[:, b0 + k * 128:b0 + (k + 1) * 128],
                              x3b[:, k * TC:(k + 1) * TC],
                              start=(k == 0), stop=False)
                    lbb = (ot % 2) * 128
                    S_.dma(lb1[:, lbb:lbb + 128], loraBf[0:128, ot * 128:(ot + 1) * 128], cast=True)
                    S_.dma(lb2[:, lbb:lbb + 128], loraBf[128:224, ot * 128:(ot + 1) * 128], cast=True)
                    S_.mm(pA[:, :], lb1[:, lbb:lbb + 128], hg1[:], start=False, stop=False)
                    S_.mm(pA[:, :], lb2[0:96, lbb:lbb + 128], hg2[0:96, :], start=False, stop=True)
                    S_.tt(r16[:, ot * TC:(ot + 1) * TC], pA[:, :], x3h[:, ot * TC:(ot + 1) * TC],
                          ALU.add)
                    S_.act(sq[:, 0:TC], r16[:, ot * TC:(ot + 1) * TC], AF.Square)
                    S_.mm(pR[0:1, 0:TC], onecol[:], r16[:, ot * TC:(ot + 1) * TC],
                          start=(ot == 0), stop=(ot == 31))
                    S_.mm(pR[0:1, TC:2 * TC], onecol[:], sq[:, 0:TC],
                          start=(ot == 0), stop=(ot == 31))
                S_.barrier()
                layer_norm(r16, None, None, out_dram=o_x4, tc0=tc)

        S_.replay(nc, serialize=_SERIALIZE)
    return nc


def host_prep_fused(inputs):
    x = np.asarray(inputs["L0_attn_x"], np.float32)
    wq_a = np.asarray(inputs["L0_attn_wq_a"], np.float32)
    q_norm_w = np.asarray(inputs["L0_attn_q_norm_w"], np.float32)
    wq_b = np.asarray(inputs["L0_attn_wq_b"], np.float32)
    wkv_a = np.asarray(inputs["L0_attn_wkv_a"], np.float32)
    kv_norm_w = np.asarray(inputs["L0_attn_kv_norm_w"], np.float32)
    wkv_b = np.asarray(inputs["L0_attn_wkv_b"], np.float32)
    wo = np.asarray(inputs["L0_attn_wo"], np.float32)
    qw = np.asarray(inputs["L0_ffn_q_weights"], np.int32)
    sc = np.asarray(inputs["L0_ffn_scales"], np.float32)
    zr = np.asarray(inputs["L0_ffn_zeros"], np.float32)
    bias = np.asarray(inputs["L0_ffn_bias"], np.float32)
    Wq = np.asarray(inputs["L1_attn_Wq"], np.float32)
    Wk = np.asarray(inputs["L1_attn_Wk"], np.float32)
    Wv = np.asarray(inputs["L1_attn_Wv"], np.float32)
    Wo1 = np.asarray(inputs["L1_attn_Wo"], np.float32)
    Wg = np.asarray(inputs["L1_attn_Wg"], np.float32)
    Win = np.asarray(inputs["L1_attn_Win"], np.float32)
    dtb = np.asarray(inputs["L1_attn_dt_bias"], np.float32).reshape(32, 1)
    alog = np.asarray(inputs["L1_attn_A_log_bias"], np.float32).reshape(32, 1)
    gnw = np.asarray(inputs["L1_attn_gnorm_w"], np.float32)
    baseW = np.asarray(inputs["L1_ffn_base_W"], np.float32)
    gateW = np.asarray(inputs["L1_ffn_gate_W"], np.float32)
    loraA = np.asarray(inputs["L1_ffn_lora_A"], np.float32)
    loraB = np.asarray(inputs["L1_ffn_lora_B"], np.float32)

    perm = _deint_perm()
    wqb = wq_b.reshape(32, 192, 1536)
    wqbnT = _t(wqb[:, :128, :].reshape(4096, 1536))
    wqbpT = _t(wqb[:, 128:, :][:, perm, :].reshape(2048, 1536))
    wqaT = _t(wq_a)
    wkva_p = wkv_a.copy()
    wkva_p[512:576] = wkv_a[512:576][perm]
    wkvaT = _t(wkva_p)
    qnw_t = np.ascontiguousarray(q_norm_w.reshape(12, 128).T)
    kvnw_t = np.ascontiguousarray(kv_norm_w.reshape(4, 128).T)
    qwT = np.ascontiguousarray(qw.T)
    scT, zrT = _t(sc), _t(zr)
    ident = np.eye(128, dtype=np.float32)
    maskneg = np.where(np.arange(128)[None, :] <= np.arange(128)[:, None],
                       0.0, -1e30).astype(np.float32)
    wkvb = wkv_b.reshape(32, 256, 512)
    woT = _t(wo)
    bias_t = np.ascontiguousarray(bias.reshape(32, 128).T)
    WqT, WkT, WvT, WinT = _t(Wq), _t(Wk), _t(Wv), _t(Win)
    triu = np.triu(np.ones((128, 128), np.float32))
    masknegD = np.where(np.arange(128)[None, :] >= np.arange(128)[:, None],
                        0.0, -1e30).astype(np.float32)
    gnt = np.tile(gnw[None, :], (128, 1)).astype(np.float32)
    WgT, WoT1, baseT, gateT = _t(Wg), _t(Wo1), _t(baseW), _t(gateW)
    loraAT = _t(loraA.reshape(224, 4096))
    loraBf = np.ascontiguousarray(loraB.transpose(0, 2, 1).reshape(224, 4096))
    R = np.zeros((28, 224), np.float32)
    for e in range(28):
        R[e, 8 * e:8 * e + 8] = 1.0

    p128 = np.arange(128)
    p64 = np.arange(64)
    maps = []
    for c in range(NCORES):
        b, t0, h0, blk = c // 4, (c % 4) * T, 8 * (c % 4), c % 4
        pos = np.arange(t0, t0 + T)
        cosp, sinp = _rope_tabs_l0(pos)
        cos1, sin1 = _rope_tabs_l1(pos)
        idxB_qn = np.empty((128, 32), np.int32)
        idxB_qp = np.empty((64, 32), np.int32)
        for hl in range(8):
            for r in range(4):
                idxB_qn[:, hl * 4 + r] = (r * 4096 + (h0 + hl) * 128 + p128) * 1024
                idxB_qp[:, hl * 4 + r] = (r * 2048 + (h0 + hl) * 64 + p64) * 1024
        idxY = np.empty((128, 128), np.int32)
        for ht in range(32):
            for tci in range(4):
                idxY[:, ht * 4 + tci] = (((ht // 8) * 4096 + ((ht % 8) * 128 + p128) * 4
                                          + blk) * 1024 + tci * 256)
        idxE_kv = np.empty((128, 8), np.int32)
        for ti in range(2):
            for r in range(4):
                idxE_kv[:, ti * 4 + r] = (r * 1024 + 256 * blk + ti * 128 + p128) * 1024
        idxE_dtg = np.empty((128, 32), np.int32)
        for cc in range(32):
            idxE_dtg[:, cc] = ((cc // 8) * 4096 + (128 * (cc % 8) + p128) * 4 + blk) * 16
        wk = np.concatenate([wkvb[h0 + i, :128, :].T for i in range(8)], axis=1)
        wv = np.concatenate([wkvb[h0 + i, 128:, :].T for i in range(8)], axis=1)
        maps.append({
            "xT": np.ascontiguousarray(x[b, t0:t0 + T, :].T),
            "wqaT": wqaT, "wqbnT": wqbnT, "wqbpT": wqbpT,
            "qnw": qnw_t, "wkvaT": wkvaT, "kvnw": kvnw_t,
            "cosp": cosp, "sinp": sinp,
            "qwT": np.ascontiguousarray(qwT[512 * c:512 * (c + 1), :]),
            "scT": np.ascontiguousarray(scT[512 * c:512 * (c + 1), :]),
            "zrT": np.ascontiguousarray(zrT[512 * c:512 * (c + 1), :]),
            "identf_in": ident, "maskneg_in": maskneg,
            "wkvkT": np.ascontiguousarray(wk, dtype=np.float32),
            "wkvvT": np.ascontiguousarray(wv, dtype=np.float32),
            "woT": woT, "bias_t": bias_t,
            "WqT": WqT, "WkT": WkT, "WvT": WvT, "WinT": WinT,
            "dtb": dtb, "alog": alog, "cos1": cos1, "sin1": sin1,
            "gnt": gnt, "triuf_in": triu, "masknegD_in": masknegD,
            "WgT": WgT, "WoT1": WoT1, "baseT": baseT, "gateT": gateT,
            "loraAT": loraAT, "loraBf": loraBf,
            "R1_in": np.ascontiguousarray(R[:, 0:128]),
            "R2_in": np.ascontiguousarray(R[:, 128:224]),
            "idxB_qn": idxB_qn, "idxB_qp": idxB_qp, "idxY": idxY,
            "idxE_kv": idxE_kv, "idxE_dtg": idxE_dtg,
        })
    return maps


def kernel(**inputs):
    res = run_bass_kernel_spmd(build_fused(), host_prep_fused(inputs),
                               core_ids=list(range(NCORES))).results
    out = np.empty((B, S, D), np.float32)
    for c in range(NCORES):
        b, t0 = c // 4, (c % 4) * T
        out[b, t0:t0 + T, :] = res[c]["o_x4"].T
    return out
